# version 01
# baseline (speedup 1.0000x reference)
import numpy as np
from contextlib import ExitStack

import concourse.bass as bass
import concourse.mybir as mybir
from concourse.bass_utils import run_bass_kernel_spmd

F32 = mybir.dt.float32
I32 = mybir.dt.int32
AF = mybir.ActivationFunctionType
OP = mybir.AluOpType
AX = mybir.AxisListType

H = 2048
T = 2048
NHC = 16
NH, HD = 16, 128
C = 128
NCH = 16
TBLK = 512
NBLK = 4
NE, NG = 32, 8
FI = 768
NJ = 12
NI = 6
LN_EPS = 1e-5
L2_EPS = 1e-6
RMS_EPS = 1e-5
HD_SCALE = float(HD) ** -0.5
TS_A2 = 256
N_CORES = 8

_NC_CACHE = {}


class Sem:
    __slots__ = ("h", "n")

    def __init__(self, h):
        self.h = h
        self.n = 0

    def inc(self, instr, k=1):
        instr.then_inc(self.h, k)
        self.n += k
        return self.n


class DmaStream:

    def __init__(self, mk, name):
        self.sems = [Sem(mk.es.enter_context(mk.nc.semaphore(f"{name}_a"))),
                     Sem(mk.es.enter_context(mk.nc.semaphore(f"{name}_b")))]
        mk.all_sems.append(self.sems[0])
        mk.all_sems.append(self.sems[1])
        self.n = 0
        self.vals = {}

    def begin_item(self):
        return self.sems[self.n % 2]

    def commit(self):
        slot = self.n % 2
        self.vals[self.n] = (self.sems[slot], self.sems[slot].n)
        self.n += 1
        return self.n - 1

    def mark(self, instr):
        self.begin_item().inc(instr, 16)
        return self.commit()

    def wait(self, eng, i):
        so, v = self.vals[i]
        eng.wait_ge(so.h, v)


class PsumPool:

    def __init__(self, mk, bank_ids):
        self.mk = mk
        self.banks = bank_ids
        self.u = 0
        self.done = {}

    def begin(self, ten):
        u = self.u
        prev = self.done.get(u - len(self.banks))
        if prev is not None:
            ten.wait_ge(prev[0].h, prev[1])
        return u, self.mk.pb[self.banks[u % len(self.banks)]]

    def took(self, u):
        self.u = u + 1

    def done_mark(self, sem, val):
        self.done[self.u - 1] = (sem, val)

    def done_mark_u(self, u, sem, val):
        self.done[u] = (sem, val)


class MK:
    def __init__(self):
        nc = bass.Bass("TRN2", target_bir_lowering=False, debug=False)
        self.nc = nc
        self.es = ExitStack()
        es = self.es

        def din(name, shape, dt=F32):
            return nc.dram_tensor(name, shape, dt, kind="ExternalInput").ap()

        self.x0 = din("x", [T, H])
        self.wq = din("wq", [H, H])
        self.wk = din("wk", [H, H])
        self.wv = din("wv", [H, H])
        self.wb = din("wb", [NH, H])
        self.wo = din("wo", [H, H])
        self.cqw = din("cq", [H, 4])
        self.ckw = din("ck", [H, 4])
        self.cvw = din("cv", [H, 4])
        self.rmsw = din("rmsw", [HD, 1])
        self.gw = din("gw", [NE, H])
        self.gup = din("gup", [NE * 2 * FI, H])
        self.dwn = din("dwn", [NE * H, FI])
        self.mg = din("mg", [NE, 1], I32)
        self.dom = din("dom", [NG, 1], I32)
        self.flag = din("flag", [128, 1])
        self.sel8 = din("sel8", [NG, NG * 128])
        self.out = nc.dram_tensor("out", [T, H], F32, kind="ExternalOutput").ap()

        dr = lambda name, shape: nc.dram_tensor(name, shape, F32).ap()
        self.x1s = dr("x1s", [T, H])
        self.x2s = dr("x2s", [T, H])
        self.gut = dr("gut", [NG, NJ, 128, H])
        self.dnt = dr("dnt", [NG, NHC, 128, FI])
        self.wot = dr("wot", [NHC, 128, H])
        self.qs = dr("qs_s", [NCH, NHC, 128, C])
        self.ks = dr("ks_s", [NCH, NHC, 128, C])
        self.vs = dr("vs_s", [NCH, NHC, 128, C])
        self.loc = dr("loc", [NCH, NH, 5, 128, 128])
        self.locr = dr("locr", [NCH, 128, NH])
        self.cc_s_in = dr("cc_s_in", [128, NH * HD])
        self.cc_s_out = dr("cc_s_out", [256, NH * HD])
        self.cc_x_in = dr("cc_x_in", [3, H])
        self.cc_x_out = dr("cc_x_out", [6, H])

        self.all_sems = []

        def sem(name):
            s = Sem(es.enter_context(nc.semaphore(name)))
            self.all_sems.append(s)
            return s
        self.qw = sem("qw")
        self.qg = sem("qg")
        self.qc = sem("qc")
        self.tm = sem("tm")
        self.vm = sem("vm")
        self.sm = sem("sm")
        self.gm = sem("gm")

        sb = lambda name, shape, dt=F32: es.enter_context(nc.sbuf_tensor(name, shape, dt))
        self.ident = sb("ident", [128, 128])
        self.stril = sb("stril", [128, 128])
        self.tril0 = sb("tril0", [128, 128])
        self.ones_row = sb("ones_row", [1, 128])
        self.cw = {p: sb(f"cw_{p}", [128, 4 * NHC]) for p in "qkv"}
        self.rmsw_sb = sb("rmsw_sb", [128, 1])
        self.flag_sb = sb("flag_sb", [128, 1])
        self.gwT = sb("gwT", [128, NE * NHC])
        self.mgrp = sb("mgrp", [NE, NG])
        self.mg_sb = sb("mg_sb", [NE, 1], I32)
        self.mgf = sb("mgf", [NE, 1])
        self.dom_sb = sb("dom_sb", [NG, 1], I32)
        self.iota8 = sb("iota8", [NE, NG])
        self._sel8_sb = sb("sel8_sb", [NG, NG * 128])
        self.beta_sb = sb("beta_sb", [NH, T])
        self.SA = sb("SA", [128, NH * HD])
        self.SB = sb("SB", [128, NH * HD])
        self.trash = sb("trash", [128, H])
        self.ln1 = sb("ln_s1", [128, 1])
        self.ln2 = sb("ln_s2", [128, 1])

        self.pb = [es.enter_context(nc.psum_tensor(f"pb{i}", [128, 512], F32))
                   for i in range(8)]
        self.block = es.enter_context(nc.Block())

    def t(self, f): self.block.tensor(f)

    def v(self, f):
        def w(vec):
            if self.vm.n:
                vec.wait_ge(self.vm.h, self.vm.n)
            f(vec)
        self.block.vector(w)

    def s(self, f):
        def w(sc):
            if self.sm.n:
                sc.wait_ge(self.sm.h, self.sm.n)
            f(sc)
        self.block.scalar(w)

    def g(self, f):
        def w(gp):
            if self.gm.n:
                gp.wait_ge(self.gm.h, self.gm.n)
            f(gp)
        self.block.gpsimd(w)

    def y(self, f): self.block.sync(f)

    def emit_ln(self, xa, slot, stream, item):
        def vln(vec):
            stream.wait(vec, item)
            xt = xa[:, slot * H:(slot + 1) * H]
            self.dve_sync(vec, vec.reduce_sum(self.ln1[:, :], xt, axis=AX.X))
            self.dve_sync(vec, vec.tensor_scalar(self.ln1[:, :], self.ln1[:, :],
                                                 1.0 / H, None, op0=OP.mult))
            self.vm.inc(vec.tensor_scalar(xt, xt, self.ln1[:, 0:1], None,
                                          op0=OP.subtract))
        self.v(vln)
        vmv = self.vm.n

        def sln(sc, vmv=vmv):
            sc.wait_ge(self.vm.h, vmv)
            self.sm.inc(sc.activation(self.trash[:, :], xa[:, slot * H:(slot + 1) * H],
                                      AF.Square, accum_out=self.ln2[:, 0:1]))
        self.s(sln)
        smv = self.sm.n

        def vv(vec, smv=smv):
            vec.wait_ge(self.sm.h, smv)
            self.dve_sync(vec, vec.tensor_scalar(self.ln2[:, :], self.ln2[:, :],
                                                 1.0 / H, None, op0=OP.mult))
            self.vm.inc(vec.tensor_scalar(self.ln2[:, :], self.ln2[:, :],
                                          LN_EPS, None, op0=OP.add))
        self.v(vv)
        vmv2 = self.vm.n

        def ss(sc, vmv2=vmv2):
            sc.wait_ge(self.vm.h, vmv2)
            self.sm.inc(sc.activation(self.ln2[:, :], self.ln2[:, :], AF.Sqrt))
        self.s(ss)
        smv2 = self.sm.n

        def vv2(vec, smv2=smv2):
            sc_ = None
            vec.wait_ge(self.sm.h, smv2)
            self.dve_sync(vec, vec.reciprocal(self.ln2[:, :], self.ln2[:, :]))
            xt = xa[:, slot * H:(slot + 1) * H]
            self.vm.inc(vec.tensor_scalar(xt, xt, self.ln2[:, 0:1], None, op0=OP.mult))
        self.v(vv2)

    def dve_sync(self, vec, instr):
        val = self.vm.inc(instr)
        vec.wait_ge(self.vm.h, val)

    def tr(self, eng, out_ps, in_ap):
        p = in_ap.shape[0]
        bp = in_ap.base_partition()
        return eng.transpose(out_ps, in_ap, self.ident[bp:bp + p, bp:bp + p])

    def phase_barrier(self):
        marks = [(s, s.n) for s in self.all_sems]

        def w(eng):
            for s, v in marks:
                if v > 0:
                    eng.wait_ge(s.h, v)
        self.t(w)
        self.v(w)
        self.s(w)
        self.g(w)
        self.y(w)

    def p0(self):
        nc = self.nc

        def g0(gp):
            gp.memset(self.ident[:, :], 0.0).then_inc(self.gm.h, 1)
            self.gm.n += 1
            gp.memset(self.stril[:, :], 1.0).then_inc(self.gm.h, 1)
            self.gm.n += 1
            gp.memset(self.tril0[:, :], 1.0).then_inc(self.gm.h, 1)
            self.gm.n += 1
            gp.wait_ge(self.gm.h, self.gm.n)
            gp.affine_select(out=self.ident[:, :], in_=self.ident[:, :],
                             compare_op=OP.not_equal, fill=1.0, base=0,
                             pattern=[[-1, 128]], channel_multiplier=1)
            gp.affine_select(out=self.stril[:, :], in_=self.stril[:, :],
                             compare_op=OP.is_gt, fill=0.0, base=0,
                             pattern=[[-1, 128]], channel_multiplier=1)
            gp.affine_select(out=self.tril0[:, :], in_=self.tril0[:, :],
                             compare_op=OP.is_ge, fill=0.0, base=0,
                             pattern=[[-1, 128]], channel_multiplier=1)
            gp.memset(self.ones_row[:, :], 1.0)
            gp.iota(self.iota8[:, :], pattern=[[1, NG]], base=0,
                    channel_multiplier=0, allow_small_or_imprecise_dtypes=True)
            gp.memset(self.SA[:, :], 0.0)
            gp.dma_start(out=self.rmsw_sb[:HD, :], in_=self.rmsw).then_inc(self.qg.h, 16)
            gp.dma_start(out=self.flag_sb[:, :], in_=self.flag).then_inc(self.qg.h, 16)
            gp.dma_start(out=self.mg_sb[:, :], in_=self.mg).then_inc(self.qg.h, 16)
            gp.dma_start(out=self.dom_sb[:, :], in_=self.dom).then_inc(self.qg.h, 16)
            gp.dma_start(out=self._sel8_sb[:, :], in_=self.sel8).then_inc(self.qg.h, 16)
            self.qg.n += 80
            gp.wait_ge(self.qg.h, self.qg.n)
            self.gm.inc(gp.nop())
        self.g(g0)
        gm0 = self.gm.n

        def v0(vec):
            vec.wait_ge(self.gm.h, gm0)
            cp = vec.tensor_copy(self.mgf[:, :], self.mg_sb[:, :])
            v = self.vm.inc(cp)
            vec.wait_ge(self.vm.h, v)
            self.vm.inc(vec.tensor_tensor(self.mgrp[:, :], self.iota8[:, :],
                                          self.mgf[:, 0:1].to_broadcast([NE, NG]),
                                          op=OP.is_equal))
        self.v(v0)

        p0es = ExitStack()
        gw_raw = p0es.enter_context(nc.sbuf_tensor("gw_raw", [NE, H], F32))

        def y0(sync):
            for p, src in (("q", self.cqw), ("k", self.ckw), ("v", self.cvw)):
                for oc in range(NHC):
                    sync.dma_start(out=self.cw[p][:, oc * 4:(oc + 1) * 4],
                                   in_=src[oc * 128:(oc + 1) * 128, :]).then_inc(self.qw.h, 16)
                    self.qw.n += 16
            sync.dma_start(out=gw_raw[:, :], in_=self.gw).then_inc(self.qw.h, 16)
            self.qw.n += 16
            sync.wait_ge(self.qw.h, self.qw.n)
        self.y(y0)

        qw0, gm1 = self.qw.n, self.gm.n
        prev_vm = [None]
        for k in range(NHC):
            def t0(ten, k=k):
                if k == 0:
                    ten.wait_ge(self.qw.h, qw0)
                    ten.wait_ge(self.gm.h, gm1)
                else:
                    ten.wait_ge(self.vm.h, prev_vm[0])
                self.tm.inc(self.tr(ten, self.pb[6][:, 0:NE], gw_raw[:, k * 128:(k + 1) * 128]))
            self.t(t0)
            tmv = self.tm.n

            def v1(vec, k=k, tmv=tmv):
                vec.wait_ge(self.tm.h, tmv)
                cp = vec.tensor_copy(self.gwT[:, k * NE:(k + 1) * NE], self.pb[6][:, 0:NE])
                prev_vm[0] = self.vm.inc(cp)
            self.v(v1)

        raw_gu = p0es.enter_context(nc.sbuf_tensor("p0_rawgu", [128, 2 * H], F32))
        stg_gu = p0es.enter_context(nc.sbuf_tensor("p0_stggu", [128, 2 * H], F32))
        raw_dn = p0es.enter_context(nc.sbuf_tensor("p0_rawdn", [128, 2 * FI], F32))
        stg_dn = p0es.enter_context(nc.sbuf_tensor("p0_stgdn", [128, 2 * FI], F32))

        slabs = ([("gu", gi, j) for gi in range(NG) for j in range(NJ)]
                 + [("dn", gi, f) for gi in range(NG) for f in range(NHC)]
                 + [("wo", osl, 0) for osl in range(NHC)])
        geom = {
            "gu": dict(raw=raw_gu, stg=stg_gu, w=H, nt=16, buf="A"),
            "dn": dict(raw=raw_dn, stg=stg_dn, w=FI, nt=6, buf="B"),
            "wo": dict(raw=raw_gu, stg=stg_gu, w=H, nt=16, buf="A"),
        }
        buf_use = {"A": 0, "B": 0}
        slab_bufuse = {}
        bufuse_to_slab = {}
        for n, (kind, a, b) in enumerate(slabs):
            bk = geom[kind]["buf"]
            u = buf_use[bk]
            buf_use[bk] = u + 1
            slab_bufuse[n] = u
            bufuse_to_slab[(bk, u)] = n
        tm_base = self.tm.n
        qg_base = self.qg.n
        load_v = {n: qg_base + 16 * (n + 1) for n in range(len(slabs))}
        done_marks = {}

        gug = self.nc.dram_tensor("gug", [NG * 2 * FI, H], F32).ap()
        dng = self.nc.dram_tensor("dng", [NG * H, FI], F32).ap()

        def ggather(gp):
            with gp.register("eo_p0") as eo:
                for gi in range(NG):
                    gp.reg_load(eo, self.dom_sb[gi:gi + 1, 0:1])
                    gp.reg_mul(eo, eo, 2 * FI * H)
                    gp.dma_start(out=gug[gi * 2 * FI:(gi + 1) * 2 * FI, :],
                                 in_=bass.AP(self.gup.tensor, eo,
                                             [[H, 2 * FI], [1, H]])
                                 ).then_inc(self.qg.h, 16)
                    self.qg.n += 16
                for gi in range(NG):
                    gp.reg_load(eo, self.dom_sb[gi:gi + 1, 0:1])
                    gp.reg_mul(eo, eo, H * FI)
                    gp.dma_start(out=dng[gi * H:(gi + 1) * H, :],
                                 in_=bass.AP(self.dwn.tensor, eo,
                                             [[FI, H], [1, FI]])
                                 ).then_inc(self.qg.h, 16)
                    self.qg.n += 16
        gath_base = self.qg.n
        self.g(ggather)
        assert self.qg.n == gath_base + 16 * 16
        gath_total = self.qg.n
        stA = DmaStream(self, "p0A")
        stB = DmaStream(self, "p0B")
        streams = {"A": stA, "B": stB}
        stores = {"A": DmaStream(self, "p0sA"), "B": DmaStream(self, "p0sB")}
        store_item = {}
        load_item = {}

        def gall(gp):
            gp.wait_ge(self.qg.h, gath_total)
            for n, (kind, a, b) in enumerate(slabs):
                gg = geom[kind]
                st = streams[gg["buf"]]
                u = slab_bufuse[n]
                slot = u % 2
                prev_slab = bufuse_to_slab.get((gg["buf"], u - 2))
                if prev_slab is not None:
                    gp.wait_ge(self.tm.h, tm_base + prev_slab + 1)
                dst = gg["raw"][:, slot * gg["w"]:(slot + 1) * gg["w"]]
                if kind == "gu":
                    r0 = a * 2 * FI + b * 128
                    dd = gp.dma_start(out=dst, in_=gug[r0:r0 + 128, :])
                elif kind == "dn":
                    r0 = a * H + b * 128
                    dd = gp.dma_start(out=dst, in_=dng[r0:r0 + 128, :])
                else:
                    dd = gp.dma_start(out=dst, in_=self.wo[a * 128:(a + 1) * 128, :])
                load_item[n] = (st, st.mark(dd))
        self.g(gall)

        for n in range(len(slabs)):
            kind, a, b = slabs[n]
            gg = geom[kind]
            u = slab_bufuse[n]
            slot = u % 2
            nt = gg["nt"]
            ngrp = (nt + 3) // 4

            def tt(ten, n=n, gg=gg, slot=slot, nt=nt, ngrp=ngrp, u=u):
                st_, it_ = load_item[n]
                st_.wait(ten, it_)
                prev = done_marks.get(n - 1)
                if prev is not None:
                    for so, val in prev:
                        ten.wait_ge(so.h, val)
                src = gg["raw"][:, slot * gg["w"]:(slot + 1) * gg["w"]]
                for grp in range(ngrp):
                    bank = self.pb[grp]
                    for q in range(min(4, nt - grp * 4)):
                        k = grp * 4 + q
                        mm = self.tr(ten, bank[:, q * 128:(q + 1) * 128],
                                     src[:, k * 128:(k + 1) * 128])
                self.tm.inc(mm)
                assert self.tm.n == tm_base + n + 1
            self.t(tt)
            tmv = self.tm.n
            marks = []

            prev_store = store_item.get((gg["buf"], u - 2))

            def emit_evac(eng, semo, is_v, kind=kind, gg=gg, slot=slot, nt=nt,
                          ngrp=ngrp, tmv=tmv, prev_store=prev_store):
                eng.wait_ge(self.tm.h, tmv)
                if prev_store is not None:
                    eng.wait_ge(prev_store[0].h, prev_store[1])
                cp = None
                for grp in range(ngrp):
                    take_v = (grp % 2 == 0) or (kind == "wo")
                    if take_v != is_v:
                        continue
                    bank = self.pb[grp]
                    w = min(4, nt - grp * 4) * 128
                    dst = gg["stg"][:, slot * gg["w"] + grp * 512:
                                    slot * gg["w"] + grp * 512 + w]
                    if kind == "wo":
                        cp = eng.tensor_scalar(dst, bank[:, :w], self.rmsw_sb[:, 0:1],
                                               None, op0=OP.mult)
                    else:
                        if is_v:
                            cp = eng.tensor_copy(dst, bank[:, :w])
                        else:
                            cp = eng.copy(dst, bank[:, :w])
                if cp is not None:
                    marks.append((semo, semo.inc(cp)))
            self.v(lambda vec: emit_evac(vec, self.vm, True))
            if kind != "wo" and ngrp > 1:
                self.s(lambda sc: emit_evac(sc, self.sm, False))
            done_marks[n] = list(marks)

            def yw(sync, kind=kind, a=a, b=b, gg=gg, slot=slot, marks=tuple(marks), u=u):
                for so, val in marks:
                    sync.wait_ge(so.h, val)
                src = gg["stg"][:, slot * gg["w"]:(slot + 1) * gg["w"]]
                dstd = (self.gut[a, b] if kind == "gu"
                        else self.dnt[a, b] if kind == "dn" else self.wot[a])
                dd = sync.dma_start(out=dstd, in_=src)
                it = stores[gg["buf"]].mark(dd)
                store_item[(gg["buf"], u)] = stores[gg["buf"]].vals[it]
            self.y(yw)
        p0es.close()

    def ffn(self, x_src, x_dst, tag):
        nc = self.nc
        es = ExitStack()
        xlnT = es.enter_context(nc.sbuf_tensor(f"xlnT_{tag}", [128, NHC * TBLK], F32))
        oacc = es.enter_context(nc.sbuf_tensor(f"oacc_{tag}", [128, NHC * TBLK], F32))
        act = es.enter_context(nc.sbuf_tensor(f"act_{tag}", [128, NI * TBLK], F32))
        s_t = es.enter_context(nc.sbuf_tensor(f"s_t_{tag}", [128, TBLK], F32))
        wbc = es.enter_context(nc.sbuf_tensor(f"wbc_{tag}", [128, TBLK], F32))
        xa = es.enter_context(nc.sbuf_tensor(f"xa_{tag}", [128, 2 * H], F32))
        xo = es.enter_context(nc.sbuf_tensor(f"xo_{tag}", [128, 2 * H], F32))
        gub = es.enter_context(nc.sbuf_tensor(f"gub_{tag}", [128, 2 * H], F32))
        dnb = es.enter_context(nc.sbuf_tensor(f"dnb_{tag}", [128, 2 * FI], F32))
        wrows = es.enter_context(nc.sbuf_tensor(f"wrows_{tag}", [NG, TBLK], F32))
        gsc = es.enter_context(nc.sbuf_tensor(f"gsc_{tag}", [128, 5 * NE], F32))
        gs1 = es.enter_context(nc.sbuf_tensor(f"gs1_{tag}", [128, 4], F32))
        m8 = es.enter_context(nc.sbuf_tensor(f"m8_{tag}", [128, 8], F32))
        lgf = es.enter_context(nc.sbuf_tensor(f"lgf_{tag}", [NE, 128], F32))
        wtg = es.enter_context(nc.sbuf_tensor(f"wtg_{tag}", [128, NG], F32))

        seq = []
        for tb in range(NBLK):
            for gi in range(NG):
                for jp in range(NI):
                    seq.append(("gu", gi, jp))
                    seq.append(("gu", gi, jp + NI))
                for f in range(NHC):
                    seq.append(("dn", gi, f))

        kctr = {"gu": 0, "dn": 0}
        kuse = {"gu": 0, "dn": 0}
        cons_v = {}
        st_slab = {"gu": DmaStream(self, f"f{tag}_gu"), "dn": DmaStream(self, f"f{tag}_dn")}

        def emit_slab_load(idx):
            kind, gi, j = seq[idx]
            ki = kctr[kind]
            kctr[kind] = ki + 1
            slot = ki % 2
            cred = cons_v.get((kind, ki - 2))
            buf, w = (gub, H) if kind == "gu" else (dnb, FI)
            src = self.gut[gi, j] if kind == "gu" else self.dnt[gi, j]

            def yl(sync, buf=buf, w=w, src=src, slot=slot, cred=cred, kind=kind, ki=ki):
                if cred is not None:
                    sync.wait_ge(self.tm.h, cred)
                dd = sync.dma_start(out=buf[:, slot * w:(slot + 1) * w], in_=src)
                it = st_slab[kind].mark(dd)
                assert it == ki
            self.y(yl)

        pbu = {"n": 0}
        evac_done = {}

        def bank_begin(ten):
            u = pbu["n"]
            prev = evac_done.get(u - 2)
            if prev is not None:
                ten.wait_ge(prev[0].h, prev[1])
            return u, self.pb[u % 2]

        def bank_end(u):
            pbu["n"] = u + 1

        tpbu = {"n": 0}
        tevac_done = {}

        def tbank_begin(ten):
            u = tpbu["n"]
            prev = tevac_done.get(u - 4)
            if prev is not None:
                ten.wait_ge(prev[0].h, prev[1])
            return u, self.pb[2 + (u % 4)]

        def tbank_end(u):
            tpbu["n"] = u + 1

        gpbu = {"n": 0}
        gevac_done = {}

        def gbank_begin(ten):
            u = gpbu["n"]
            prev = gevac_done.get(u - 1)
            if prev is not None:
                ten.wait_ge(prev[0].h, prev[1])
            return u, self.pb[6]

        def gbank_end(u):
            gpbu["n"] = u + 1

        def gmark(val):
            gevac_done[gpbu["n"] - 1] = (self.vm, val)

        xa_credit = {}
        xo_credit = {}
        st_xa = DmaStream(self, f"f{tag}_xa")
        st_xo = DmaStream(self, f"f{tag}_xo")
        st_xs = DmaStream(self, f"f{tag}_xs")
        last_ep_tm = [None]
        silu_done = [0]
        up_done = [0]
        seq_pos = [0]
        emit_slab_load(0)
        emit_slab_load(1)

        for tb in range(NBLK):
            for tc in range(4):
                slot = tc % 2
                cred = xa_credit.get(tb * 4 + tc - 2)

                def yx(sync, tb=tb, tc=tc, slot=slot, cred=cred):
                    if cred is not None:
                        sync.wait_ge(cred[0].h, cred[1])
                    r0 = tb * TBLK + tc * 128
                    dd = sync.dma_start(out=xa[:, slot * H:(slot + 1) * H],
                                        in_=x_src[r0:r0 + 128, :])
                    st_xa.mark(dd)
                self.y(yx)
                xa_item = st_xa.n - 1

                def vln(vec, slot=slot, xa_item=xa_item):
                    st_xa.wait(vec, xa_item)
                    xt = xa[:, slot * H:(slot + 1) * H]
                    self.dve_sync(vec, vec.reduce_sum(self.ln1[:, :], xt, axis=AX.X))
                    self.dve_sync(vec, vec.tensor_scalar(self.ln1[:, :], self.ln1[:, :],
                                                         1.0 / H, None, op0=OP.mult))
                    self.vm.inc(vec.tensor_scalar(xt, xt, self.ln1[:, 0:1], None,
                                                  op0=OP.subtract))
                self.v(vln)
                vmv = self.vm.n

                def sln(sc, slot=slot, vmv=vmv):
                    sc.wait_ge(self.vm.h, vmv)
                    self.sm.inc(sc.activation(self.trash[:, :], xa[:, slot * H:(slot + 1) * H],
                                              AF.Square, accum_out=self.ln2[:, 0:1]))
                self.s(sln)
                smv = self.sm.n

                def vv(vec, smv=smv):
                    vec.wait_ge(self.sm.h, smv)
                    self.dve_sync(vec, vec.tensor_scalar(self.ln2[:, :], self.ln2[:, :],
                                                         1.0 / H, None, op0=OP.mult))
                    self.vm.inc(vec.tensor_scalar(self.ln2[:, :], self.ln2[:, :],
                                                  LN_EPS, None, op0=OP.add))
                self.v(vv)
                vmv = self.vm.n

                def ss(sc, vmv=vmv):
                    sc.wait_ge(self.vm.h, vmv)
                    self.sm.inc(sc.activation(self.ln2[:, :], self.ln2[:, :], AF.Sqrt))
                self.s(ss)
                smv = self.sm.n

                def vv2(vec, slot=slot, smv=smv):
                    vec.wait_ge(self.sm.h, smv)
                    self.dve_sync(vec, vec.reciprocal(self.ln2[:, :], self.ln2[:, :]))
                    xt = xa[:, slot * H:(slot + 1) * H]
                    self.vm.inc(vec.tensor_scalar(xt, xt, self.ln2[:, 0:1], None,
                                                  op0=OP.mult))
                self.v(vv2)
                v_ln = self.vm.n

                tgrp = []
                for grp in range(4):
                    def tx(ten, grp=grp, slot=slot, v_ln=v_ln):
                        u, bank = tbank_begin(ten)
                        if grp == 0:
                            ten.wait_ge(self.vm.h, v_ln)
                        for q in range(4):
                            k = grp * 4 + q
                            mm = self.tr(ten, bank[:, q * 128:(q + 1) * 128],
                                         xa[:, slot * H + k * 128:slot * H + (k + 1) * 128])
                        self.tm.inc(mm)
                        tgrp.append((u, bank, self.tm.n))
                        tbank_end(u)
                    self.t(tx)
                xa_credit[tb * 4 + tc] = (self.tm, tgrp[3][2])

                ev_marks = []
                for grp in range(4):
                    u, bank, tmv = tgrp[grp]

                    def ev(eng, semo, is_v, grp=grp, bank=bank, tmv=tmv, tc=tc, u=u):
                        eng.wait_ge(self.tm.h, tmv)
                        cp = None
                        for q in range(4):
                            k = grp * 4 + q
                            dst = xlnT[:, k * TBLK + tc * 128:k * TBLK + tc * 128 + 128]
                            if is_v:
                                cp = eng.tensor_copy(dst, bank[:, q * 128:(q + 1) * 128])
                            else:
                                cp = eng.copy(dst, bank[:, q * 128:(q + 1) * 128])
                        val = semo.inc(cp)
                        ev_marks.append((semo, val))
                        tevac_done[u] = (semo, val)
                    if grp % 2 == 0:
                        self.v(lambda vec, grp=grp, bank=bank, tmv=tmv, u=u:
                               ev(vec, self.vm, True, grp, bank, tmv, tc, u))
                    else:
                        self.s(lambda sc, grp=grp, bank=bank, tmv=tmv, u=u:
                               ev(sc, self.sm, False, grp, bank, tmv, tc, u))

                def tlg(ten, tc=tc, ev_marks=tuple(ev_marks)):
                    u, bank = gbank_begin(ten)
                    for so, val in ev_marks:
                        ten.wait_ge(so.h, val)
                    for k in range(NHC):
                        mm = ten.matmul(bank[:NE, 0:128],
                                        self.gwT[:, k * NE:(k + 1) * NE],
                                        xlnT[:, k * TBLK + tc * 128:k * TBLK + tc * 128 + 128],
                                        start=(k == 0), stop=(k == NHC - 1))
                    self.tm.inc(mm)
                    gbank_end(u)
                self.t(tlg)
                tmv = self.tm.n

                def vlg(vec, tmv=tmv):
                    vec.wait_ge(self.tm.h, tmv)
                    gmark(self.vm.inc(vec.tensor_copy(lgf[:, :], self.pb[6][:NE, 0:128])))
                self.v(vlg)
                vmv = self.vm.n

                def tlg2(ten, vmv=vmv):
                    u, bank = gbank_begin(ten)
                    ten.wait_ge(self.vm.h, vmv)
                    self.tm.inc(self.tr(ten, bank[:, 128:128 + NE], lgf[:, :]))
                    gbank_end(u)
                self.t(tlg2)
                tmv = self.tm.n

                def vsm(vec, tmv=tmv):
                    vec.wait_ge(self.tm.h, tmv)
                    cp = vec.tensor_copy(gsc[:, 0:NE], self.pb[6][:, 128:128 + NE])
                    gmark(self.vm.inc(cp))
                    vec.wait_ge(self.vm.h, self.vm.n)
                    self.dve_sync(vec, vec.reduce_max(gs1[:, 0:1], gsc[:, 0:NE], axis=AX.X))
                    self.vm.inc(vec.tensor_scalar(gsc[:, 0:NE], gsc[:, 0:NE], gs1[:, 0:1],
                                                  None, op0=OP.subtract))
                self.v(vsm)
                vmv = self.vm.n

                def sexp(sc, vmv=vmv):
                    sc.wait_ge(self.vm.h, vmv)
                    self.sm.inc(sc.activation(gsc[:, NE:2 * NE], gsc[:, 0:NE], AF.Exp))
                self.s(sexp)
                smv = self.sm.n

                def vtop(vec, smv=smv):
                    vec.wait_ge(self.sm.h, smv)
                    pex = gsc[:, NE:2 * NE]
                    probs = gsc[:, 2 * NE:3 * NE]
                    self.dve_sync(vec, vec.reduce_sum(gs1[:, 1:2], pex, axis=AX.X))
                    self.dve_sync(vec, vec.reciprocal(gs1[:, 1:2], gs1[:, 1:2]))
                    self.dve_sync(vec, vec.tensor_scalar(probs, pex, gs1[:, 1:2], None,
                                                         op0=OP.mult))
                    mx8 = vec.max(out=m8[:, :], in_=probs)
                    v = self.vm.inc(mx8)
                    for _ in range(8):
                        vec.engine_nop()
                    vec.wait_ge(self.vm.h, v)
                    self.dve_sync(vec, vec.reduce_sum(gs1[:, 2:3], m8[:, :], axis=AX.X))
                    self.dve_sync(vec, vec.reciprocal(gs1[:, 2:3], gs1[:, 2:3]))
                    mask = gsc[:, 3 * NE:4 * NE]
                    self.dve_sync(vec, vec.tensor_tensor(
                        mask, probs, m8[:, 7:8].to_broadcast([128, NE]), op=OP.is_ge))
                    pm = gsc[:, 4 * NE:5 * NE]
                    self.dve_sync(vec, vec.tensor_tensor(pm, probs, mask, op=OP.mult))
                    self.vm.inc(vec.tensor_scalar(pm, pm, gs1[:, 2:3], None, op0=OP.mult))
                self.v(vtop)
                vmv = self.vm.n

                def tpm(ten, vmv=vmv):
                    u, bank = gbank_begin(ten)
                    ten.wait_ge(self.vm.h, vmv)
                    self.tm.inc(self.tr(ten, bank[:NE, 0:128], gsc[:, 4 * NE:5 * NE]))
                    gbank_end(u)
                self.t(tpm)
                tmv = self.tm.n

                def vpm(vec, tmv=tmv):
                    vec.wait_ge(self.tm.h, tmv)
                    gmark(self.vm.inc(vec.tensor_copy(lgf[:, :], self.pb[6][:NE, 0:128])))
                self.v(vpm)
                vmv = self.vm.n

                def twtg(ten, vmv=vmv):
                    u, bank = gbank_begin(ten)
                    ten.wait_ge(self.vm.h, vmv)
                    mm = ten.matmul(bank[:, 128:128 + NG], lgf[:, :], self.mgrp[:, :],
                                    start=True, stop=True)
                    self.tm.inc(mm)
                    gbank_end(u)
                self.t(twtg)
                tmv = self.tm.n

                def vwtg(vec, tmv=tmv):
                    vec.wait_ge(self.tm.h, tmv)
                    gmark(self.vm.inc(vec.tensor_copy(wtg[:, :], self.pb[6][:, 128:128 + NG])))
                self.v(vwtg)
                vmv = self.vm.n

                def twr(ten, vmv=vmv):
                    u, bank = gbank_begin(ten)
                    ten.wait_ge(self.vm.h, vmv)
                    self.tm.inc(self.tr(ten, bank[:NG, 0:128], wtg[:, :]))
                    gbank_end(u)
                self.t(twr)
                tmv = self.tm.n

                def vwr(vec, tmv=tmv, tc=tc):
                    vec.wait_ge(self.tm.h, tmv)
                    cp = vec.tensor_copy(wrows[:, tc * 128:(tc + 1) * 128],
                                         self.pb[6][:NG, 0:128])
                    gmark(self.vm.inc(cp))
                self.v(vwr)

            wrows_ready = self.vm.n

            for gi in range(NG):
                def twb(ten, gi=gi, wrows_ready=wrows_ready):
                    u, bank = gbank_begin(ten)
                    if gi == 0:
                        ten.wait_ge(self.vm.h, wrows_ready)
                    mm = ten.matmul(bank[:, 0:TBLK],
                                    self.sel8[gi] if False else self._sel8_sb[:, gi * 128:(gi + 1) * 128],
                                    wrows[0:NG, :], start=True, stop=True)
                    self.tm.inc(mm)
                    gbank_end(u)
                self.t(twb)
                tmv = self.tm.n

                def vwb(vec, tmv=tmv, gi=gi):
                    vec.wait_ge(self.tm.h, tmv)
                    if gi > 0:
                        vec.wait_ge(self.vm.h, up_done[0])
                    gmark(self.vm.inc(vec.tensor_copy(wbc[:, :], self.pb[6][:, 0:TBLK])))
                self.v(vwb)
                wbc_ready = self.vm.n

                for jp in range(NI):
                    for half, j in ((0, jp), (1, jp + NI)):
                        ki = kuse["gu"]
                        kuse["gu"] = ki + 1
                        slot = ki % 2

                        def tgu(ten, slot=slot, ki=ki):
                            u, bank = bank_begin(ten)
                            st_slab["gu"].wait(ten, ki)
                            for k in range(NHC):
                                mm = ten.matmul(bank[:, 0:TBLK],
                                                gub[:, slot * H + k * 128:slot * H + (k + 1) * 128],
                                                xlnT[:, k * TBLK:(k + 1) * TBLK],
                                                start=(k == 0), stop=(k == NHC - 1))
                            self.tm.inc(mm)
                            cons_v[("gu", ki)] = self.tm.n
                            bank_end(u)
                        self.t(tgu)
                        u_used = pbu["n"] - 1
                        tmv = self.tm.n
                        seq_pos[0] += 1
                        if seq_pos[0] + 1 < len(seq):
                            emit_slab_load(seq_pos[0] + 1)

                        if half == 0:
                            def sgate(sc, u=u_used, tmv=tmv):
                                sc.wait_ge(self.tm.h, tmv)
                                if up_done[0]:
                                    sc.wait_ge(self.vm.h, up_done[0])
                                cp = sc.activation(s_t[:, :], self.pb[u % 2][:, 0:TBLK],
                                                   AF.Silu)
                                evac_done[u] = (self.sm, self.sm.inc(cp))
                            self.s(sgate)
                            silu_done[0] = self.sm.n
                        else:
                            def vup(vec, u=u_used, tmv=tmv, jp=jp,
                                    wbc_ready=wbc_ready, sd=silu_done[0]):
                                vec.wait_ge(self.tm.h, tmv)
                                vec.wait_ge(self.sm.h, sd)
                                if jp == 0:
                                    vec.wait_ge(self.vm.h, wbc_ready)
                                dst = act[:, jp * TBLK:(jp + 1) * TBLK]
                                self.dve_sync(vec, vec.tensor_tensor(
                                    dst, self.pb[u % 2][:, 0:TBLK], s_t[:, :], op=OP.mult))
                                cp = vec.tensor_tensor(dst, dst, wbc[:, :], op=OP.mult)
                                evac_done[u] = (self.vm, self.vm.inc(cp))
                            self.v(vup)
                            up_done[0] = self.vm.n

                act_ready = self.vm.n
                for f in range(NHC):
                    ki = kuse["dn"]
                    kuse["dn"] = ki + 1
                    slot = ki % 2

                    def tdn(ten, f=f, slot=slot, ki=ki, act_ready=act_ready):
                        u, bank = bank_begin(ten)
                        st_slab["dn"].wait(ten, ki)
                        if f == 0:
                            ten.wait_ge(self.vm.h, act_ready)
                        for i in range(NI):
                            mm = ten.matmul(bank[:, 0:TBLK],
                                            dnb[:, slot * FI + i * 128:slot * FI + (i + 1) * 128],
                                            act[:, i * TBLK:(i + 1) * TBLK],
                                            start=(i == 0), stop=(i == NI - 1))
                        self.tm.inc(mm)
                        cons_v[("dn", ki)] = self.tm.n
                        bank_end(u)
                    self.t(tdn)
                    u_used = pbu["n"] - 1
                    tmv = self.tm.n
                    seq_pos[0] += 1
                    if seq_pos[0] + 1 < len(seq):
                        emit_slab_load(seq_pos[0] + 1)

                    def vdn(vec, u=u_used, tmv=tmv, f=f, gi=gi, tb=tb):
                        vec.wait_ge(self.tm.h, tmv)
                        if gi == 0 and f == 0 and tb > 0 and last_ep_tm[0]:
                            vec.wait_ge(self.tm.h, last_ep_tm[0])
                        dst = oacc[:, f * TBLK:(f + 1) * TBLK]
                        if gi == 0:
                            cp = vec.tensor_copy(dst, self.pb[u % 2][:, 0:TBLK])
                        else:
                            cp = vec.tensor_tensor(dst, dst, self.pb[u % 2][:, 0:TBLK],
                                                   op=OP.add)
                        evac_done[u] = (self.vm, self.vm.inc(cp))
                    self.v(vdn)
            oacc_ready = self.vm.n

            for tc in range(4):
                slot = tc % 2
                cred = xo_credit.get(tb * 4 + tc - 2)

                def yx2(sync, tb=tb, tc=tc, slot=slot, cred=cred):
                    if cred is not None:
                        sync.wait_ge(cred[0].h, cred[1])
                    r0 = tb * TBLK + tc * 128
                    dd = sync.dma_start(out=xo[:, slot * H:(slot + 1) * H],
                                        in_=x_src[r0:r0 + 128, :])
                    st_xo.mark(dd)
                self.y(yx2)
                xo_item = st_xo.n - 1

                ep = []
                for grp in range(4):
                    def tep(ten, grp=grp, tc=tc, oacc_ready=oacc_ready):
                        u, bank = tbank_begin(ten)
                        if grp == 0 and tc == 0:
                            ten.wait_ge(self.vm.h, oacc_ready)
                        for q in range(4):
                            f = grp * 4 + q
                            mm = self.tr(ten, bank[:, q * 128:(q + 1) * 128],
                                         oacc[:, f * TBLK + tc * 128:f * TBLK + tc * 128 + 128])
                        self.tm.inc(mm)
                        ep.append((u, bank, self.tm.n))
                        tbank_end(u)
                    self.t(tep)
                last_ep_tm[0] = self.tm.n

                st_marks = []
                for grp in range(4):
                    u, bank, tmv = ep[grp]

                    def vev(vec, grp=grp, bank=bank, tmv=tmv, slot=slot, xo_item=xo_item, u=u):
                        vec.wait_ge(self.tm.h, tmv)
                        st_xo.wait(vec, xo_item)
                        cp = None
                        for q in range(4):
                            f = grp * 4 + q
                            dst = xo[:, slot * H + f * 128:slot * H + (f + 1) * 128]
                            cp = vec.scalar_tensor_tensor(
                                dst, bank[:, q * 128:(q + 1) * 128], 0.5, dst,
                                op0=OP.mult, op1=OP.add)
                        val = self.vm.inc(cp)
                        st_marks.append((self.vm, val))
                        tevac_done[u] = (self.vm, val)
                    self.v(vev)

                def yst(sync, tb=tb, tc=tc, slot=slot, st_marks=tuple(st_marks)):
                    for so, val in st_marks:
                        sync.wait_ge(so.h, val)
                    r0 = tb * TBLK + tc * 128
                    dd = sync.dma_start(out=x_dst[r0:r0 + 128, :],
                                        in_=xo[:, slot * H:(slot + 1) * H])
                    it = st_xs.mark(dd)
                    xo_credit[tb * 4 + tc] = st_xs.vals[it]
                self.y(yst)

        es.close()

    def attn_proj(self):
        nc = self.nc
        es = ExitStack()
        xlnT = es.enter_context(nc.sbuf_tensor("a1_xlnT", [128, NHC * T], F32))
        xa = es.enter_context(nc.sbuf_tensor("a1_xa", [128, H], F32))
        wraw = es.enter_context(nc.sbuf_tensor("a1_wraw", [128, H], F32))
        wT = es.enter_context(nc.sbuf_tensor("a1_wT", [128, H], F32))
        ext = es.enter_context(nc.sbuf_tensor("a1_ext", [128, 520], F32))
        qcA = es.enter_context(nc.sbuf_tensor("a1_qcA", [128, TBLK], F32))
        qcB = es.enter_context(nc.sbuf_tensor("a1_qcB", [128, TBLK], F32))
        qcv = es.enter_context(nc.sbuf_tensor("a1_qcv", [128, 2 * TBLK], F32))
        pr3 = es.enter_context(nc.sbuf_tensor("a1_pr3", [128, 3], F32))
        pr3i = es.enter_context(nc.sbuf_tensor("a1_pr3i", [128, 3], F32))
        x3row = self.trash[0:3, :]
        xln3T = es.enter_context(nc.sbuf_tensor("a1_xln3T", [128, 3 * NHC], F32))
        wbraw = self.trash[32:48, :]
        wbT = es.enter_context(nc.sbuf_tensor("a1_wbT", [128, NH * NHC], F32))
        l3a = es.enter_context(nc.sbuf_tensor("a1_l3a", [3, 1], F32))
        l3b = es.enter_context(nc.sbuf_tensor("a1_l3b", [3, 1], F32))

        st_xa = DmaStream(self, "a1_sxa")
        st_w = DmaStream(self, "a1_sw")
        st_qcv = DmaStream(self, "a1_sqcv")
        self._a1_xa_cons = {}
        self._a1_w_cons = {}

        def gcc(gp):
            gp.dma_start(out=self.cc_x_in[:, :], in_=self.x1s[T - 3:T, :]
                         ).then_inc(self.qg.h, 16)
            self.qg.n += 16
            gp.wait_ge(self.qg.h, self.qg.n)
            gp.collective_compute(
                "AllGather", OP.bypass,
                replica_groups=[[0, 1], [2, 3], [4, 5], [6, 7]],
                ins=[self.cc_x_in.opt()],
                outs=[self.cc_x_out.opt()],
            ).then_inc(self.gm.h, 1)
            self.gm.n += 1
            gp.wait_ge(self.gm.h, self.gm.n)
            gp.dma_start(out=x3row[:, :], in_=self.cc_x_out[0:3, :]
                         ).then_inc(self.qg.h, 16)
            self.qg.n += 16
            gp.wait_ge(self.qg.h, self.qg.n)
            self.gm.inc(gp.nop())
        self.g(gcc)
        gm_x3 = self.gm.n

        def v3a(vec):
            vec.wait_ge(self.gm.h, gm_x3)
            self.dve_sync(vec, vec.reduce_sum(l3a[:, :], x3row[:, :], axis=AX.X))
            self.dve_sync(vec, vec.tensor_scalar(l3a[:, :], l3a[:, :], 1.0 / H, None,
                                                 op0=OP.mult))
            self.vm.inc(vec.tensor_scalar(x3row[:, :], x3row[:, :], l3a[:, 0:1], None,
                                          op0=OP.subtract))
        self.v(v3a)
        vmv = self.vm.n

        def s3a(sc, vmv=vmv):
            sc.wait_ge(self.vm.h, vmv)
            self.sm.inc(sc.activation(self.trash[64:67, :], x3row[:, :], AF.Square,
                                      accum_out=l3b[:, 0:1]))
        self.s(s3a)
        smv = self.sm.n

        def v3b(vec, smv=smv):
            vec.wait_ge(self.sm.h, smv)
            self.dve_sync(vec, vec.tensor_scalar(l3b[:, :], l3b[:, :], 1.0 / H, None,
                                                 op0=OP.mult))
            self.vm.inc(vec.tensor_scalar(l3b[:, :], l3b[:, :], LN_EPS, None, op0=OP.add))
        self.v(v3b)
        smv2 = self.vm.n

        def s3b(sc, smv2=smv2):
            sc.wait_ge(self.vm.h, smv2)
            self.sm.inc(sc.activation(l3b[:, :], l3b[:, :], AF.Sqrt))
        self.s(s3b)
        smv3 = self.sm.n

        def v3c(vec, smv3=smv3):
            vec.wait_ge(self.sm.h, smv3)
            self.dve_sync(vec, vec.reciprocal(l3b[:, :], l3b[:, :]))
            self.vm.inc(vec.tensor_scalar(x3row[:, :], x3row[:, :], l3b[:, 0:1], None,
                                          op0=OP.mult))
        self.v(v3c)
        v_x3 = self.vm.n

        def yb(sync):
            sync.dma_start(out=wbraw[:, :], in_=self.wb).then_inc(self.qw.h, 16)
            self.qw.n += 16
            sync.wait_ge(self.qw.h, self.qw.n)
        self.y(yb)
        qw_wb = self.qw.n

        pool_t = PsumPool(self, [2, 3, 4, 5])
        pool_m = PsumPool(self, [0, 1])
        pool_x = PsumPool(self, [7])

        for k in range(NHC):
            def t3(ten, k=k, v_x3=v_x3):
                u, bank = pool_x.begin(ten)
                if k == 0:
                    ten.wait_ge(self.vm.h, v_x3)
                self.tm.inc(self.tr(ten, bank[:, 0:3], x3row[:, k * 128:(k + 1) * 128]))
                pool_x.took(u)
            self.t(t3)
            tmv = self.tm.n

            def v3d(vec, k=k, tmv=tmv):
                vec.wait_ge(self.tm.h, tmv)
                cp = vec.tensor_copy(xln3T[:, k * 3:(k + 1) * 3], self.pb[7][:, 0:3])
                pool_x.done_mark(self.vm, self.vm.inc(cp))
            self.v(v3d)

        for k in range(NHC):
            def tb(ten, k=k, qw_wb=qw_wb):
                u, bank = pool_x.begin(ten)
                if k == 0:
                    ten.wait_ge(self.qw.h, qw_wb)
                self.tm.inc(self.tr(ten, bank[:, 0:NH], wbraw[:, k * 128:(k + 1) * 128]))
                pool_x.took(u)
            self.t(tb)
            tmv = self.tm.n

            def vb(vec, k=k, tmv=tmv):
                vec.wait_ge(self.tm.h, tmv)
                cp = vec.tensor_copy(wbT[:, k * NH:(k + 1) * NH], self.pb[7][:, 0:NH])
                pool_x.done_mark(self.vm, self.vm.inc(cp))
            self.v(vb)

        for tc in range(NCH):
            slot = 0

            def yx(sync, tc=tc):
                if tc >= 1:
                    sync.wait_ge(self.tm.h, self._a1_xa_cons[tc - 1])
                dd = sync.dma_start(out=xa[:, :],
                                    in_=self.x1s[tc * 128:(tc + 1) * 128, :])
                st_xa.mark(dd)
            self.y(yx)
            xa_item = st_xa.n - 1
            self.emit_ln(xa, slot, st_xa, xa_item)
            v_ln = self.vm.n

            tg = []
            for grp in range(4):
                def tx(ten, grp=grp, v_ln=v_ln):
                    u, bank = pool_t.begin(ten)
                    if grp == 0:
                        ten.wait_ge(self.vm.h, v_ln)
                    for q in range(4):
                        k = grp * 4 + q
                        mm = self.tr(ten, bank[:, q * 128:(q + 1) * 128],
                                     xa[:, k * 128:(k + 1) * 128])
                    self.tm.inc(mm)
                    tg.append((u, bank, self.tm.n))
                    pool_t.took(u)
                self.t(tx)
            self._a1_xa_cons[tc] = tg[3][2]
            for grp in range(4):
                u, bank, tmv = tg[grp]

                def ev(eng, semo, is_v, grp=grp, bank=bank, tmv=tmv, tc=tc, u=u):
                    eng.wait_ge(self.tm.h, tmv)
                    for q in range(4):
                        k = grp * 4 + q
                        dst = xlnT[:, k * T + tc * 128:k * T + tc * 128 + 128]
                        if is_v:
                            cp = eng.tensor_copy(dst, bank[:, q * 128:(q + 1) * 128])
                        else:
                            cp = eng.copy(dst, bank[:, q * 128:(q + 1) * 128])
                    pool_t.done_mark_u(u, semo, semo.inc(cp))
                if grp % 2 == 0:
                    self.v(lambda vec, grp=grp, bank=bank, tmv=tmv, u=u:
                           ev(vec, self.vm, True, grp, bank, tmv, tc, u))
                else:
                    self.s(lambda sc, grp=grp, bank=bank, tmv=tmv, u=u:
                           ev(sc, self.sm, False, grp, bank, tmv, tc, u))
        xlnT_ready_v = self.vm.n
        xlnT_ready_s = self.sm.n

        for ts in range(4):
            def tbt(ten, ts=ts):
                u, bank = pool_m.begin(ten)
                if ts == 0:
                    ten.wait_ge(self.vm.h, xlnT_ready_v)
                    ten.wait_ge(self.sm.h, xlnT_ready_s)
                for k in range(NHC):
                    mm = ten.matmul(bank[:NH, 0:TBLK],
                                    wbT[:, k * NH:(k + 1) * NH],
                                    xlnT[:, k * T + ts * TBLK:k * T + (ts + 1) * TBLK],
                                    start=(k == 0), stop=(k == NHC - 1))
                self.tm.inc(mm)
                pool_m.took(u)
            self.t(tbt)
            tmv = self.tm.n
            u_used = pool_m.u - 1

            def sbt(sc, ts=ts, tmv=tmv, u=u_used):
                sc.wait_ge(self.tm.h, tmv)
                cp = sc.activation(self.beta_sb[:, ts * TBLK:(ts + 1) * TBLK],
                                   self.pb[u % 2][:NH, 0:TBLK], AF.Sigmoid)
                pool_m.done_mark_u(u, self.sm, self.sm.inc(cp))
            self.s(sbt)

        conv_sm = [0]
        for pi, (p, wsrc, dsts) in enumerate((("q", self.wq, self.qs),
                                              ("k", self.wk, self.ks),
                                              ("v", self.wv, self.vs))):
            for osl in range(NHC):
                widx = pi * NHC + osl
                slot = 0

                def yw(sync, wsrc=wsrc, osl=osl, widx=widx):
                    if widx >= 1:
                        sync.wait_ge(self.tm.h, self._a1_w_cons[widx - 1])
                    dd = sync.dma_start(out=wraw[:, :],
                                        in_=wsrc[osl * 128:(osl + 1) * 128, :])
                    st_w.mark(dd)
                self.y(yw)
                w_item = st_w.n - 1

                tg = []
                for grp in range(4):
                    def twt(ten, grp=grp, w_item=w_item):
                        u, bank = pool_t.begin(ten)
                        if grp == 0:
                            st_w.wait(ten, w_item)
                        for q in range(4):
                            k = grp * 4 + q
                            mm = self.tr(ten, bank[:, q * 128:(q + 1) * 128],
                                         wraw[:, k * 128:(k + 1) * 128])
                        self.tm.inc(mm)
                        tg.append((u, bank, self.tm.n))
                        pool_t.took(u)
                    self.t(twt)
                wT_marks = []
                for grp in range(4):
                    u, bank, tmv = tg[grp]

                    def evw(eng, semo, is_v, grp=grp, bank=bank, tmv=tmv, u=u):
                        eng.wait_ge(self.tm.h, tmv)
                        for q in range(4):
                            k = grp * 4 + q
                            dst = wT[:, k * 128:(k + 1) * 128]
                            if is_v:
                                cp = eng.tensor_copy(dst, bank[:, q * 128:(q + 1) * 128])
                            else:
                                cp = eng.copy(dst, bank[:, q * 128:(q + 1) * 128])
                        val = semo.inc(cp)
                        wT_marks.append((semo, val))
                        pool_t.done_mark_u(u, semo, val)
                    if grp % 2 == 0:
                        self.v(lambda vec, grp=grp, bank=bank, tmv=tmv, u=u:
                               evw(vec, self.vm, True, grp, bank, tmv, u))
                    else:
                        self.s(lambda sc, grp=grp, bank=bank, tmv=tmv, u=u:
                               evw(sc, self.sm, False, grp, bank, tmv, u))

                def tp3(ten, wT_marks=tuple(wT_marks)):
                    u, bank = pool_x.begin(ten)
                    for so, val in wT_marks:
                        ten.wait_ge(so.h, val)
                    for k in range(NHC):
                        mm = ten.matmul(bank[:, 0:3],
                                        wT[:, k * 128:(k + 1) * 128],
                                        xln3T[:, k * 3:(k + 1) * 3],
                                        start=(k == 0), stop=(k == NHC - 1))
                    self.tm.inc(mm)
                    pool_x.took(u)
                self.t(tp3)
                tmv = self.tm.n

                def vp3(vec, tmv=tmv):
                    vec.wait_ge(self.tm.h, tmv)
                    cp = vec.tensor_scalar(pr3i[:, :], self.pb[7][:, 0:3],
                                           self.flag_sb[:, 0:1], None, op0=OP.mult)
                    pool_x.done_mark(self.vm, self.vm.inc(cp))
                self.v(vp3)

                for ts in range(4):
                    def tmm(ten, ts=ts):
                        u, bank = pool_m.begin(ten)
                        for k in range(NHC):
                            mm = ten.matmul(bank[:, 0:TBLK],
                                            wT[:, k * 128:(k + 1) * 128],
                                            xlnT[:, k * T + ts * TBLK:k * T + (ts + 1) * TBLK],
                                            start=(k == 0), stop=(k == NHC - 1))
                        self.tm.inc(mm)
                        pool_m.took(u)
                    self.t(tmm)
                    tmv = self.tm.n
                    self._a1_w_cons[widx] = tmv
                    u_used = pool_m.u - 1

                    def vcv(vec, ts=ts, u=u_used, tmv=tmv, p=p, osl=osl):
                        vec.wait_ge(self.tm.h, tmv)
                        if conv_sm[0]:
                            vec.wait_ge(self.sm.h, conv_sm[0])
                        self.dve_sync(vec, vec.tensor_copy(ext[:, 3:515],
                                                           self.pb[u % 2][:, 0:TBLK]))
                        src3 = pr3i if ts == 0 else pr3
                        self.dve_sync(vec, vec.tensor_copy(ext[:, 0:3], src3[:, :]))
                        cw = self.cw[p]
                        self.dve_sync(vec, vec.tensor_scalar(
                            qcA[:, :], ext[:, 0:512], cw[:, osl * 4:osl * 4 + 1],
                            None, op0=OP.mult))
                        self.dve_sync(vec, vec.scalar_tensor_tensor(
                            qcB[:, :], ext[:, 1:513], cw[:, osl * 4 + 1:osl * 4 + 2],
                            qcA[:, :], op0=OP.mult, op1=OP.add))
                        self.dve_sync(vec, vec.scalar_tensor_tensor(
                            qcA[:, :], ext[:, 2:514], cw[:, osl * 4 + 2:osl * 4 + 3],
                            qcB[:, :], op0=OP.mult, op1=OP.add))
                        self.dve_sync(vec, vec.scalar_tensor_tensor(
                            qcB[:, :], ext[:, 3:515], cw[:, osl * 4 + 3:osl * 4 + 4],
                            qcA[:, :], op0=OP.mult, op1=OP.add))
                        cp = vec.tensor_copy(pr3[:, :], ext[:, 512:515])
                        val = self.vm.inc(cp)
                        pool_m.done_mark_u(u, self.vm, val)
                    self.v(vcv)
                    vmv = self.vm.n
                    qitem = st_qcv.n

                    def scv(sc, vmv=vmv, qslot=qitem % 2, qitem=qitem):
                        sc.wait_ge(self.vm.h, vmv)
                        if qitem >= 2:
                            st_qcv.wait(sc, qitem - 2)
                        self.sm.inc(sc.activation(qcv[:, qslot * TBLK:(qslot + 1) * TBLK],
                                                  qcB[:, :], AF.Silu))
                        conv_sm[0] = self.sm.n
                    self.s(scv)
                    smv = self.sm.n

                    def ycv(sync, dsts=dsts, ts=ts, osl=osl, smv=smv, qslot=qitem % 2):
                        sync.wait_ge(self.sm.h, smv)
                        ssem = st_qcv.begin_item()
                        for q in range(4):
                            c = ts * 4 + q
                            dd = sync.dma_start(
                                out=dsts[c, osl],
                                in_=qcv[:, qslot * TBLK + q * 128:qslot * TBLK + (q + 1) * 128])
                            ssem.inc(dd, 16)
                        st_qcv.commit()
                    self.y(ycv)
        es.close()

    def attn_a(self):
        nc = self.nc
        es = ExitStack()
        qb = {p: es.enter_context(nc.sbuf_tensor(f"aa_qb_{p}", [128, 2 * H], F32))
              for p in "qkv"}
        D = lambda name, w=128: es.enter_context(nc.sbuf_tensor(name, [128, w], F32))
        t_qst = [D("aa_qst0"), D("aa_qst1")]
        t_kst = [D("aa_kst0"), D("aa_kst1")]
        t_g = [D("aa_g0"), D("aa_g1")]
        t_oint = [D("aa_oint0"), D("aa_oint1")]
        t_wfm = [D("aa_wfm0"), D("aa_wfm1")]
        Z = [D("aa_Z0", 256), D("aa_Z1", 256)]
        t_kfn = D("aa_kfn")
        t_B = D("aa_B")
        t_tq = D("aa_tq")
        TB = [D(f"aa_TB{i}") for i in range(7)]
        Bp = [D("aa_Bp0"), D("aa_Bp1")]
        t_u = D("aa_u")
        e_q = [D("aa_eq0", 1), D("aa_eq1", 1)]
        e_k = [D("aa_ek0", 1), D("aa_ek1", 1)]
        ssq = D("aa_ssq", 1)
        ssk = D("aa_ssk", 1)
        tr1 = D("aa_tr1")
        tr2 = D("aa_tr2")
        rq_all = D("aa_rq", NH)
        betaT = D("aa_betaT", NH)

        st_qkv = DmaStream(self, "aa_sqkv")
        st_loc = DmaStream(self, "aa_sloc")
        st_rq = DmaStream(self, "aa_srq")
        pool = PsumPool(self, [0, 1, 2, 3, 4, 5])
        pool_b = PsumPool(self, [6])

        chunk_cons = {}
        hu = [0]
        sa_mark = {}
        tm_end = {}

        for c in range(NCH):
            def yq(sync, c=c):
                if c >= 2:
                    sync.wait_ge(self.tm.h, chunk_cons[c - 2])
                slot = c % 2
                ssem = st_qkv.begin_item()
                for p, src in (("q", self.qs), ("k", self.ks), ("v", self.vs)):
                    for oc in range(NHC):
                        dd = sync.dma_start(
                            out=qb[p][:, slot * H + oc * 128:slot * H + (oc + 1) * 128],
                            in_=src[c, oc])
                        ssem.inc(dd, 16)
                st_qkv.commit()
            self.y(yq)
            q_item = st_qkv.n - 1
            cslot = c % 2

            def tbt(ten, c=c):
                u, bank = pool_b.begin(ten)
                self.tm.inc(self.tr(ten, bank[:, 0:NH],
                                    self.beta_sb[:, c * 128:(c + 1) * 128]))
                pool_b.took(u)
            self.t(tbt)
            tmv = self.tm.n

            def vbt(vec, tmv=tmv):
                vec.wait_ge(self.tm.h, tmv)
                cp = vec.tensor_copy(betaT[:, :], self.pb[6][:, 0:NH])
                pool_b.done_mark(self.vm, self.vm.inc(cp))
            self.v(vbt)

            for h in range(NH):
                hs = hu[0] % 2
                qf = qb["q"][:, cslot * H + h * 128:cslot * H + (h + 1) * 128]
                kf = qb["k"][:, cslot * H + h * 128:cslot * H + (h + 1) * 128]
                vf = qb["v"][:, cslot * H + h * 128:cslot * H + (h + 1) * 128]
                prev_tm_end = tm_end.get(hu[0] - 1)
                prev_loc = st_loc.vals.get(hu[0] - 2)

                def t1(ten, qf=qf, kf=kf, vf=vf, h=h, q_item=q_item):
                    u, bank = pool.begin(ten)
                    if h == 0:
                        st_qkv.wait(ten, q_item)
                    self.tr(ten, bank[:, 0:128], qf)
                    self.tr(ten, bank[:, 128:256], kf)
                    self.tm.inc(self.tr(ten, bank[:, 256:384], vf))
                    pool.took(u)
                self.t(t1)
                tmv = self.tm.n
                uu = pool.u - 1

                def v2(vec, uu=uu, tmv=tmv, hs=hs, prev_tm_end=prev_tm_end,
                       prev_loc=prev_loc, h=h):
                    vec.wait_ge(self.tm.h, tmv)
                    if prev_tm_end is not None:
                        vec.wait_ge(self.tm.h, prev_tm_end)
                    if prev_loc is not None:
                        vec.wait_ge(prev_loc[0].h, prev_loc[1])
                    bank = self.pb[pool.banks[uu % len(pool.banks)]]
                    self.dve_sync(vec, vec.tensor_copy(t_qst[hs][:, :], bank[:, 0:128]))
                    self.dve_sync(vec, vec.tensor_copy(t_kst[hs][:, :], bank[:, 128:256]))
                    cp = vec.tensor_scalar(Z[hs][:, 128:256], bank[:, 256:384],
                                           betaT[:, h:h + 1], None, op0=OP.mult)
                    pool.done_mark_u(uu, self.vm, self.vm.inc(cp))
                self.v(v2)
                vmv = self.vm.n

                def s3(sc, hs=hs, vmv=vmv):
                    sc.wait_ge(self.vm.h, vmv)
                    sc.activation(tr1[:, :], t_qst[hs][:, :], AF.Square,
                                  accum_out=ssq[:, 0:1])
                    self.sm.inc(sc.activation(tr2[:, :], t_kst[hs][:, :], AF.Square,
                                              accum_out=ssk[:, 0:1]))
                self.s(s3)
                smv = self.sm.n

                def v4(vec, hs=hs, smv=smv):
                    vec.wait_ge(self.sm.h, smv)
                    self.dve_sync(vec, vec.tensor_scalar(e_q[hs][:, :], ssq[:, 0:1],
                                                         L2_EPS, None, op0=OP.add))
                    self.vm.inc(vec.tensor_scalar(e_k[hs][:, :], ssk[:, 0:1],
                                                  L2_EPS, None, op0=OP.add))
                self.v(v4)
                vmv = self.vm.n

                def s4(sc, hs=hs, vmv=vmv):
                    sc.wait_ge(self.vm.h, vmv)
                    sc.activation(e_q[hs][:, :], e_q[hs][:, :], AF.Sqrt)
                    self.sm.inc(sc.activation(e_k[hs][:, :], e_k[hs][:, :], AF.Sqrt))
                self.s(s4)
                smv = self.sm.n

                def v5(vec, hs=hs, smv=smv, h=h, c=c):
                    vec.wait_ge(self.sm.h, smv)
                    if h == 0 and c >= 1:
                        st_rq.wait(vec, c - 1)
                    self.dve_sync(vec, vec.reciprocal(e_q[hs][:, :], e_q[hs][:, :]))
                    self.dve_sync(vec, vec.reciprocal(e_k[hs][:, :], e_k[hs][:, :]))
                    self.dve_sync(vec, vec.tensor_scalar(rq_all[:, h:h + 1], e_q[hs][:, 0:1],
                                                         HD_SCALE, None, op0=OP.mult))
                    self.dve_sync(vec, vec.tensor_scalar(t_kst[hs][:, :], t_kst[hs][:, :],
                                                         e_k[hs][:, 0:1], None, op0=OP.mult))
                    cp = vec.tensor_scalar(Z[hs][:, 0:128], t_kst[hs][:, :],
                                           betaT[:, h:h + 1], None, op0=OP.mult)
                    self.vm.inc(cp)
                self.v(v5)
                v_kst = self.vm.n

                def t6(ten, hs=hs, v_kst=v_kst):
                    u, bank = pool.begin(ten)
                    ten.wait_ge(self.vm.h, v_kst)
                    self.tm.inc(self.tr(ten, bank[:, 0:128], t_kst[hs][:, :]))
                    pool.took(u)
                self.t(t6)
                tmv = self.tm.n
                uu = pool.u - 1

                def v6(vec, uu=uu, tmv=tmv):
                    vec.wait_ge(self.tm.h, tmv)
                    bank = self.pb[pool.banks[uu % len(pool.banks)]]
                    cp = vec.tensor_copy(t_kfn[:, :], bank[:, 0:128])
                    pool.done_mark_u(uu, self.vm, self.vm.inc(cp))
                self.v(v6)
                v_kfn = self.vm.n

                def t7(ten, qf=qf, v_kfn=v_kfn):
                    u, bank = pool.begin(ten)
                    ten.wait_ge(self.vm.h, v_kfn)
                    ten.matmul(bank[:, 0:128], t_kfn[:, :], t_kfn[:, :],
                               start=True, stop=True)
                    self.tm.inc(ten.matmul(bank[:, 128:256], qf, t_kfn[:, :],
                                           start=True, stop=True))
                    pool.took(u)
                self.t(t7)
                tmv = self.tm.n
                uu = pool.u - 1

                def v8(vec, uu=uu, tmv=tmv, h=h):
                    vec.wait_ge(self.tm.h, tmv)
                    bank = self.pb[pool.banks[uu % len(pool.banks)]]
                    self.dve_sync(vec, vec.scalar_tensor_tensor(
                        t_B[:, :], bank[:, 0:128], betaT[:, h:h + 1], self.stril[:, :],
                        op0=OP.mult, op1=OP.mult))
                    cp = vec.tensor_tensor(t_tq[:, :], bank[:, 128:256], self.tril0[:, :],
                                           op=OP.mult)
                    pool.done_mark_u(uu, self.vm, self.vm.inc(cp))
                self.v(v8)
                vmv = self.vm.n

                def t9(ten, vmv=vmv):
                    u, bank = pool.begin(ten)
                    ten.wait_ge(self.vm.h, vmv)
                    self.tr(ten, bank[:, 0:128], t_B[:, :])
                    self.tm.inc(self.tr(ten, bank[:, 128:256], t_tq[:, :]))
                    pool.took(u)
                self.t(t9)
                tmv = self.tm.n
                uu = pool.u - 1
                TT = t_tq

                def v10(vec, uu=uu, tmv=tmv):
                    vec.wait_ge(self.tm.h, tmv)
                    bank = self.pb[pool.banks[uu % len(pool.banks)]]
                    self.dve_sync(vec, vec.tensor_copy(TB[0][:, :], bank[:, 0:128]))
                    cp = vec.tensor_copy(TT[:, :], bank[:, 128:256])
                    pool.done_mark_u(uu, self.vm, self.vm.inc(cp))
                self.v(v10)
                vmv = self.vm.n
                Bp_cur = t_B

                for i in range(1, 7):
                    def tsq(ten, i=i, Bp_cur=Bp_cur, vmv=vmv):
                        u, bank = pool.begin(ten)
                        ten.wait_ge(self.vm.h, vmv)
                        mm = ten.matmul(bank[:, 0:128], Bp_cur[:, :], TB[i - 1][:, :],
                                        start=True, stop=True)
                        if i < 6:
                            mm = ten.matmul(bank[:, 128:256], TB[i - 1][:, :], Bp_cur[:, :],
                                            start=True, stop=True)
                        self.tm.inc(mm)
                        pool.took(u)
                    self.t(tsq)
                    tmv = self.tm.n
                    uu = pool.u - 1
                    Bp_nxt = Bp[i % 2]

                    def vsq(vec, i=i, uu=uu, tmv=tmv, Bp_nxt=Bp_nxt):
                        vec.wait_ge(self.tm.h, tmv)
                        bank = self.pb[pool.banks[uu % len(pool.banks)]]
                        self.dve_sync(vec, vec.tensor_copy(TB[i][:, :], bank[:, 0:128]))
                        if i < 6:
                            cp = vec.tensor_copy(Bp_nxt[:, :], bank[:, 128:256])
                        else:
                            cp = vec.engine_nop()
                        pool.done_mark_u(uu, self.vm, self.vm.inc(cp))
                    self.v(vsq)
                    vmv = self.vm.n
                    Bp_cur = Bp_nxt

                for i in [6, 5, 4, 3, 2, 1, 0]:
                    def tap(ten, i=i, hs=hs, vmv=vmv):
                        u, bank = pool.begin(ten)
                        ten.wait_ge(self.vm.h, vmv)
                        self.tm.inc(ten.matmul(bank[:, 0:256], TB[i][:, :], Z[hs][:, :],
                                               start=True, stop=True))
                        pool.took(u)
                    self.t(tap)
                    tmv = self.tm.n
                    uu = pool.u - 1

                    def vap(vec, i=i, hs=hs, uu=uu, tmv=tmv):
                        vec.wait_ge(self.tm.h, tmv)
                        bank = self.pb[pool.banks[uu % len(pool.banks)]]
                        op = OP.subtract if i == 0 else OP.add
                        cp = vec.tensor_tensor(Z[hs][:, :], Z[hs][:, :], bank[:, 0:256],
                                               op=op)
                        pool.done_mark_u(uu, self.vm, self.vm.inc(cp))
                    self.v(vap)
                    vmv = self.vm.n

                def t13(ten, hs=hs, vmv=vmv):
                    u, bank = pool.begin(ten)
                    ten.wait_ge(self.vm.h, vmv)
                    ten.matmul(bank[:, 0:128], Z[hs][:, 0:128], TT[:, :],
                               start=True, stop=True)
                    ten.matmul(bank[:, 128:256], TT[:, :], Z[hs][:, 128:256],
                               start=True, stop=True)
                    self.tm.inc(self.tr(ten, bank[:, 256:384], Z[hs][:, 0:128]))
                    pool.took(u)
                self.t(t13)
                tmv = self.tm.n
                uu = pool.u - 1

                def v14(vec, hs=hs, qf=qf, uu=uu, tmv=tmv):
                    vec.wait_ge(self.tm.h, tmv)
                    bank = self.pb[pool.banks[uu % len(pool.banks)]]
                    self.dve_sync(vec, vec.tensor_tensor(t_g[hs][:, :], qf, bank[:, 0:128],
                                                         op=OP.subtract))
                    self.dve_sync(vec, vec.tensor_copy(t_oint[hs][:, :], bank[:, 128:256]))
                    cp = vec.tensor_copy(t_wfm[hs][:, :], bank[:, 256:384])
                    pool.done_mark_u(uu, self.vm, self.vm.inc(cp))
                self.v(v14)
                v_loc = self.vm.n

                def sst(sc, c=c, h=h, hs=hs, v_loc=v_loc):
                    sc.wait_ge(self.vm.h, v_loc)
                    ssem = st_loc.begin_item()
                    for j, srcb in ((0, t_wfm[hs][:, :]), (1, Z[hs][:, 128:256]),
                                    (2, t_kst[hs][:, :]), (3, t_g[hs][:, :]),
                                    (4, t_oint[hs][:, :])):
                        ssem.inc(sc.dma_start(out=self.loc[c, h, j], in_=srcb), 16)
                    st_loc.commit()
                self.s(sst)

                prev_sa = sa_mark.get(h)

                def t16a(ten, hs=hs, h=h, v_loc=v_loc, prev_sa=prev_sa):
                    u, bank = pool.begin(ten)
                    if prev_sa is not None:
                        ten.wait_ge(self.vm.h, prev_sa)
                    ten.wait_ge(self.vm.h, v_loc)
                    self.tm.inc(ten.matmul(bank[:, 0:128], t_wfm[hs][:, :],
                                           self.SA[:, h * 128:(h + 1) * 128],
                                           start=True, stop=True))
                    pool.took(u)
                self.t(t16a)
                tmv = self.tm.n
                uu = pool.u - 1

                def v16a(vec, hs=hs, uu=uu, tmv=tmv):
                    vec.wait_ge(self.tm.h, tmv)
                    bank = self.pb[pool.banks[uu % len(pool.banks)]]
                    cp = vec.tensor_tensor(t_u[:, :], Z[hs][:, 128:256], bank[:, 0:128],
                                           op=OP.subtract)
                    pool.done_mark_u(uu, self.vm, self.vm.inc(cp))
                self.v(v16a)
                vmv = self.vm.n

                def t16b(ten, hs=hs, vmv=vmv):
                    u, bank = pool.begin(ten)
                    ten.wait_ge(self.vm.h, vmv)
                    self.tm.inc(ten.matmul(bank[:, 0:128], t_kst[hs][:, :], t_u[:, :],
                                           start=True, stop=True))
                    pool.took(u)
                self.t(t16b)
                tmv = self.tm.n
                tm_end[hu[0]] = self.tm.n
                uu = pool.u - 1

                def v16b(vec, h=h, uu=uu, tmv=tmv):
                    vec.wait_ge(self.tm.h, tmv)
                    bank = self.pb[pool.banks[uu % len(pool.banks)]]
                    cp = vec.tensor_tensor(self.SA[:, h * 128:(h + 1) * 128],
                                           self.SA[:, h * 128:(h + 1) * 128],
                                           bank[:, 0:128], op=OP.add)
                    val = self.vm.inc(cp)
                    sa_mark[h] = val
                    pool.done_mark_u(uu, self.vm, val)
                self.v(v16b)
                hu[0] += 1
            chunk_cons[c] = tm_end[hu[0] - 1]

            v_rq = self.vm.n

            def srq(sc, c=c, v_rq=v_rq):
                sc.wait_ge(self.vm.h, v_rq)
                dd = sc.dma_start(out=self.locr[c], in_=rq_all[:, :])
                st_rq.mark(dd)
            self.s(srq)
        es.close()

    def attn_cc(self):
        sa_final = self.vm.n

        def gcc(gp):
            gp.wait_ge(self.vm.h, sa_final)
            gp.dma_start(out=self.cc_s_in[:, :], in_=self.SA[:, :]).then_inc(self.qg.h, 16)
            self.qg.n += 16
            gp.wait_ge(self.qg.h, self.qg.n)
            gp.collective_compute(
                "AllGather", OP.bypass,
                replica_groups=[[0, 1], [2, 3], [4, 5], [6, 7]],
                ins=[self.cc_s_in.opt()],
                outs=[self.cc_s_out.opt()],
            ).then_inc(self.gm.h, 1)
            self.gm.n += 1
            gp.wait_ge(self.gm.h, self.gm.n)
            gp.dma_start(out=self.SB[:, :], in_=self.cc_s_out[0:128, :]
                         ).then_inc(self.qg.h, 16)
            self.qg.n += 16
            gp.wait_ge(self.qg.h, self.qg.n)
            self.gm.inc(gp.nop())
        self.g(gcc)
        gm_v = self.gm.n

        def vflag(vec):
            vec.wait_ge(self.gm.h, gm_v)
            self.vm.inc(vec.tensor_scalar(self.SB[:, :], self.SB[:, :],
                                          self.flag_sb[:, 0:1], None, op0=OP.mult))
        self.v(vflag)

    def attn_b(self, o_fm):
        nc = self.nc
        es = ExitStack()
        lb = [es.enter_context(nc.sbuf_tensor(f"ab_lb{i}", [128, 5 * 128], F32))
              for i in range(2)]
        rqb = [es.enter_context(nc.sbuf_tensor(f"ab_rq{i}", [128, NH], F32))
               for i in range(2)]
        t_ou = [es.enter_context(nc.sbuf_tensor(f"ab_ou{i}", [128, 128], F32))
                for i in range(2)]
        t_u = es.enter_context(nc.sbuf_tensor("ab_u", [128, 128], F32))
        sso = es.enter_context(nc.sbuf_tensor("ab_sso", [128, 1], F32))
        rq2 = es.enter_context(nc.sbuf_tensor("ab_rq2", [128, 1], F32))
        tr3 = es.enter_context(nc.sbuf_tensor("ab_tr3", [128, 128], F32))

        st_lb = DmaStream(self, "ab_slb")
        st_rq = DmaStream(self, "ab_srq2")
        pool = PsumPool(self, [0, 1, 2, 3])
        sb_mark = {}
        ou_cons = {}
        hu = [0]

        for c in range(NCH):
            def yrq(sync, c=c):
                if c >= 2:
                    sync.wait_ge(self.vm.h, self._ab_rq_cons.get(c - 2, 0))
                dd = sync.dma_start(out=rqb[c % 2][:, :], in_=self.locr[c])
                st_rq.mark(dd)
            self.y(yrq)
            rq_item = st_rq.n - 1
            if not hasattr(self, "_ab_rq_cons"):
                self._ab_rq_cons = {}

            for h in range(NH):
                hs = hu[0] % 2
                def ylb(sync, c=c, h=h, hs=hs, it=hu[0]):
                    if it >= 2:
                        prev = ou_cons.get(it - 2)
                        if prev is not None:
                            sync.wait_ge(prev[0].h, prev[1])
                    ssem = st_lb.begin_item()
                    for j in range(5):
                        dd = sync.dma_start(out=lb[hs][:, j * 128:(j + 1) * 128],
                                            in_=self.loc[c, h, j])
                        ssem.inc(dd, 16)
                    st_lb.commit()
                self.y(ylb)
                lb_item = st_lb.n - 1
                W_ = lb[hs][:, 0:128]
                Uv = lb[hs][:, 128:256]
                Kst = lb[hs][:, 256:384]
                G_ = lb[hs][:, 384:512]
                Oint = lb[hs][:, 512:640]
                prev_sb = sb_mark.get(h)

                def t1(ten, hs=hs, h=h, lb_item=lb_item, prev_sb=prev_sb):
                    u, bank = pool.begin(ten)
                    st_lb.wait(ten, lb_item)
                    if prev_sb is not None:
                        ten.wait_ge(self.vm.h, prev_sb)
                    ten.matmul(bank[:, 0:128], lb[hs][:, 384:512],
                               self.SB[:, h * 128:(h + 1) * 128], start=True, stop=True)
                    self.tm.inc(ten.matmul(bank[:, 128:256], lb[hs][:, 0:128],
                                           self.SB[:, h * 128:(h + 1) * 128],
                                           start=True, stop=True))
                    pool.took(u)
                self.t(t1)
                tmv = self.tm.n
                uu = pool.u - 1

                def v2(vec, hs=hs, uu=uu, tmv=tmv):
                    vec.wait_ge(self.tm.h, tmv)
                    bank = self.pb[pool.banks[uu % len(pool.banks)]]
                    self.dve_sync(vec, vec.tensor_tensor(t_ou[hs][:, :], bank[:, 0:128],
                                                         lb[hs][:, 512:640], op=OP.add))
                    cp = vec.tensor_tensor(t_u[:, :], lb[hs][:, 128:256], bank[:, 128:256],
                                           op=OP.subtract)
                    pool.done_mark_u(uu, self.vm, self.vm.inc(cp))
                self.v(v2)
                vmv = self.vm.n

                def t3(ten, hs=hs, vmv=vmv):
                    u, bank = pool.begin(ten)
                    ten.wait_ge(self.vm.h, vmv)
                    self.tm.inc(ten.matmul(bank[:, 0:128], lb[hs][:, 256:384], t_u[:, :],
                                           start=True, stop=True))
                    pool.took(u)
                self.t(t3)
                tmv = self.tm.n
                uu = pool.u - 1

                def v3(vec, h=h, uu=uu, tmv=tmv):
                    vec.wait_ge(self.tm.h, tmv)
                    bank = self.pb[pool.banks[uu % len(pool.banks)]]
                    cp = vec.tensor_tensor(self.SB[:, h * 128:(h + 1) * 128],
                                           self.SB[:, h * 128:(h + 1) * 128],
                                           bank[:, 0:128], op=OP.add)
                    val = self.vm.inc(cp)
                    sb_mark[h] = val
                    pool.done_mark_u(uu, self.vm, val)
                self.v(v3)
                vmv = self.vm.n

                def s5(sc, hs=hs, vmv=vmv):
                    sc.wait_ge(self.vm.h, vmv)
                    self.sm.inc(sc.activation(tr3[:, :], t_ou[hs][:, :], AF.Square,
                                              accum_out=sso[:, 0:1]))
                self.s(s5)
                smv = self.sm.n

                def v6(vec, hs=hs, smv=smv, rq_item=rq_item, h=h, c=c):
                    vec.wait_ge(self.sm.h, smv)
                    st_rq.wait(vec, rq_item)
                    self.dve_sync(vec, vec.tensor_tensor(rq2[:, :], rqb[c % 2][:, h:h + 1],
                                                         rqb[c % 2][:, h:h + 1], op=OP.mult))
                    self.dve_sync(vec, vec.tensor_scalar(sso[:, :], sso[:, :],
                                                         1.0 / HD, None, op0=OP.mult))
                    self.dve_sync(vec, vec.tensor_tensor(sso[:, :], sso[:, :], rq2[:, :],
                                                         op=OP.mult))
                    self.vm.inc(vec.tensor_scalar(sso[:, :], sso[:, :], RMS_EPS,
                                                  None, op0=OP.add))
                self.v(v6)
                vmv = self.vm.n

                def s6(sc, vmv=vmv):
                    sc.wait_ge(self.vm.h, vmv)
                    self.sm.inc(sc.activation(sso[:, :], sso[:, :], AF.Sqrt))
                self.s(s6)
                smv = self.sm.n

                def v7(vec, hs=hs, smv=smv, h=h, c=c):
                    vec.wait_ge(self.sm.h, smv)
                    self.dve_sync(vec, vec.reciprocal(sso[:, :], sso[:, :]))
                    self.dve_sync(vec, vec.tensor_tensor(sso[:, :], sso[:, :],
                                                         rqb[c % 2][:, h:h + 1], op=OP.mult))
                    self.vm.inc(vec.tensor_scalar(t_ou[hs][:, :], t_ou[hs][:, :],
                                                  sso[:, 0:1], None, op0=OP.mult))
                    self._ab_rq_cons[c] = self.vm.n
                self.v(v7)
                vmv = self.vm.n

                def t8(ten, hs=hs, vmv=vmv):
                    u, bank = pool.begin(ten)
                    ten.wait_ge(self.vm.h, vmv)
                    self.tm.inc(self.tr(ten, bank[:, 0:128], t_ou[hs][:, :]))
                    pool.took(u)
                self.t(t8)
                tmv = self.tm.n
                uu = pool.u - 1

                def s8(sc, h=h, c=c, uu=uu, tmv=tmv):
                    sc.wait_ge(self.tm.h, tmv)
                    bank = self.pb[pool.banks[uu % len(pool.banks)]]
                    cp = sc.copy(o_fm[:, h * T + c * 128:h * T + (c + 1) * 128],
                                 bank[:, 0:128])
                    val = self.sm.inc(cp)
                    ou_cons[hu[0]] = (self.sm, val)
                    pool.done_mark_u(uu, self.sm, val)
                self.s(s8)
                hu[0] += 1
        es.close()

    def attn_out(self, o_fm):
        nc = self.nc
        es = ExitStack()
        woT = es.enter_context(nc.sbuf_tensor("a2_woT", [128, 2 * H], F32))
        x2tm = es.enter_context(nc.sbuf_tensor("a2_x2tm", [128, 2 * H], F32))
        yb = es.enter_context(nc.sbuf_tensor("a2_yb", [128, TS_A2], F32))
        st_wo = DmaStream(self, "a2_swo")
        st_x = DmaStream(self, "a2_sx")
        st_st = DmaStream(self, "a2_sst")
        pool = PsumPool(self, [0, 1])
        pool_t = PsumPool(self, [2, 3])
        o_ready_v = self.vm.n
        o_ready_s = self.sm.n
        NTS = T // TS_A2
        wo_cons = {}
        x2_done = {}

        for ts in range(NTS):
            for tc in range(2):
                def yx(sync, ts=ts, tc=tc):
                    it = ts * 2 + tc
                    if it >= 2:
                        prev = x2_done.get(it - 2)
                        if prev is not None:
                            sync.wait_ge(prev[0].h, prev[1])
                    r0 = ts * TS_A2 + tc * 128
                    dd = sync.dma_start(out=x2tm[:, tc * H:(tc + 1) * H],
                                        in_=self.x1s[r0:r0 + 128, :])
                    st_x.mark(dd)
                self.y(yx)
            x_item = st_x.n - 1

            for osl in range(NHC):
                widx = ts * NHC + osl
                slot = widx % 2

                def ywo(sync, osl=osl, widx=widx, slot=slot):
                    if widx >= 2:
                        sync.wait_ge(self.tm.h, wo_cons[widx - 2])
                    dd = sync.dma_start(out=woT[:, slot * H:(slot + 1) * H],
                                        in_=self.wot[osl])
                    st_wo.mark(dd)
                self.y(ywo)
                w_item = st_wo.n - 1

                def tmm(ten, ts=ts, slot=slot, w_item=w_item, osl=osl, widx=widx):
                    u, bank = pool.begin(ten)
                    st_wo.wait(ten, w_item)
                    if widx == 0:
                        ten.wait_ge(self.vm.h, o_ready_v)
                        ten.wait_ge(self.sm.h, o_ready_s)
                    for k in range(NHC):
                        mm = ten.matmul(bank[:, 0:TS_A2],
                                        woT[:, slot * H + k * 128:slot * H + (k + 1) * 128],
                                        o_fm[:, k * T + ts * TS_A2:k * T + (ts + 1) * TS_A2],
                                        start=(k == 0), stop=(k == NHC - 1))
                    self.tm.inc(mm)
                    wo_cons[widx] = self.tm.n
                    pool.took(u)
                self.t(tmm)
                tmv = self.tm.n
                uu = pool.u - 1

                def vev(vec, uu=uu, tmv=tmv):
                    vec.wait_ge(self.tm.h, tmv)
                    bank = self.pb[pool.banks[uu % len(pool.banks)]]
                    cp = vec.tensor_copy(yb[:, :], bank[:, 0:TS_A2])
                    pool.done_mark_u(uu, self.vm, self.vm.inc(cp))
                self.v(vev)
                vmv = self.vm.n

                def ttr(ten, vmv=vmv):
                    u, bank = pool_t.begin(ten)
                    ten.wait_ge(self.vm.h, vmv)
                    self.tr(ten, bank[:, 0:128], yb[:, 0:128])
                    self.tm.inc(self.tr(ten, bank[:, 128:256], yb[:, 128:256]))
                    pool_t.took(u)
                self.t(ttr)
                tmv = self.tm.n
                uu = pool_t.u - 1

                def vadd(vec, osl=osl, uu=uu, tmv=tmv, x_item=x_item):
                    vec.wait_ge(self.tm.h, tmv)
                    st_x.wait(vec, x_item - 1)
                    st_x.wait(vec, x_item)
                    bank = self.pb[pool_t.banks[uu % len(pool_t.banks)]]
                    for tc in range(2):
                        dst = x2tm[:, tc * H + osl * 128:tc * H + (osl + 1) * 128]
                        cp = vec.tensor_tensor(dst, dst, bank[:, tc * 128:(tc + 1) * 128],
                                               op=OP.add)
                    pool_t.done_mark_u(uu, self.vm, self.vm.inc(cp))
                self.v(vadd)
            v_x2 = self.vm.n

            def yst(sync, ts=ts, v_x2=v_x2):
                sync.wait_ge(self.vm.h, v_x2)
                for tc in range(2):
                    r0 = ts * TS_A2 + tc * 128
                    dd = sync.dma_start(out=self.x2s[r0:r0 + 128, :],
                                        in_=x2tm[:, tc * H:(tc + 1) * H])
                    it = st_st.mark(dd)
                    x2_done[ts * 2 + tc] = st_st.vals[it]
            self.y(yst)
        es.close()

    def finish(self):
        final = self.qw.n

        def yend(sync):
            sync.wait_ge(self.qw.h, final)
        self.y(yend)
        self.es.close()
        return self.nc


def _build_nc(phases=("f1", "attn", "f2")):
    mk = MK()
    mk.p0()
    mk.phase_barrier()
    if "f1" in phases:
        mk.ffn(mk.x0, mk.x1s, "f1")
        mk.phase_barrier()
    if "f1out" in phases:
        mk.ffn(mk.x0, mk.out, "f1")
        mk.phase_barrier()
    if "attn" in phases:
        mk.attn_proj()
        mk.phase_barrier()
        mk.attn_a()
        mk.phase_barrier()
        mk.attn_cc()
        mk.phase_barrier()
        o_es = ExitStack()
        o_fm = o_es.enter_context(mk.nc.sbuf_tensor("o_fm", [128, NHC * T], F32))
        mk.attn_b(o_fm)
        mk.phase_barrier()
        mk.attn_out(o_fm)
        mk.phase_barrier()
        o_es.close()
    if "attnout" in phases:
        pass
    if "f2" in phases:
        mk.ffn(mk.x2s, mk.out, "f2")
        mk.phase_barrier()
    return mk.finish()


SEL8 = np.zeros((NG, NG * 128), np.float32)
for _g in range(NG):
    SEL8[_g, _g * 128:(_g + 1) * 128] = 1.0


def _make_in_maps(inputs):
    x = np.ascontiguousarray(inputs["L0_attn_x"])
    common = {
        "wq": np.ascontiguousarray(inputs["L0_attn_Wq"]),
        "wk": np.ascontiguousarray(inputs["L0_attn_Wk"]),
        "wv": np.ascontiguousarray(inputs["L0_attn_Wv"]),
        "wb": np.ascontiguousarray(inputs["L0_attn_Wb"]),
        "wo": np.ascontiguousarray(inputs["L0_attn_Wo"]),
        "cq": np.ascontiguousarray(inputs["L0_attn_conv_q"]),
        "ck": np.ascontiguousarray(inputs["L0_attn_conv_k"]),
        "cv": np.ascontiguousarray(inputs["L0_attn_conv_v"]),
        "rmsw": np.ascontiguousarray(
            np.asarray(inputs["L0_attn_rms_w"]).reshape(HD, 1)),
        "gw": np.ascontiguousarray(inputs["L0_ffn_gate_weight"]),
        "gup": np.ascontiguousarray(
            np.asarray(inputs["L0_ffn_gate_up_proj"]).reshape(NE * 2 * FI, H)),
        "dwn": np.ascontiguousarray(
            np.asarray(inputs["L0_ffn_down_proj"]).reshape(NE * H, FI)),
        "mg": np.ascontiguousarray(
            np.asarray(inputs["L0_ffn_merge_groups"]).reshape(NE, 1).astype(np.int32)),
        "dom": np.ascontiguousarray(
            np.asarray(inputs["L0_ffn_dominant_experts"]).reshape(NG, 1).astype(np.int32)),
    }
    in_maps = []
    for c in range(N_CORES):
        b, hf = divmod(c, 2)
        m = dict(common)
        m["x"] = np.ascontiguousarray(x[b, hf * T:(hf + 1) * T, :])
        m["flag"] = np.full((128, 1), float(hf), np.float32)
        m["sel8"] = SEL8
        in_maps.append(m)
    return in_maps


def kernel(**inputs):
    in_maps = _make_in_maps(inputs)
    if "nc" not in _NC_CACHE:
        _NC_CACHE["nc"] = _build_nc()
    nc = _NC_CACHE["nc"]
    res = run_bass_kernel_spmd(nc, in_maps, core_ids=list(range(N_CORES)))
    out = np.empty((4, 4096, H), np.float32)
    for c in range(N_CORES):
        b, hf = divmod(c, 2)
        out[b, hf * T:(hf + 1) * T, :] = res.results[c]["out"]
    return out


# version 03
# speedup vs baseline: 1.2072x; 1.2072x over previous
import numpy as np
from contextlib import ExitStack

import concourse.bass as bass
import concourse.mybir as mybir
from concourse.bass_utils import run_bass_kernel_spmd

F32 = mybir.dt.float32
I32 = mybir.dt.int32
AF = mybir.ActivationFunctionType
OP = mybir.AluOpType
AX = mybir.AxisListType

H = 2048
T = 2048
NHC = 16
NH, HD = 16, 128
C = 128
NCH = 16
TBLK = 512
NBLK = 4
NE, NG = 32, 8
FI = 768
NJ = 12
NI = 6
LN_EPS = 1e-5
L2_EPS = 1e-6
RMS_EPS = 1e-5
HD_SCALE = float(HD) ** -0.5
TS_A2 = 256
N_CORES = 8

_NC_CACHE = {}


class Sem:
    __slots__ = ("h", "n")

    def __init__(self, h):
        self.h = h
        self.n = 0

    def inc(self, instr, k=1):
        instr.then_inc(self.h, k)
        self.n += k
        return self.n


class DmaStream:

    def __init__(self, mk, name):
        self.sems = [Sem(mk.es.enter_context(mk.nc.semaphore(f"{name}_a"))),
                     Sem(mk.es.enter_context(mk.nc.semaphore(f"{name}_b")))]
        mk.all_sems.append(self.sems[0])
        mk.all_sems.append(self.sems[1])
        self.n = 0
        self.vals = {}

    def begin_item(self):
        return self.sems[self.n % 2]

    def commit(self):
        slot = self.n % 2
        self.vals[self.n] = (self.sems[slot], self.sems[slot].n)
        self.n += 1
        return self.n - 1

    def mark(self, instr):
        self.begin_item().inc(instr, 16)
        return self.commit()

    def wait(self, eng, i):
        so, v = self.vals[i]
        eng.wait_ge(so.h, v)


class PsumPool:

    def __init__(self, mk, bank_ids):
        self.mk = mk
        self.banks = bank_ids
        self.u = 0
        self.done = {}

    def begin(self, ten):
        u = self.u
        prev = self.done.get(u - len(self.banks))
        if prev is not None:
            ten.wait_ge(prev[0].h, prev[1])
        return u, self.mk.pb[self.banks[u % len(self.banks)]]

    def took(self, u):
        self.u = u + 1

    def done_mark(self, sem, val):
        self.done[self.u - 1] = (sem, val)

    def done_mark_u(self, u, sem, val):
        self.done[u] = (sem, val)

    def bank(self, u):
        return self.mk.pb[self.banks[u % len(self.banks)]]


class MK:
    def __init__(self):
        nc = bass.Bass("TRN2", target_bir_lowering=False, debug=False)
        self.nc = nc
        self.es = ExitStack()
        es = self.es

        def din(name, shape, dt=F32):
            return nc.dram_tensor(name, shape, dt, kind="ExternalInput").ap()

        self.x0 = din("x", [T, H])
        self.wq = din("wq", [H, H])
        self.wk = din("wk", [H, H])
        self.wv = din("wv", [H, H])
        self.wb = din("wb", [NH, H])
        self.wo = din("wo", [H, H])
        self.cqw = din("cq", [H, 4])
        self.ckw = din("ck", [H, 4])
        self.cvw = din("cv", [H, 4])
        self.rmsw = din("rmsw", [HD, 1])
        self.gw = din("gw", [NE, H])
        self.gup = din("gup", [NE * 2 * FI, H])
        self.dwn = din("dwn", [NE * H, FI])
        self.mg = din("mg", [NE, 1], I32)
        self.dom = din("dom", [NG, 1], I32)
        self.flag = din("flag", [128, 1])
        self.sel8 = din("sel8", [NG, NG * 128])
        self.out = nc.dram_tensor("out", [T, H], F32, kind="ExternalOutput").ap()

        dr = lambda name, shape: nc.dram_tensor(name, shape, F32).ap()
        self.x1s = dr("x1s", [T, H])
        self.x2s = dr("x2s", [T, H])
        self.gut = dr("gut", [NG, NJ, 128, H])
        self.dnt = dr("dnt", [NG, NHC, 128, FI])
        self.wot = dr("wot", [NHC, 128, H])
        self.qs = dr("qs_s", [NCH, NHC, 128, C])
        self.ks = dr("ks_s", [NCH, NHC, 128, C])
        self.vs = dr("vs_s", [NCH, NHC, 128, C])
        self.loc = dr("loc", [NCH, NH, 5, 128, 128])
        self.locr = dr("locr", [NCH, 128, NH])
        self.cc_s_in = dr("cc_s_in", [128, NH * HD])
        self.cc_s_out = dr("cc_s_out", [256, NH * HD])
        self.cc_x_in = dr("cc_x_in", [3, H])
        self.cc_x_out = dr("cc_x_out", [6, H])

        self.all_sems = []

        def sem(name):
            s = Sem(es.enter_context(nc.semaphore(name)))
            self.all_sems.append(s)
            return s
        self.qw = sem("qw")
        self.qg = sem("qg")
        self.qc = sem("qc")
        self.tm = sem("tm")
        self.vm = sem("vm")
        self.sm = sem("sm")
        self.gm = sem("gm")

        sb = lambda name, shape, dt=F32: es.enter_context(nc.sbuf_tensor(name, shape, dt))
        self.ident = sb("ident", [128, 128])
        self.stril = sb("stril", [128, 128])
        self.tril0 = sb("tril0", [128, 128])
        self.ones_row = sb("ones_row", [1, 128])
        self.cw = {p: sb(f"cw_{p}", [128, 4 * NHC]) for p in "qkv"}
        self.rmsw_sb = sb("rmsw_sb", [128, 1])
        self.flag_sb = sb("flag_sb", [128, 1])
        self.gwT = sb("gwT", [128, NE * NHC])
        self.mgrp = sb("mgrp", [NE, NG])
        self.mg_sb = sb("mg_sb", [NE, 1], I32)
        self.mgf = sb("mgf", [NE, 1])
        self.dom_sb = sb("dom_sb", [NG, 1], I32)
        self.iota8 = sb("iota8", [NE, NG])
        self._sel8_sb = sb("sel8_sb", [NG, NG * 128])
        self.beta_sb = sb("beta_sb", [NH, T])
        self.SA = sb("SA", [128, NH * HD])
        self.SB = sb("SB", [128, NH * HD])
        self.tril0w = sb("tril0w", [128, 512])
        self.trash = sb("trash", [128, H])
        self.ln1 = sb("ln_s1", [128, 1])
        self.ln2 = sb("ln_s2", [128, 1])

        self.pb = [es.enter_context(nc.psum_tensor(f"pb{i}", [128, 512], F32))
                   for i in range(8)]
        self.block = es.enter_context(nc.Block())

    def t(self, f): self.block.tensor(f)

    def v(self, f):
        def w(vec):
            if self.vm.n:
                vec.wait_ge(self.vm.h, self.vm.n)
            f(vec)
        self.block.vector(w)

    def s(self, f):
        def w(sc):
            if self.sm.n:
                sc.wait_ge(self.sm.h, self.sm.n)
            f(sc)
        self.block.scalar(w)

    def g(self, f):
        def w(gp):
            if self.gm.n:
                gp.wait_ge(self.gm.h, self.gm.n)
            f(gp)
        self.block.gpsimd(w)

    def y(self, f): self.block.sync(f)

    def emit_ln(self, xa, slot, stream, item):
        def vln(vec):
            stream.wait(vec, item)
            xt = xa[:, slot * H:(slot + 1) * H]
            self.dve_sync(vec, vec.reduce_sum(self.ln1[:, :], xt, axis=AX.X))
            self.dve_sync(vec, vec.tensor_scalar(self.ln1[:, :], self.ln1[:, :],
                                                 1.0 / H, None, op0=OP.mult))
            self.vm.inc(vec.tensor_scalar(xt, xt, self.ln1[:, 0:1], None,
                                          op0=OP.subtract))
        self.v(vln)
        vmv = self.vm.n

        def sln(sc, vmv=vmv):
            sc.wait_ge(self.vm.h, vmv)
            self.sm.inc(sc.activation(self.trash[:, :], xa[:, slot * H:(slot + 1) * H],
                                      AF.Square, accum_out=self.ln2[:, 0:1]))
        self.s(sln)
        smv = self.sm.n

        def vv(vec, smv=smv):
            vec.wait_ge(self.sm.h, smv)
            self.dve_sync(vec, vec.tensor_scalar(self.ln2[:, :], self.ln2[:, :],
                                                 1.0 / H, None, op0=OP.mult))
            self.vm.inc(vec.tensor_scalar(self.ln2[:, :], self.ln2[:, :],
                                          LN_EPS, None, op0=OP.add))
        self.v(vv)
        vmv2 = self.vm.n

        def ss(sc, vmv2=vmv2):
            sc.wait_ge(self.vm.h, vmv2)
            self.sm.inc(sc.activation(self.ln2[:, :], self.ln2[:, :], AF.Sqrt))
        self.s(ss)
        smv2 = self.sm.n

        def vv2(vec, smv2=smv2):
            sc_ = None
            vec.wait_ge(self.sm.h, smv2)
            self.dve_sync(vec, vec.reciprocal(self.ln2[:, :], self.ln2[:, :]))
            xt = xa[:, slot * H:(slot + 1) * H]
            self.vm.inc(vec.tensor_scalar(xt, xt, self.ln2[:, 0:1], None, op0=OP.mult))
        self.v(vv2)

    def dve_sync(self, vec, instr):
        val = self.vm.inc(instr)
        vec.wait_ge(self.vm.h, val)

    def tr(self, eng, out_ps, in_ap):
        p = in_ap.shape[0]
        bp = in_ap.base_partition()
        return eng.transpose(out_ps, in_ap, self.ident[bp:bp + p, bp:bp + p])

    def phase_barrier(self):
        marks = [(s, s.n) for s in self.all_sems]

        def w(eng):
            for s, v in marks:
                if v > 0:
                    eng.wait_ge(s.h, v)
        self.t(w)
        self.v(w)
        self.s(w)
        self.g(w)
        self.y(w)

    def p0(self):
        nc = self.nc

        def g0(gp):
            gp.memset(self.ident[:, :], 0.0).then_inc(self.gm.h, 1)
            self.gm.n += 1
            gp.memset(self.stril[:, :], 1.0).then_inc(self.gm.h, 1)
            self.gm.n += 1
            gp.memset(self.tril0[:, :], 1.0).then_inc(self.gm.h, 1)
            self.gm.n += 1
            gp.wait_ge(self.gm.h, self.gm.n)
            gp.affine_select(out=self.ident[:, :], in_=self.ident[:, :],
                             compare_op=OP.not_equal, fill=1.0, base=0,
                             pattern=[[-1, 128]], channel_multiplier=1)
            gp.affine_select(out=self.stril[:, :], in_=self.stril[:, :],
                             compare_op=OP.is_gt, fill=0.0, base=0,
                             pattern=[[-1, 128]], channel_multiplier=1)
            gp.affine_select(out=self.tril0[:, :], in_=self.tril0[:, :],
                             compare_op=OP.is_ge, fill=0.0, base=0,
                             pattern=[[-1, 128]], channel_multiplier=1
                             ).then_inc(self.gm.h, 1)
            self.gm.n += 1
            gp.wait_ge(self.gm.h, self.gm.n)
            for _j in range(4):
                gp.tensor_copy(self.tril0w[:, _j * 128:(_j + 1) * 128], self.tril0[:, :])
            gp.memset(self.ones_row[:, :], 1.0)
            gp.iota(self.iota8[:, :], pattern=[[1, NG]], base=0,
                    channel_multiplier=0, allow_small_or_imprecise_dtypes=True)
            gp.memset(self.SA[:, :], 0.0)
            gp.dma_start(out=self.rmsw_sb[:HD, :], in_=self.rmsw).then_inc(self.qg.h, 16)
            gp.dma_start(out=self.flag_sb[:, :], in_=self.flag).then_inc(self.qg.h, 16)
            gp.dma_start(out=self.mg_sb[:, :], in_=self.mg).then_inc(self.qg.h, 16)
            gp.dma_start(out=self.dom_sb[:, :], in_=self.dom).then_inc(self.qg.h, 16)
            gp.dma_start(out=self._sel8_sb[:, :], in_=self.sel8).then_inc(self.qg.h, 16)
            self.qg.n += 80
            gp.wait_ge(self.qg.h, self.qg.n)
            self.gm.inc(gp.nop())
        self.g(g0)
        gm0 = self.gm.n

        def v0(vec):
            vec.wait_ge(self.gm.h, gm0)
            cp = vec.tensor_copy(self.mgf[:, :], self.mg_sb[:, :])
            v = self.vm.inc(cp)
            vec.wait_ge(self.vm.h, v)
            self.vm.inc(vec.tensor_tensor(self.mgrp[:, :], self.iota8[:, :],
                                          self.mgf[:, 0:1].to_broadcast([NE, NG]),
                                          op=OP.is_equal))
        self.v(v0)

        p0es = ExitStack()
        gw_raw = p0es.enter_context(nc.sbuf_tensor("gw_raw", [NE, H], F32))

        def y0(sync):
            for p, src in (("q", self.cqw), ("k", self.ckw), ("v", self.cvw)):
                for oc in range(NHC):
                    sync.dma_start(out=self.cw[p][:, oc * 4:(oc + 1) * 4],
                                   in_=src[oc * 128:(oc + 1) * 128, :]).then_inc(self.qw.h, 16)
                    self.qw.n += 16
            sync.dma_start(out=gw_raw[:, :], in_=self.gw).then_inc(self.qw.h, 16)
            self.qw.n += 16
            sync.wait_ge(self.qw.h, self.qw.n)
        self.y(y0)

        qw0, gm1 = self.qw.n, self.gm.n
        prev_vm = [None]
        for k in range(NHC):
            def t0(ten, k=k):
                if k == 0:
                    ten.wait_ge(self.qw.h, qw0)
                    ten.wait_ge(self.gm.h, gm1)
                else:
                    ten.wait_ge(self.vm.h, prev_vm[0])
                self.tm.inc(self.tr(ten, self.pb[6][:, 0:NE], gw_raw[:, k * 128:(k + 1) * 128]))
            self.t(t0)
            tmv = self.tm.n

            def v1(vec, k=k, tmv=tmv):
                vec.wait_ge(self.tm.h, tmv)
                cp = vec.tensor_copy(self.gwT[:, k * NE:(k + 1) * NE], self.pb[6][:, 0:NE])
                prev_vm[0] = self.vm.inc(cp)
            self.v(v1)

        raw_gu = p0es.enter_context(nc.sbuf_tensor("p0_rawgu", [128, 2 * H], F32))
        stg_gu = p0es.enter_context(nc.sbuf_tensor("p0_stggu", [128, 2 * H], F32))
        raw_dn = p0es.enter_context(nc.sbuf_tensor("p0_rawdn", [128, 2 * FI], F32))
        stg_dn = p0es.enter_context(nc.sbuf_tensor("p0_stgdn", [128, 2 * FI], F32))

        slabs = ([("gu", gi, j) for gi in range(NG) for j in range(NJ)]
                 + [("dn", gi, f) for gi in range(NG) for f in range(NHC)]
                 + [("wo", osl, 0) for osl in range(NHC)])
        geom = {
            "gu": dict(raw=raw_gu, stg=stg_gu, w=H, nt=16, buf="A"),
            "dn": dict(raw=raw_dn, stg=stg_dn, w=FI, nt=6, buf="B"),
            "wo": dict(raw=raw_gu, stg=stg_gu, w=H, nt=16, buf="A"),
        }
        buf_use = {"A": 0, "B": 0}
        slab_bufuse = {}
        bufuse_to_slab = {}
        for n, (kind, a, b) in enumerate(slabs):
            bk = geom[kind]["buf"]
            u = buf_use[bk]
            buf_use[bk] = u + 1
            slab_bufuse[n] = u
            bufuse_to_slab[(bk, u)] = n
        tm_base = self.tm.n
        qg_base = self.qg.n
        load_v = {n: qg_base + 16 * (n + 1) for n in range(len(slabs))}
        done_marks = {}

        gug = self.nc.dram_tensor("gug", [NG * 2 * FI, H], F32).ap()
        dng = self.nc.dram_tensor("dng", [NG * H, FI], F32).ap()

        def ggather(gp):
            with gp.register("eo_p0") as eo:
                for gi in range(NG):
                    gp.reg_load(eo, self.dom_sb[gi:gi + 1, 0:1])
                    gp.reg_mul(eo, eo, 2 * FI * H)
                    gp.dma_start(out=gug[gi * 2 * FI:(gi + 1) * 2 * FI, :],
                                 in_=bass.AP(self.gup.tensor, eo,
                                             [[H, 2 * FI], [1, H]])
                                 ).then_inc(self.qg.h, 16)
                    self.qg.n += 16
                for gi in range(NG):
                    gp.reg_load(eo, self.dom_sb[gi:gi + 1, 0:1])
                    gp.reg_mul(eo, eo, H * FI)
                    gp.dma_start(out=dng[gi * H:(gi + 1) * H, :],
                                 in_=bass.AP(self.dwn.tensor, eo,
                                             [[FI, H], [1, FI]])
                                 ).then_inc(self.qg.h, 16)
                    self.qg.n += 16
        gath_base = self.qg.n
        self.g(ggather)
        assert self.qg.n == gath_base + 16 * 16
        gath_total = self.qg.n
        stA = DmaStream(self, "p0A")
        stB = DmaStream(self, "p0B")
        streams = {"A": stA, "B": stB}
        stores = {"A": DmaStream(self, "p0sA"), "B": DmaStream(self, "p0sB")}
        store_item = {}
        load_item = {}

        def gall(gp):
            gp.wait_ge(self.qg.h, gath_total)
            for n, (kind, a, b) in enumerate(slabs):
                gg = geom[kind]
                st = streams[gg["buf"]]
                u = slab_bufuse[n]
                slot = u % 2
                prev_slab = bufuse_to_slab.get((gg["buf"], u - 2))
                if prev_slab is not None:
                    gp.wait_ge(self.tm.h, tm_base + prev_slab + 1)
                dst = gg["raw"][:, slot * gg["w"]:(slot + 1) * gg["w"]]
                if kind == "gu":
                    r0 = a * 2 * FI + b * 128
                    dd = gp.dma_start(out=dst, in_=gug[r0:r0 + 128, :])
                elif kind == "dn":
                    r0 = a * H + b * 128
                    dd = gp.dma_start(out=dst, in_=dng[r0:r0 + 128, :])
                else:
                    dd = gp.dma_start(out=dst, in_=self.wo[a * 128:(a + 1) * 128, :])
                load_item[n] = (st, st.mark(dd))
        self.g(gall)

        for n in range(len(slabs)):
            kind, a, b = slabs[n]
            gg = geom[kind]
            u = slab_bufuse[n]
            slot = u % 2
            nt = gg["nt"]
            ngrp = (nt + 3) // 4

            def tt(ten, n=n, gg=gg, slot=slot, nt=nt, ngrp=ngrp, u=u):
                st_, it_ = load_item[n]
                st_.wait(ten, it_)
                prev = done_marks.get(n - 1)
                if prev is not None:
                    for so, val in prev:
                        ten.wait_ge(so.h, val)
                src = gg["raw"][:, slot * gg["w"]:(slot + 1) * gg["w"]]
                for grp in range(ngrp):
                    bank = self.pb[grp]
                    for q in range(min(4, nt - grp * 4)):
                        k = grp * 4 + q
                        mm = self.tr(ten, bank[:, q * 128:(q + 1) * 128],
                                     src[:, k * 128:(k + 1) * 128])
                self.tm.inc(mm)
                assert self.tm.n == tm_base + n + 1
            self.t(tt)
            tmv = self.tm.n
            marks = []

            prev_store = store_item.get((gg["buf"], u - 2))

            def emit_evac(eng, semo, is_v, kind=kind, gg=gg, slot=slot, nt=nt,
                          ngrp=ngrp, tmv=tmv, prev_store=prev_store):
                eng.wait_ge(self.tm.h, tmv)
                if prev_store is not None:
                    eng.wait_ge(prev_store[0].h, prev_store[1])
                cp = None
                for grp in range(ngrp):
                    take_v = (grp % 2 == 0) or (kind == "wo")
                    if take_v != is_v:
                        continue
                    bank = self.pb[grp]
                    w = min(4, nt - grp * 4) * 128
                    dst = gg["stg"][:, slot * gg["w"] + grp * 512:
                                    slot * gg["w"] + grp * 512 + w]
                    if kind == "wo":
                        cp = eng.tensor_scalar(dst, bank[:, :w], self.rmsw_sb[:, 0:1],
                                               None, op0=OP.mult)
                    else:
                        if is_v:
                            cp = eng.tensor_copy(dst, bank[:, :w])
                        else:
                            cp = eng.copy(dst, bank[:, :w])
                if cp is not None:
                    marks.append((semo, semo.inc(cp)))
            self.v(lambda vec: emit_evac(vec, self.vm, True))
            if kind != "wo" and ngrp > 1:
                self.s(lambda sc: emit_evac(sc, self.sm, False))
            done_marks[n] = list(marks)

            def yw(sync, kind=kind, a=a, b=b, gg=gg, slot=slot, marks=tuple(marks), u=u):
                for so, val in marks:
                    sync.wait_ge(so.h, val)
                src = gg["stg"][:, slot * gg["w"]:(slot + 1) * gg["w"]]
                dstd = (self.gut[a, b] if kind == "gu"
                        else self.dnt[a, b] if kind == "dn" else self.wot[a])
                dd = sync.dma_start(out=dstd, in_=src)
                it = stores[gg["buf"]].mark(dd)
                store_item[(gg["buf"], u)] = stores[gg["buf"]].vals[it]
            self.y(yw)
        p0es.close()

    def ffn(self, x_src, x_dst, tag):
        nc = self.nc
        es = ExitStack()
        xlnT = es.enter_context(nc.sbuf_tensor(f"xlnT_{tag}", [128, NHC * TBLK], F32))
        oacc = es.enter_context(nc.sbuf_tensor(f"oacc_{tag}", [128, NHC * TBLK], F32))
        act = es.enter_context(nc.sbuf_tensor(f"act_{tag}", [128, NI * TBLK], F32))
        s_t = es.enter_context(nc.sbuf_tensor(f"s_t_{tag}", [128, TBLK], F32))
        wbc = es.enter_context(nc.sbuf_tensor(f"wbc_{tag}", [128, TBLK], F32))
        xa = es.enter_context(nc.sbuf_tensor(f"xa_{tag}", [128, 2 * H], F32))
        xo = es.enter_context(nc.sbuf_tensor(f"xo_{tag}", [128, 2 * H], F32))
        gub = es.enter_context(nc.sbuf_tensor(f"gub_{tag}", [128, 2 * H], F32))
        dnb = es.enter_context(nc.sbuf_tensor(f"dnb_{tag}", [128, 2 * FI], F32))
        wrows = es.enter_context(nc.sbuf_tensor(f"wrows_{tag}", [NG, TBLK], F32))
        gsc = es.enter_context(nc.sbuf_tensor(f"gsc_{tag}", [128, 5 * NE], F32))
        gs1 = es.enter_context(nc.sbuf_tensor(f"gs1_{tag}", [128, 4], F32))
        m8 = es.enter_context(nc.sbuf_tensor(f"m8_{tag}", [128, 8], F32))
        lgf = es.enter_context(nc.sbuf_tensor(f"lgf_{tag}", [NE, 128], F32))
        wtg = es.enter_context(nc.sbuf_tensor(f"wtg_{tag}", [128, NG], F32))

        seq = []
        for tb in range(NBLK):
            for gi in range(NG):
                for jp in range(NI):
                    seq.append(("gu", gi, jp))
                    seq.append(("gu", gi, jp + NI))
                for f in range(NHC):
                    seq.append(("dn", gi, f))

        kctr = {"gu": 0, "dn": 0}
        kuse = {"gu": 0, "dn": 0}
        cons_v = {}
        st_slab = {"gu": DmaStream(self, f"f{tag}_gu"), "dn": DmaStream(self, f"f{tag}_dn")}

        def emit_slab_load(idx):
            kind, gi, j = seq[idx]
            ki = kctr[kind]
            kctr[kind] = ki + 1
            slot = ki % 2
            cred = cons_v.get((kind, ki - 2))
            buf, w = (gub, H) if kind == "gu" else (dnb, FI)
            src = self.gut[gi, j] if kind == "gu" else self.dnt[gi, j]

            def yl(sync, buf=buf, w=w, src=src, slot=slot, cred=cred, kind=kind, ki=ki):
                if cred is not None:
                    sync.wait_ge(self.tm.h, cred)
                dd = sync.dma_start(out=buf[:, slot * w:(slot + 1) * w], in_=src)
                it = st_slab[kind].mark(dd)
                assert it == ki
            self.y(yl)

        pbu = {"n": 0}
        evac_done = {}

        def bank_begin(ten):
            u = pbu["n"]
            prev = evac_done.get(u - 2)
            if prev is not None:
                ten.wait_ge(prev[0].h, prev[1])
            return u, self.pb[u % 2]

        def bank_end(u):
            pbu["n"] = u + 1

        tpbu = {"n": 0}
        tevac_done = {}

        def tbank_begin(ten):
            u = tpbu["n"]
            prev = tevac_done.get(u - 4)
            if prev is not None:
                ten.wait_ge(prev[0].h, prev[1])
            return u, self.pb[2 + (u % 4)]

        def tbank_end(u):
            tpbu["n"] = u + 1

        gpbu = {"n": 0}
        gevac_done = {}

        def gbank_begin(ten):
            u = gpbu["n"]
            prev = gevac_done.get(u - 1)
            if prev is not None:
                ten.wait_ge(prev[0].h, prev[1])
            return u, self.pb[6]

        def gbank_end(u):
            gpbu["n"] = u + 1

        def gmark(val):
            gevac_done[gpbu["n"] - 1] = (self.vm, val)

        xa_credit = {}
        xo_credit = {}
        st_xa = DmaStream(self, f"f{tag}_xa")
        st_xo = DmaStream(self, f"f{tag}_xo")
        st_xs = DmaStream(self, f"f{tag}_xs")
        last_ep_tm = [None]
        silu_done = [0]
        up_done = [0]
        seq_pos = [0]
        emit_slab_load(0)
        emit_slab_load(1)

        for tb in range(NBLK):
            for tc in range(4):
                slot = tc % 2
                cred = xa_credit.get(tb * 4 + tc - 2)

                def yx(sync, tb=tb, tc=tc, slot=slot, cred=cred):
                    if cred is not None:
                        sync.wait_ge(cred[0].h, cred[1])
                    r0 = tb * TBLK + tc * 128
                    dd = sync.dma_start(out=xa[:, slot * H:(slot + 1) * H],
                                        in_=x_src[r0:r0 + 128, :])
                    st_xa.mark(dd)
                self.y(yx)
                xa_item = st_xa.n - 1

                def vln(vec, slot=slot, xa_item=xa_item):
                    st_xa.wait(vec, xa_item)
                    xt = xa[:, slot * H:(slot + 1) * H]
                    self.dve_sync(vec, vec.reduce_sum(self.ln1[:, :], xt, axis=AX.X))
                    self.dve_sync(vec, vec.tensor_scalar(self.ln1[:, :], self.ln1[:, :],
                                                         1.0 / H, None, op0=OP.mult))
                    self.vm.inc(vec.tensor_scalar(xt, xt, self.ln1[:, 0:1], None,
                                                  op0=OP.subtract))
                self.v(vln)
                vmv = self.vm.n

                def sln(sc, slot=slot, vmv=vmv):
                    sc.wait_ge(self.vm.h, vmv)
                    self.sm.inc(sc.activation(self.trash[:, :], xa[:, slot * H:(slot + 1) * H],
                                              AF.Square, accum_out=self.ln2[:, 0:1]))
                self.s(sln)
                smv = self.sm.n

                def vv(vec, smv=smv):
                    vec.wait_ge(self.sm.h, smv)
                    self.dve_sync(vec, vec.tensor_scalar(self.ln2[:, :], self.ln2[:, :],
                                                         1.0 / H, None, op0=OP.mult))
                    self.vm.inc(vec.tensor_scalar(self.ln2[:, :], self.ln2[:, :],
                                                  LN_EPS, None, op0=OP.add))
                self.v(vv)
                vmv = self.vm.n

                def ss(sc, vmv=vmv):
                    sc.wait_ge(self.vm.h, vmv)
                    self.sm.inc(sc.activation(self.ln2[:, :], self.ln2[:, :], AF.Sqrt))
                self.s(ss)
                smv = self.sm.n

                def vv2(vec, slot=slot, smv=smv):
                    vec.wait_ge(self.sm.h, smv)
                    self.dve_sync(vec, vec.reciprocal(self.ln2[:, :], self.ln2[:, :]))
                    xt = xa[:, slot * H:(slot + 1) * H]
                    self.vm.inc(vec.tensor_scalar(xt, xt, self.ln2[:, 0:1], None,
                                                  op0=OP.mult))
                self.v(vv2)
                v_ln = self.vm.n

                tgrp = []
                for grp in range(4):
                    def tx(ten, grp=grp, slot=slot, v_ln=v_ln):
                        u, bank = tbank_begin(ten)
                        if grp == 0:
                            ten.wait_ge(self.vm.h, v_ln)
                        for q in range(4):
                            k = grp * 4 + q
                            mm = self.tr(ten, bank[:, q * 128:(q + 1) * 128],
                                         xa[:, slot * H + k * 128:slot * H + (k + 1) * 128])
                        self.tm.inc(mm)
                        tgrp.append((u, bank, self.tm.n))
                        tbank_end(u)
                    self.t(tx)
                xa_credit[tb * 4 + tc] = (self.tm, tgrp[3][2])

                ev_marks = []
                for grp in range(4):
                    u, bank, tmv = tgrp[grp]

                    def ev(eng, semo, is_v, grp=grp, bank=bank, tmv=tmv, tc=tc, u=u):
                        eng.wait_ge(self.tm.h, tmv)
                        cp = None
                        for q in range(4):
                            k = grp * 4 + q
                            dst = xlnT[:, k * TBLK + tc * 128:k * TBLK + tc * 128 + 128]
                            if is_v:
                                cp = eng.tensor_copy(dst, bank[:, q * 128:(q + 1) * 128])
                            else:
                                cp = eng.copy(dst, bank[:, q * 128:(q + 1) * 128])
                        val = semo.inc(cp)
                        ev_marks.append((semo, val))
                        tevac_done[u] = (semo, val)
                    if grp % 2 == 0:
                        self.v(lambda vec, grp=grp, bank=bank, tmv=tmv, u=u:
                               ev(vec, self.vm, True, grp, bank, tmv, tc, u))
                    else:
                        self.s(lambda sc, grp=grp, bank=bank, tmv=tmv, u=u:
                               ev(sc, self.sm, False, grp, bank, tmv, tc, u))

                def tlg(ten, tc=tc, ev_marks=tuple(ev_marks)):
                    u, bank = gbank_begin(ten)
                    for so, val in ev_marks:
                        ten.wait_ge(so.h, val)
                    for k in range(NHC):
                        mm = ten.matmul(bank[:NE, 0:128],
                                        self.gwT[:, k * NE:(k + 1) * NE],
                                        xlnT[:, k * TBLK + tc * 128:k * TBLK + tc * 128 + 128],
                                        start=(k == 0), stop=(k == NHC - 1))
                    self.tm.inc(mm)
                    gbank_end(u)
                self.t(tlg)
                tmv = self.tm.n

                def vlg(vec, tmv=tmv):
                    vec.wait_ge(self.tm.h, tmv)
                    gmark(self.vm.inc(vec.tensor_copy(lgf[:, :], self.pb[6][:NE, 0:128])))
                self.v(vlg)
                vmv = self.vm.n

                def tlg2(ten, vmv=vmv):
                    u, bank = gbank_begin(ten)
                    ten.wait_ge(self.vm.h, vmv)
                    self.tm.inc(self.tr(ten, bank[:, 128:128 + NE], lgf[:, :]))
                    gbank_end(u)
                self.t(tlg2)
                tmv = self.tm.n

                def vsm(vec, tmv=tmv):
                    vec.wait_ge(self.tm.h, tmv)
                    cp = vec.tensor_copy(gsc[:, 0:NE], self.pb[6][:, 128:128 + NE])
                    gmark(self.vm.inc(cp))
                    vec.wait_ge(self.vm.h, self.vm.n)
                    self.dve_sync(vec, vec.reduce_max(gs1[:, 0:1], gsc[:, 0:NE], axis=AX.X))
                    self.vm.inc(vec.tensor_scalar(gsc[:, 0:NE], gsc[:, 0:NE], gs1[:, 0:1],
                                                  None, op0=OP.subtract))
                self.v(vsm)
                vmv = self.vm.n

                def sexp(sc, vmv=vmv):
                    sc.wait_ge(self.vm.h, vmv)
                    self.sm.inc(sc.activation(gsc[:, NE:2 * NE], gsc[:, 0:NE], AF.Exp))
                self.s(sexp)
                smv = self.sm.n

                def vtop(vec, smv=smv):
                    vec.wait_ge(self.sm.h, smv)
                    pex = gsc[:, NE:2 * NE]
                    probs = gsc[:, 2 * NE:3 * NE]
                    self.dve_sync(vec, vec.reduce_sum(gs1[:, 1:2], pex, axis=AX.X))
                    self.dve_sync(vec, vec.reciprocal(gs1[:, 1:2], gs1[:, 1:2]))
                    self.dve_sync(vec, vec.tensor_scalar(probs, pex, gs1[:, 1:2], None,
                                                         op0=OP.mult))
                    mx8 = vec.max(out=m8[:, :], in_=probs)
                    v = self.vm.inc(mx8)
                    for _ in range(8):
                        vec.engine_nop()
                    vec.wait_ge(self.vm.h, v)
                    self.dve_sync(vec, vec.reduce_sum(gs1[:, 2:3], m8[:, :], axis=AX.X))
                    self.dve_sync(vec, vec.reciprocal(gs1[:, 2:3], gs1[:, 2:3]))
                    mask = gsc[:, 3 * NE:4 * NE]
                    self.dve_sync(vec, vec.tensor_tensor(
                        mask, probs, m8[:, 7:8].to_broadcast([128, NE]), op=OP.is_ge))
                    pm = gsc[:, 4 * NE:5 * NE]
                    self.dve_sync(vec, vec.tensor_tensor(pm, probs, mask, op=OP.mult))
                    self.vm.inc(vec.tensor_scalar(pm, pm, gs1[:, 2:3], None, op0=OP.mult))
                self.v(vtop)
                vmv = self.vm.n

                def tpm(ten, vmv=vmv):
                    u, bank = gbank_begin(ten)
                    ten.wait_ge(self.vm.h, vmv)
                    self.tm.inc(self.tr(ten, bank[:NE, 0:128], gsc[:, 4 * NE:5 * NE]))
                    gbank_end(u)
                self.t(tpm)
                tmv = self.tm.n

                def vpm(vec, tmv=tmv):
                    vec.wait_ge(self.tm.h, tmv)
                    gmark(self.vm.inc(vec.tensor_copy(lgf[:, :], self.pb[6][:NE, 0:128])))
                self.v(vpm)
                vmv = self.vm.n

                def twtg(ten, vmv=vmv):
                    u, bank = gbank_begin(ten)
                    ten.wait_ge(self.vm.h, vmv)
                    mm = ten.matmul(bank[:, 128:128 + NG], lgf[:, :], self.mgrp[:, :],
                                    start=True, stop=True)
                    self.tm.inc(mm)
                    gbank_end(u)
                self.t(twtg)
                tmv = self.tm.n

                def vwtg(vec, tmv=tmv):
                    vec.wait_ge(self.tm.h, tmv)
                    gmark(self.vm.inc(vec.tensor_copy(wtg[:, :], self.pb[6][:, 128:128 + NG])))
                self.v(vwtg)
                vmv = self.vm.n

                def twr(ten, vmv=vmv):
                    u, bank = gbank_begin(ten)
                    ten.wait_ge(self.vm.h, vmv)
                    self.tm.inc(self.tr(ten, bank[:NG, 0:128], wtg[:, :]))
                    gbank_end(u)
                self.t(twr)
                tmv = self.tm.n

                def vwr(vec, tmv=tmv, tc=tc):
                    vec.wait_ge(self.tm.h, tmv)
                    cp = vec.tensor_copy(wrows[:, tc * 128:(tc + 1) * 128],
                                         self.pb[6][:NG, 0:128])
                    gmark(self.vm.inc(cp))
                self.v(vwr)

            wrows_ready = self.vm.n

            for gi in range(NG):
                def twb(ten, gi=gi, wrows_ready=wrows_ready):
                    u, bank = gbank_begin(ten)
                    if gi == 0:
                        ten.wait_ge(self.vm.h, wrows_ready)
                    mm = ten.matmul(bank[:, 0:TBLK],
                                    self.sel8[gi] if False else self._sel8_sb[:, gi * 128:(gi + 1) * 128],
                                    wrows[0:NG, :], start=True, stop=True)
                    self.tm.inc(mm)
                    gbank_end(u)
                self.t(twb)
                tmv = self.tm.n

                def vwb(vec, tmv=tmv, gi=gi):
                    vec.wait_ge(self.tm.h, tmv)
                    if gi > 0:
                        vec.wait_ge(self.vm.h, up_done[0])
                    gmark(self.vm.inc(vec.tensor_copy(wbc[:, :], self.pb[6][:, 0:TBLK])))
                self.v(vwb)
                wbc_ready = self.vm.n

                for jp in range(NI):
                    for half, j in ((0, jp), (1, jp + NI)):
                        ki = kuse["gu"]
                        kuse["gu"] = ki + 1
                        slot = ki % 2

                        def tgu(ten, slot=slot, ki=ki):
                            u, bank = bank_begin(ten)
                            st_slab["gu"].wait(ten, ki)
                            for k in range(NHC):
                                mm = ten.matmul(bank[:, 0:TBLK],
                                                gub[:, slot * H + k * 128:slot * H + (k + 1) * 128],
                                                xlnT[:, k * TBLK:(k + 1) * TBLK],
                                                start=(k == 0), stop=(k == NHC - 1))
                            self.tm.inc(mm)
                            cons_v[("gu", ki)] = self.tm.n
                            bank_end(u)
                        self.t(tgu)
                        u_used = pbu["n"] - 1
                        tmv = self.tm.n
                        seq_pos[0] += 1
                        if seq_pos[0] + 1 < len(seq):
                            emit_slab_load(seq_pos[0] + 1)

                        if half == 0:
                            def sgate(sc, u=u_used, tmv=tmv):
                                sc.wait_ge(self.tm.h, tmv)
                                if up_done[0]:
                                    sc.wait_ge(self.vm.h, up_done[0])
                                cp = sc.activation(s_t[:, :], self.pb[u % 2][:, 0:TBLK],
                                                   AF.Silu)
                                evac_done[u] = (self.sm, self.sm.inc(cp))
                            self.s(sgate)
                            silu_done[0] = self.sm.n
                        else:
                            def vup(vec, u=u_used, tmv=tmv, jp=jp,
                                    wbc_ready=wbc_ready, sd=silu_done[0]):
                                vec.wait_ge(self.tm.h, tmv)
                                vec.wait_ge(self.sm.h, sd)
                                if jp == 0:
                                    vec.wait_ge(self.vm.h, wbc_ready)
                                dst = act[:, jp * TBLK:(jp + 1) * TBLK]
                                self.dve_sync(vec, vec.tensor_tensor(
                                    dst, self.pb[u % 2][:, 0:TBLK], s_t[:, :], op=OP.mult))
                                cp = vec.tensor_tensor(dst, dst, wbc[:, :], op=OP.mult)
                                evac_done[u] = (self.vm, self.vm.inc(cp))
                            self.v(vup)
                            up_done[0] = self.vm.n

                act_ready = self.vm.n
                for f in range(NHC):
                    ki = kuse["dn"]
                    kuse["dn"] = ki + 1
                    slot = ki % 2

                    def tdn(ten, f=f, slot=slot, ki=ki, act_ready=act_ready):
                        u, bank = bank_begin(ten)
                        st_slab["dn"].wait(ten, ki)
                        if f == 0:
                            ten.wait_ge(self.vm.h, act_ready)
                        for i in range(NI):
                            mm = ten.matmul(bank[:, 0:TBLK],
                                            dnb[:, slot * FI + i * 128:slot * FI + (i + 1) * 128],
                                            act[:, i * TBLK:(i + 1) * TBLK],
                                            start=(i == 0), stop=(i == NI - 1))
                        self.tm.inc(mm)
                        cons_v[("dn", ki)] = self.tm.n
                        bank_end(u)
                    self.t(tdn)
                    u_used = pbu["n"] - 1
                    tmv = self.tm.n
                    seq_pos[0] += 1
                    if seq_pos[0] + 1 < len(seq):
                        emit_slab_load(seq_pos[0] + 1)

                    def vdn(vec, u=u_used, tmv=tmv, f=f, gi=gi, tb=tb):
                        vec.wait_ge(self.tm.h, tmv)
                        if gi == 0 and f == 0 and tb > 0 and last_ep_tm[0]:
                            vec.wait_ge(self.tm.h, last_ep_tm[0])
                        dst = oacc[:, f * TBLK:(f + 1) * TBLK]
                        if gi == 0:
                            cp = vec.tensor_copy(dst, self.pb[u % 2][:, 0:TBLK])
                        else:
                            cp = vec.tensor_tensor(dst, dst, self.pb[u % 2][:, 0:TBLK],
                                                   op=OP.add)
                        evac_done[u] = (self.vm, self.vm.inc(cp))
                    self.v(vdn)
            oacc_ready = self.vm.n

            for tc in range(4):
                slot = tc % 2
                cred = xo_credit.get(tb * 4 + tc - 2)

                def yx2(sync, tb=tb, tc=tc, slot=slot, cred=cred):
                    if cred is not None:
                        sync.wait_ge(cred[0].h, cred[1])
                    r0 = tb * TBLK + tc * 128
                    dd = sync.dma_start(out=xo[:, slot * H:(slot + 1) * H],
                                        in_=x_src[r0:r0 + 128, :])
                    st_xo.mark(dd)
                self.y(yx2)
                xo_item = st_xo.n - 1

                ep = []
                for grp in range(4):
                    def tep(ten, grp=grp, tc=tc, oacc_ready=oacc_ready):
                        u, bank = tbank_begin(ten)
                        if grp == 0 and tc == 0:
                            ten.wait_ge(self.vm.h, oacc_ready)
                        for q in range(4):
                            f = grp * 4 + q
                            mm = self.tr(ten, bank[:, q * 128:(q + 1) * 128],
                                         oacc[:, f * TBLK + tc * 128:f * TBLK + tc * 128 + 128])
                        self.tm.inc(mm)
                        ep.append((u, bank, self.tm.n))
                        tbank_end(u)
                    self.t(tep)
                last_ep_tm[0] = self.tm.n

                st_marks = []
                for grp in range(4):
                    u, bank, tmv = ep[grp]

                    def vev(vec, grp=grp, bank=bank, tmv=tmv, slot=slot, xo_item=xo_item, u=u):
                        vec.wait_ge(self.tm.h, tmv)
                        st_xo.wait(vec, xo_item)
                        cp = None
                        for q in range(4):
                            f = grp * 4 + q
                            dst = xo[:, slot * H + f * 128:slot * H + (f + 1) * 128]
                            cp = vec.scalar_tensor_tensor(
                                dst, bank[:, q * 128:(q + 1) * 128], 0.5, dst,
                                op0=OP.mult, op1=OP.add)
                        val = self.vm.inc(cp)
                        st_marks.append((self.vm, val))
                        tevac_done[u] = (self.vm, val)
                    self.v(vev)

                def yst(sync, tb=tb, tc=tc, slot=slot, st_marks=tuple(st_marks)):
                    for so, val in st_marks:
                        sync.wait_ge(so.h, val)
                    r0 = tb * TBLK + tc * 128
                    dd = sync.dma_start(out=x_dst[r0:r0 + 128, :],
                                        in_=xo[:, slot * H:(slot + 1) * H])
                    it = st_xs.mark(dd)
                    xo_credit[tb * 4 + tc] = st_xs.vals[it]
                self.y(yst)

        es.close()

    def attn_proj(self):
        nc = self.nc
        es = ExitStack()
        xlnT = es.enter_context(nc.sbuf_tensor("a1_xlnT", [128, NHC * T], F32))
        xa = es.enter_context(nc.sbuf_tensor("a1_xa", [128, H], F32))
        wraw = es.enter_context(nc.sbuf_tensor("a1_wraw", [128, H], F32))
        wT = es.enter_context(nc.sbuf_tensor("a1_wT", [128, H], F32))
        ext = es.enter_context(nc.sbuf_tensor("a1_ext", [128, 520], F32))
        qcA = es.enter_context(nc.sbuf_tensor("a1_qcA", [128, TBLK], F32))
        qcB = es.enter_context(nc.sbuf_tensor("a1_qcB", [128, TBLK], F32))
        qcv = es.enter_context(nc.sbuf_tensor("a1_qcv", [128, 2 * TBLK], F32))
        pr3 = es.enter_context(nc.sbuf_tensor("a1_pr3", [128, 3], F32))
        pr3i = es.enter_context(nc.sbuf_tensor("a1_pr3i", [128, 3], F32))
        x3row = self.trash[0:3, :]
        xln3T = es.enter_context(nc.sbuf_tensor("a1_xln3T", [128, 3 * NHC], F32))
        wbraw = self.trash[32:48, :]
        wbT = es.enter_context(nc.sbuf_tensor("a1_wbT", [128, NH * NHC], F32))
        l3a = es.enter_context(nc.sbuf_tensor("a1_l3a", [3, 1], F32))
        l3b = es.enter_context(nc.sbuf_tensor("a1_l3b", [3, 1], F32))

        st_xa = DmaStream(self, "a1_sxa")
        st_w = DmaStream(self, "a1_sw")
        st_qcv = DmaStream(self, "a1_sqcv")
        self._a1_xa_cons = {}
        self._a1_w_cons = {}

        def gcc(gp):
            gp.dma_start(out=self.cc_x_in[:, :], in_=self.x1s[T - 3:T, :]
                         ).then_inc(self.qg.h, 16)
            self.qg.n += 16
            gp.wait_ge(self.qg.h, self.qg.n)
            gp.collective_compute(
                "AllGather", OP.bypass,
                replica_groups=[[0, 1], [2, 3], [4, 5], [6, 7]],
                ins=[self.cc_x_in.opt()],
                outs=[self.cc_x_out.opt()],
            ).then_inc(self.gm.h, 1)
            self.gm.n += 1
            gp.wait_ge(self.gm.h, self.gm.n)
            gp.dma_start(out=x3row[:, :], in_=self.cc_x_out[0:3, :]
                         ).then_inc(self.qg.h, 16)
            self.qg.n += 16
            gp.wait_ge(self.qg.h, self.qg.n)
            self.gm.inc(gp.nop())
        self.g(gcc)
        gm_x3 = self.gm.n

        def v3a(vec):
            vec.wait_ge(self.gm.h, gm_x3)
            self.dve_sync(vec, vec.reduce_sum(l3a[:, :], x3row[:, :], axis=AX.X))
            self.dve_sync(vec, vec.tensor_scalar(l3a[:, :], l3a[:, :], 1.0 / H, None,
                                                 op0=OP.mult))
            self.vm.inc(vec.tensor_scalar(x3row[:, :], x3row[:, :], l3a[:, 0:1], None,
                                          op0=OP.subtract))
        self.v(v3a)
        vmv = self.vm.n

        def s3a(sc, vmv=vmv):
            sc.wait_ge(self.vm.h, vmv)
            self.sm.inc(sc.activation(self.trash[64:67, :], x3row[:, :], AF.Square,
                                      accum_out=l3b[:, 0:1]))
        self.s(s3a)
        smv = self.sm.n

        def v3b(vec, smv=smv):
            vec.wait_ge(self.sm.h, smv)
            self.dve_sync(vec, vec.tensor_scalar(l3b[:, :], l3b[:, :], 1.0 / H, None,
                                                 op0=OP.mult))
            self.vm.inc(vec.tensor_scalar(l3b[:, :], l3b[:, :], LN_EPS, None, op0=OP.add))
        self.v(v3b)
        smv2 = self.vm.n

        def s3b(sc, smv2=smv2):
            sc.wait_ge(self.vm.h, smv2)
            self.sm.inc(sc.activation(l3b[:, :], l3b[:, :], AF.Sqrt))
        self.s(s3b)
        smv3 = self.sm.n

        def v3c(vec, smv3=smv3):
            vec.wait_ge(self.sm.h, smv3)
            self.dve_sync(vec, vec.reciprocal(l3b[:, :], l3b[:, :]))
            self.vm.inc(vec.tensor_scalar(x3row[:, :], x3row[:, :], l3b[:, 0:1], None,
                                          op0=OP.mult))
        self.v(v3c)
        v_x3 = self.vm.n

        def yb(sync):
            sync.dma_start(out=wbraw[:, :], in_=self.wb).then_inc(self.qw.h, 16)
            self.qw.n += 16
            sync.wait_ge(self.qw.h, self.qw.n)
        self.y(yb)
        qw_wb = self.qw.n

        pool_t = PsumPool(self, [2, 3, 4, 5])
        pool_m = PsumPool(self, [0, 1])
        pool_x = PsumPool(self, [7])

        for k in range(NHC):
            def t3(ten, k=k, v_x3=v_x3):
                u, bank = pool_x.begin(ten)
                if k == 0:
                    ten.wait_ge(self.vm.h, v_x3)
                self.tm.inc(self.tr(ten, bank[:, 0:3], x3row[:, k * 128:(k + 1) * 128]))
                pool_x.took(u)
            self.t(t3)
            tmv = self.tm.n

            def v3d(vec, k=k, tmv=tmv):
                vec.wait_ge(self.tm.h, tmv)
                cp = vec.tensor_copy(xln3T[:, k * 3:(k + 1) * 3], self.pb[7][:, 0:3])
                pool_x.done_mark(self.vm, self.vm.inc(cp))
            self.v(v3d)

        for k in range(NHC):
            def tb(ten, k=k, qw_wb=qw_wb):
                u, bank = pool_x.begin(ten)
                if k == 0:
                    ten.wait_ge(self.qw.h, qw_wb)
                self.tm.inc(self.tr(ten, bank[:, 0:NH], wbraw[:, k * 128:(k + 1) * 128]))
                pool_x.took(u)
            self.t(tb)
            tmv = self.tm.n

            def vb(vec, k=k, tmv=tmv):
                vec.wait_ge(self.tm.h, tmv)
                cp = vec.tensor_copy(wbT[:, k * NH:(k + 1) * NH], self.pb[7][:, 0:NH])
                pool_x.done_mark(self.vm, self.vm.inc(cp))
            self.v(vb)

        for tc in range(NCH):
            slot = 0

            def yx(sync, tc=tc):
                if tc >= 1:
                    sync.wait_ge(self.tm.h, self._a1_xa_cons[tc - 1])
                dd = sync.dma_start(out=xa[:, :],
                                    in_=self.x1s[tc * 128:(tc + 1) * 128, :])
                st_xa.mark(dd)
            self.y(yx)
            xa_item = st_xa.n - 1
            self.emit_ln(xa, slot, st_xa, xa_item)
            v_ln = self.vm.n

            tg = []
            for grp in range(4):
                def tx(ten, grp=grp, v_ln=v_ln):
                    u, bank = pool_t.begin(ten)
                    if grp == 0:
                        ten.wait_ge(self.vm.h, v_ln)
                    for q in range(4):
                        k = grp * 4 + q
                        mm = self.tr(ten, bank[:, q * 128:(q + 1) * 128],
                                     xa[:, k * 128:(k + 1) * 128])
                    self.tm.inc(mm)
                    tg.append((u, bank, self.tm.n))
                    pool_t.took(u)
                self.t(tx)
            self._a1_xa_cons[tc] = tg[3][2]
            for grp in range(4):
                u, bank, tmv = tg[grp]

                def ev(eng, semo, is_v, grp=grp, bank=bank, tmv=tmv, tc=tc, u=u):
                    eng.wait_ge(self.tm.h, tmv)
                    for q in range(4):
                        k = grp * 4 + q
                        dst = xlnT[:, k * T + tc * 128:k * T + tc * 128 + 128]
                        if is_v:
                            cp = eng.tensor_copy(dst, bank[:, q * 128:(q + 1) * 128])
                        else:
                            cp = eng.copy(dst, bank[:, q * 128:(q + 1) * 128])
                    pool_t.done_mark_u(u, semo, semo.inc(cp))
                if grp % 2 == 0:
                    self.v(lambda vec, grp=grp, bank=bank, tmv=tmv, u=u:
                           ev(vec, self.vm, True, grp, bank, tmv, tc, u))
                else:
                    self.s(lambda sc, grp=grp, bank=bank, tmv=tmv, u=u:
                           ev(sc, self.sm, False, grp, bank, tmv, tc, u))
        xlnT_ready_v = self.vm.n
        xlnT_ready_s = self.sm.n

        for ts in range(4):
            def tbt(ten, ts=ts):
                u, bank = pool_m.begin(ten)
                if ts == 0:
                    ten.wait_ge(self.vm.h, xlnT_ready_v)
                    ten.wait_ge(self.sm.h, xlnT_ready_s)
                for k in range(NHC):
                    mm = ten.matmul(bank[:NH, 0:TBLK],
                                    wbT[:, k * NH:(k + 1) * NH],
                                    xlnT[:, k * T + ts * TBLK:k * T + (ts + 1) * TBLK],
                                    start=(k == 0), stop=(k == NHC - 1))
                self.tm.inc(mm)
                pool_m.took(u)
            self.t(tbt)
            tmv = self.tm.n
            u_used = pool_m.u - 1

            def sbt(sc, ts=ts, tmv=tmv, u=u_used):
                sc.wait_ge(self.tm.h, tmv)
                cp = sc.activation(self.beta_sb[:, ts * TBLK:(ts + 1) * TBLK],
                                   self.pb[u % 2][:NH, 0:TBLK], AF.Sigmoid)
                pool_m.done_mark_u(u, self.sm, self.sm.inc(cp))
            self.s(sbt)

        conv_sm = [0]
        for pi, (p, wsrc, dsts) in enumerate((("q", self.wq, self.qs),
                                              ("k", self.wk, self.ks),
                                              ("v", self.wv, self.vs))):
            for osl in range(NHC):
                widx = pi * NHC + osl
                slot = 0

                def yw(sync, wsrc=wsrc, osl=osl, widx=widx):
                    if widx >= 1:
                        sync.wait_ge(self.tm.h, self._a1_w_cons[widx - 1])
                    dd = sync.dma_start(out=wraw[:, :],
                                        in_=wsrc[osl * 128:(osl + 1) * 128, :])
                    st_w.mark(dd)
                self.y(yw)
                w_item = st_w.n - 1

                tg = []
                for grp in range(4):
                    def twt(ten, grp=grp, w_item=w_item):
                        u, bank = pool_t.begin(ten)
                        if grp == 0:
                            st_w.wait(ten, w_item)
                        for q in range(4):
                            k = grp * 4 + q
                            mm = self.tr(ten, bank[:, q * 128:(q + 1) * 128],
                                         wraw[:, k * 128:(k + 1) * 128])
                        self.tm.inc(mm)
                        tg.append((u, bank, self.tm.n))
                        pool_t.took(u)
                    self.t(twt)
                wT_marks = []
                for grp in range(4):
                    u, bank, tmv = tg[grp]

                    def evw(eng, semo, is_v, grp=grp, bank=bank, tmv=tmv, u=u):
                        eng.wait_ge(self.tm.h, tmv)
                        for q in range(4):
                            k = grp * 4 + q
                            dst = wT[:, k * 128:(k + 1) * 128]
                            if is_v:
                                cp = eng.tensor_copy(dst, bank[:, q * 128:(q + 1) * 128])
                            else:
                                cp = eng.copy(dst, bank[:, q * 128:(q + 1) * 128])
                        val = semo.inc(cp)
                        wT_marks.append((semo, val))
                        pool_t.done_mark_u(u, semo, val)
                    if grp % 2 == 0:
                        self.v(lambda vec, grp=grp, bank=bank, tmv=tmv, u=u:
                               evw(vec, self.vm, True, grp, bank, tmv, u))
                    else:
                        self.s(lambda sc, grp=grp, bank=bank, tmv=tmv, u=u:
                               evw(sc, self.sm, False, grp, bank, tmv, u))

                def tp3(ten, wT_marks=tuple(wT_marks)):
                    u, bank = pool_x.begin(ten)
                    for so, val in wT_marks:
                        ten.wait_ge(so.h, val)
                    for k in range(NHC):
                        mm = ten.matmul(bank[:, 0:3],
                                        wT[:, k * 128:(k + 1) * 128],
                                        xln3T[:, k * 3:(k + 1) * 3],
                                        start=(k == 0), stop=(k == NHC - 1))
                    self.tm.inc(mm)
                    pool_x.took(u)
                self.t(tp3)
                tmv = self.tm.n

                def vp3(vec, tmv=tmv):
                    vec.wait_ge(self.tm.h, tmv)
                    cp = vec.tensor_scalar(pr3i[:, :], self.pb[7][:, 0:3],
                                           self.flag_sb[:, 0:1], None, op0=OP.mult)
                    pool_x.done_mark(self.vm, self.vm.inc(cp))
                self.v(vp3)

                for ts in range(4):
                    def tmm(ten, ts=ts):
                        u, bank = pool_m.begin(ten)
                        for k in range(NHC):
                            mm = ten.matmul(bank[:, 0:TBLK],
                                            wT[:, k * 128:(k + 1) * 128],
                                            xlnT[:, k * T + ts * TBLK:k * T + (ts + 1) * TBLK],
                                            start=(k == 0), stop=(k == NHC - 1))
                        self.tm.inc(mm)
                        pool_m.took(u)
                    self.t(tmm)
                    tmv = self.tm.n
                    self._a1_w_cons[widx] = tmv
                    u_used = pool_m.u - 1

                    def vcv(vec, ts=ts, u=u_used, tmv=tmv, p=p, osl=osl):
                        vec.wait_ge(self.tm.h, tmv)
                        if conv_sm[0]:
                            vec.wait_ge(self.sm.h, conv_sm[0])
                        self.dve_sync(vec, vec.tensor_copy(ext[:, 3:515],
                                                           self.pb[u % 2][:, 0:TBLK]))
                        src3 = pr3i if ts == 0 else pr3
                        self.dve_sync(vec, vec.tensor_copy(ext[:, 0:3], src3[:, :]))
                        cw = self.cw[p]
                        self.dve_sync(vec, vec.tensor_scalar(
                            qcA[:, :], ext[:, 0:512], cw[:, osl * 4:osl * 4 + 1],
                            None, op0=OP.mult))
                        self.dve_sync(vec, vec.scalar_tensor_tensor(
                            qcB[:, :], ext[:, 1:513], cw[:, osl * 4 + 1:osl * 4 + 2],
                            qcA[:, :], op0=OP.mult, op1=OP.add))
                        self.dve_sync(vec, vec.scalar_tensor_tensor(
                            qcA[:, :], ext[:, 2:514], cw[:, osl * 4 + 2:osl * 4 + 3],
                            qcB[:, :], op0=OP.mult, op1=OP.add))
                        self.dve_sync(vec, vec.scalar_tensor_tensor(
                            qcB[:, :], ext[:, 3:515], cw[:, osl * 4 + 3:osl * 4 + 4],
                            qcA[:, :], op0=OP.mult, op1=OP.add))
                        cp = vec.tensor_copy(pr3[:, :], ext[:, 512:515])
                        val = self.vm.inc(cp)
                        pool_m.done_mark_u(u, self.vm, val)
                    self.v(vcv)
                    vmv = self.vm.n
                    qitem = st_qcv.n

                    def scv(sc, vmv=vmv, qslot=qitem % 2, qitem=qitem):
                        sc.wait_ge(self.vm.h, vmv)
                        if qitem >= 2:
                            st_qcv.wait(sc, qitem - 2)
                        self.sm.inc(sc.activation(qcv[:, qslot * TBLK:(qslot + 1) * TBLK],
                                                  qcB[:, :], AF.Silu))
                        conv_sm[0] = self.sm.n
                    self.s(scv)
                    smv = self.sm.n

                    def ycv(sync, dsts=dsts, ts=ts, osl=osl, smv=smv, qslot=qitem % 2):
                        sync.wait_ge(self.sm.h, smv)
                        ssem = st_qcv.begin_item()
                        for q in range(4):
                            c = ts * 4 + q
                            dd = sync.dma_start(
                                out=dsts[c, osl],
                                in_=qcv[:, qslot * TBLK + q * 128:qslot * TBLK + (q + 1) * 128])
                            ssem.inc(dd, 16)
                        st_qcv.commit()
                    self.y(ycv)
        es.close()

    def attn_a(self):
        nc = self.nc
        es = ExitStack()
        HG = 4
        NGR = NH // HG
        W = HG * 128
        qb = {p: es.enter_context(nc.sbuf_tensor(f"aa_qb_{p}", [128, 2 * H], F32))
              for p in "qkv"}
        DW = lambda name, w=W: es.enter_context(nc.sbuf_tensor(name, [128, w], F32))
        qst = [DW("aa_qst0"), DW("aa_qst1")]
        kst = [DW("aa_kst0"), DW("aa_kst1")]
        kfn = [DW("aa_kfn0"), DW("aa_kfn1")]
        Bw = [DW("aa_B0"), DW("aa_B1")]
        tqw = [DW("aa_tq0"), DW("aa_tq1")]
        TBw = [[DW(f"aa_TB{p}_{i}") for i in range(7)] for p in range(2)]
        Bpw = [[DW(f"aa_Bp{p}_{i}") for i in range(2)] for p in range(2)]
        ZK = [DW("aa_ZK0"), DW("aa_ZK1")]
        ZV = [DW("aa_ZV0"), DW("aa_ZV1")]
        Uw = [DW("aa_U0"), DW("aa_U1")]
        Gw = [DW("aa_G0"), DW("aa_G1")]
        OIw = [DW("aa_OI0"), DW("aa_OI1")]
        WFw = [DW("aa_WF0"), DW("aa_WF1")]
        sqw = [DW("aa_sq0"), DW("aa_sq1")]
        eall = [DW("aa_e0", 8), DW("aa_e1", 8)]
        rq_all = DW("aa_rq", NH)
        betaT = DW("aa_betaT", NH)

        st_qkv = DmaStream(self, "aa_sqkv")
        st_loc = DmaStream(self, "aa_sloc")
        st_rq = DmaStream(self, "aa_srq")
        pool = PsumPool(self, [0, 1, 2, 3, 4, 5, 6, 7])

        chunk_cons = {}
        sa_mark = {}
        gidx = [0]

        def emit_group_pair(c, q_item, pair):
            cslot = c % 2
            groups = []
            for gsub in range(2):
                g = pair * 2 + gsub
                h0 = g * HG
                gi = gidx[0] + gsub
                p = gi % 2
                groups.append(dict(g=g, h0=h0, p=p, gi=gi,
                                   qf=qb["q"][:, cslot * H + h0 * 128:cslot * H + (h0 + HG) * 128],
                                   kf=qb["k"][:, cslot * H + h0 * 128:cslot * H + (h0 + HG) * 128],
                                   vf=qb["v"][:, cslot * H + h0 * 128:cslot * H + (h0 + HG) * 128],
                                   st={}))
            gidx[0] += 2

            def seg(wt, j, w=128):
                return wt[:, j * w:(j + 1) * w]

            stages = []

            def t1(G, ten):
                p = G["p"]
                banks = []
                for src in (G["qf"], G["kf"], G["vf"]):
                    u, bank = pool.begin(ten)
                    for j in range(HG):
                        mm = self.tr(ten, seg(bank, j), seg(src, j))
                    self.tm.inc(mm)
                    pool.took(u)
                    banks.append(u)
                G["st"]["b1"] = banks
                if G["gi"] >= 2:
                    pass
                G["tmv1"] = self.tm.n

            def v1(G, vec):
                p = G["p"]
                vec.wait_ge(self.tm.h, G["tmv1"])
                prev_loc = st_loc.vals.get(G["gi"] - 2)
                if prev_loc is not None:
                    vec.wait_ge(prev_loc[0].h, prev_loc[1])
                bq, bk, bv = [pool.bank(u) for u in G["st"]["b1"]]
                self.dve_sync(vec, vec.tensor_copy(qst[p][:, :], bq[:, 0:W]))
                self.dve_sync(vec, vec.tensor_copy(kst[p][:, :], bk[:, 0:W]))
                last = None
                for j in range(HG):
                    last = vec.tensor_scalar(seg(ZV[p], j), seg(bv, j),
                                             betaT[:, G["h0"] + j:G["h0"] + j + 1],
                                             None, op0=OP.mult)
                val = self.vm.inc(last)
                for u in G["st"]["b1"]:
                    pool.done_mark_u(u, self.vm, val)
            stages.append((t1, v1, None))

            def v2(G, vec):
                p = G["p"]
                self.dve_sync(vec, vec.tensor_tensor(sqw[p][:, :], qst[p][:, :],
                                                     qst[p][:, :], op=OP.mult))
                for j in range(HG):
                    self.dve_sync(vec, vec.reduce_sum(eall[p][:, j:j + 1],
                                                      seg(sqw[p], j), axis=AX.X))
                self.dve_sync(vec, vec.tensor_tensor(sqw[p][:, :], kst[p][:, :],
                                                     kst[p][:, :], op=OP.mult))
                for j in range(HG):
                    self.dve_sync(vec, vec.reduce_sum(eall[p][:, 4 + j:5 + j],
                                                      seg(sqw[p], j), axis=AX.X))
                self.vm.inc(vec.tensor_scalar(eall[p][:, :], eall[p][:, :],
                                              L2_EPS, None, op0=OP.add))
                G["v2"] = self.vm.n

            def s2(G, sc):
                sc.wait_ge(self.vm.h, G["v2"])
                self.sm.inc(sc.activation(eall[G["p"]][:, :], eall[G["p"]][:, :], AF.Sqrt))
                G["s2"] = self.sm.n
            stages.append((None, v2, s2))

            def v3(G, vec):
                p = G["p"]
                vec.wait_ge(self.sm.h, G["s2"])
                if G["g"] == 0 and c >= 1:
                    st_rq.wait(vec, c - 1)
                self.dve_sync(vec, vec.reciprocal(eall[p][:, :], eall[p][:, :]))
                self.dve_sync(vec, vec.tensor_scalar(
                    rq_all[:, G["h0"]:G["h0"] + HG], eall[p][:, 0:4], HD_SCALE,
                    None, op0=OP.mult))
                for j in range(HG):
                    self.dve_sync(vec, vec.tensor_scalar(
                        seg(kst[p], j), seg(kst[p], j), eall[p][:, 4 + j:5 + j],
                        None, op0=OP.mult))
                last = None
                for j in range(HG):
                    last = vec.tensor_scalar(seg(ZK[p], j), seg(kst[p], j),
                                             betaT[:, G["h0"] + j:G["h0"] + j + 1],
                                             None, op0=OP.mult)
                val = self.vm.inc(last)
                G["v3"] = val
            stages.append((None, v3, None))

            def t4(G, ten):
                p = G["p"]
                u, bank = pool.begin(ten)
                ten.wait_ge(self.vm.h, G["v3"])
                for j in range(HG):
                    mm = self.tr(ten, seg(bank, j), seg(kst[p], j))
                self.tm.inc(mm)
                pool.took(u)
                G["st"]["b4"] = u
                G["tmv4"] = self.tm.n

            def v4(G, vec):
                p = G["p"]
                vec.wait_ge(self.tm.h, G["tmv4"])
                bank = pool.bank(G["st"]["b4"])
                cp = vec.tensor_copy(kfn[p][:, :], bank[:, 0:W])
                pool.done_mark_u(G["st"]["b4"], self.vm, self.vm.inc(cp))
                G["v4"] = self.vm.n
            stages.append((t4, v4, None))

            def t5(G, ten):
                p = G["p"]
                uk, bk = pool.begin(ten)
                ten.wait_ge(self.vm.h, G["v4"])
                for j in range(HG):
                    mm = ten.matmul(seg(bk, j), seg(kfn[p], j), seg(kfn[p], j),
                                    start=True, stop=True)
                self.tm.inc(mm)
                pool.took(uk)
                uq, bq = pool.begin(ten)
                for j in range(HG):
                    mm = ten.matmul(seg(bq, j), seg(G["qf"], j), seg(kfn[p], j),
                                    start=True, stop=True)
                self.tm.inc(mm)
                pool.took(uq)
                G["st"]["b5"] = (uk, uq)
                G["tmv5"] = self.tm.n

            def v5(G, vec):
                p = G["p"]
                vec.wait_ge(self.tm.h, G["tmv5"])
                uk, uq = G["st"]["b5"]
                bk, bq = pool.bank(uk), pool.bank(uq)
                last = None
                for j in range(HG):
                    last = vec.scalar_tensor_tensor(
                        seg(Bw[p], j), seg(bk, j),
                        betaT[:, G["h0"] + j:G["h0"] + j + 1], self.stril[:, :],
                        op0=OP.mult, op1=OP.mult)
                self.dve_sync(vec, last)
                cp = vec.tensor_tensor(tqw[p][:, :], bq[:, 0:W], self.tril0w[:, :],
                                       op=OP.mult)
                val = self.vm.inc(cp)
                pool.done_mark_u(uk, self.vm, val)
                pool.done_mark_u(uq, self.vm, val)
                G["v5"] = val
            stages.append((t5, v5, None))

            def t6(G, ten):
                p = G["p"]
                ub, bb = pool.begin(ten)
                ten.wait_ge(self.vm.h, G["v5"])
                for j in range(HG):
                    mm = self.tr(ten, seg(bb, j), seg(Bw[p], j))
                self.tm.inc(mm)
                pool.took(ub)
                ut, bt = pool.begin(ten)
                for j in range(HG):
                    mm = self.tr(ten, seg(bt, j), seg(tqw[p], j))
                self.tm.inc(mm)
                pool.took(ut)
                G["st"]["b6"] = (ub, ut)
                G["tmv6"] = self.tm.n

            def v6(G, vec):
                p = G["p"]
                vec.wait_ge(self.tm.h, G["tmv6"])
                ub, ut = G["st"]["b6"]
                self.dve_sync(vec, vec.tensor_copy(TBw[p][0][:, :],
                                                   pool.bank(ub)[:, 0:W]))
                cp = vec.tensor_copy(tqw[p][:, :], pool.bank(ut)[:, 0:W])
                val = self.vm.inc(cp)
                pool.done_mark_u(ub, self.vm, val)
                pool.done_mark_u(ut, self.vm, val)
                G["v6"] = val
            stages.append((t6, v6, None))

            for i in range(1, 7):
                def tsq(G, ten, i=i):
                    p = G["p"]
                    Bp_cur = Bw[p] if i == 1 else Bpw[p][(i - 1) % 2]
                    ten.wait_ge(self.vm.h, G["v6"] if i == 1 else G[f"vsq{i-1}"])
                    if i >= 2:
                        ten.wait_ge(self.sm.h, G[f"ssq{i-1}"])
                    uT, bT = pool.begin(ten)
                    for j in range(HG):
                        mm = ten.matmul(seg(bT, j), seg(Bp_cur, j), seg(TBw[p][i - 1], j),
                                        start=True, stop=True)
                    self.tm.inc(mm)
                    pool.took(uT)
                    uB = None
                    if i < 6:
                        uB, bB = pool.begin(ten)
                        for j in range(HG):
                            mm = ten.matmul(seg(bB, j), seg(TBw[p][i - 1], j),
                                            seg(Bp_cur, j), start=True, stop=True)
                        self.tm.inc(mm)
                        pool.took(uB)
                    G["st"][f"bsq{i}"] = (uT, uB)
                    G[f"tmsq{i}"] = self.tm.n

                def vsq(G, vec, i=i):
                    p = G["p"]
                    vec.wait_ge(self.tm.h, G[f"tmsq{i}"])
                    uT, uB = G["st"][f"bsq{i}"]
                    cp = vec.tensor_copy(TBw[p][i][:, :], pool.bank(uT)[:, 0:W])
                    val = self.vm.inc(cp)
                    pool.done_mark_u(uT, self.vm, val)
                    G[f"vsq{i}"] = val

                def ssq(G, sc, i=i):
                    p = G["p"]
                    uT, uB = G["st"][f"bsq{i}"]
                    if uB is None:
                        G[f"ssq{i}"] = self.sm.n
                        return
                    sc.wait_ge(self.tm.h, G[f"tmsq{i}"])
                    cp = sc.copy(Bpw[p][i % 2][:, :], pool.bank(uB)[:, 0:W])
                    val = self.sm.inc(cp)
                    pool.done_mark_u(uB, self.sm, val)
                    G[f"ssq{i}"] = val
                stages.append((tsq, vsq, ssq))

            for ai, i in enumerate([6, 5, 4, 3, 2, 1, 0]):
                def tap(G, ten, i=i, ai=ai):
                    p = G["p"]
                    ten.wait_ge(self.vm.h, G["vsq6"] if ai == 0 else G[f"vap{ai-1}"])
                    TBi = Bw[p] if i == 0 else TBw[p][i]
                    TBi = TBw[p][0] if i == 0 else TBw[p][i]
                    uK, bK = pool.begin(ten)
                    for j in range(HG):
                        mm = ten.matmul(seg(bK, j), seg(TBi, j), seg(ZK[p], j),
                                        start=True, stop=True)
                    self.tm.inc(mm)
                    pool.took(uK)
                    uV, bV = pool.begin(ten)
                    for j in range(HG):
                        mm = ten.matmul(seg(bV, j), seg(TBi, j), seg(ZV[p], j),
                                        start=True, stop=True)
                    self.tm.inc(mm)
                    pool.took(uV)
                    G["st"][f"bap{ai}"] = (uK, uV)
                    G[f"tmap{ai}"] = self.tm.n

                def vap(G, vec, i=i, ai=ai):
                    p = G["p"]
                    vec.wait_ge(self.tm.h, G[f"tmap{ai}"])
                    uK, uV = G["st"][f"bap{ai}"]
                    op = OP.subtract if i == 0 else OP.add
                    self.dve_sync(vec, vec.tensor_tensor(
                        ZK[p][:, :], ZK[p][:, :], pool.bank(uK)[:, 0:W], op=op))
                    cp = vec.tensor_tensor(ZV[p][:, :], ZV[p][:, :],
                                           pool.bank(uV)[:, 0:W], op=op)
                    val = self.vm.inc(cp)
                    pool.done_mark_u(uK, self.vm, val)
                    pool.done_mark_u(uV, self.vm, val)
                    G[f"vap{ai}"] = val
                stages.append((tap, vap, None))

            def t20(G, ten):
                p = G["p"]
                ten.wait_ge(self.vm.h, G["vap6"])
                ug, bg = pool.begin(ten)
                for j in range(HG):
                    mm = ten.matmul(seg(bg, j), seg(ZK[p], j), seg(tqw[p], j),
                                    start=True, stop=True)
                self.tm.inc(mm)
                pool.took(ug)
                uo, bo = pool.begin(ten)
                for j in range(HG):
                    mm = ten.matmul(seg(bo, j), seg(tqw[p], j), seg(ZV[p], j),
                                    start=True, stop=True)
                self.tm.inc(mm)
                pool.took(uo)
                uw, bw = pool.begin(ten)
                for j in range(HG):
                    mm = self.tr(ten, seg(bw, j), seg(ZK[p], j))
                self.tm.inc(mm)
                pool.took(uw)
                G["st"]["b20"] = (ug, uo, uw)
                G["tmv20"] = self.tm.n

            def v20(G, vec):
                p = G["p"]
                vec.wait_ge(self.tm.h, G["tmv20"])
                ug, uo, uw = G["st"]["b20"]
                self.dve_sync(vec, vec.tensor_tensor(Gw[p][:, :], G["qf"],
                                                     pool.bank(ug)[:, 0:W],
                                                     op=OP.subtract))
                self.dve_sync(vec, vec.tensor_copy(OIw[p][:, :],
                                                   pool.bank(uo)[:, 0:W]))
                cp = vec.tensor_copy(WFw[p][:, :], pool.bank(uw)[:, 0:W])
                val = self.vm.inc(cp)
                for u in (ug, uo, uw):
                    pool.done_mark_u(u, self.vm, val)
                G["v20"] = val
            stages.append((t20, v20, None))

            def s21(G, sc):
                p = G["p"]
                sc.wait_ge(self.vm.h, G["v20"])
                ssem = st_loc.begin_item()
                for j in range(HG):
                    h = G["h0"] + j
                    ssem.inc(sc.dma_start(out=self.loc[c, h, 0], in_=seg(WFw[p], j)), 16)
                    ssem.inc(sc.dma_start(out=self.loc[c, h, 1], in_=seg(ZV[p], j)), 16)
                    ssem.inc(sc.dma_start(out=self.loc[c, h, 2], in_=seg(kst[p], j)), 16)
                    ssem.inc(sc.dma_start(out=self.loc[c, h, 3], in_=seg(Gw[p], j)), 16)
                    ssem.inc(sc.dma_start(out=self.loc[c, h, 4], in_=seg(OIw[p], j)), 16)
                it = st_loc.commit()
                assert it == G["gi"]
            stages.append((None, None, s21))

            def t22(G, ten):
                p = G["p"]
                prev = sa_mark.get(G["g"])
                if prev is not None:
                    ten.wait_ge(self.vm.h, prev)
                ten.wait_ge(self.vm.h, G["v20"])
                u, bank = pool.begin(ten)
                for j in range(HG):
                    mm = ten.matmul(seg(bank, j), seg(WFw[p], j),
                                    self.SA[:, (G["h0"] + j) * 128:(G["h0"] + j + 1) * 128],
                                    start=True, stop=True)
                self.tm.inc(mm)
                pool.took(u)
                G["st"]["b22"] = u
                G["tmv22"] = self.tm.n

            def v22(G, vec):
                p = G["p"]
                vec.wait_ge(self.tm.h, G["tmv22"])
                u = G["st"]["b22"]
                cp = vec.tensor_tensor(Uw[p][:, :], ZV[p][:, :],
                                       pool.bank(u)[:, 0:W], op=OP.subtract)
                val = self.vm.inc(cp)
                pool.done_mark_u(u, self.vm, val)
                G["v22"] = val
            stages.append((t22, v22, None))

            def t23(G, ten):
                p = G["p"]
                ten.wait_ge(self.vm.h, G["v22"])
                u, bank = pool.begin(ten)
                for j in range(HG):
                    mm = ten.matmul(seg(bank, j), seg(kst[p], j), seg(Uw[p], j),
                                    start=True, stop=True)
                self.tm.inc(mm)
                pool.took(u)
                G["st"]["b23"] = u
                G["tmv23"] = self.tm.n
                chunk_cons[(c, G["g"])] = self.tm.n

            def v23(G, vec):
                p = G["p"]
                vec.wait_ge(self.tm.h, G["tmv23"])
                u = G["st"]["b23"]
                sl = self.SA[:, G["h0"] * 128:(G["h0"] + HG) * 128]
                cp = vec.tensor_tensor(sl, sl, pool.bank(u)[:, 0:W], op=OP.add)
                val = self.vm.inc(cp)
                pool.done_mark_u(u, self.vm, val)
                sa_mark[G["g"]] = val
            stages.append((t23, v23, None))

            for (tf, vf, sf) in stages:
                if tf is not None:
                    for G in groups:
                        self.t(lambda ten, tf=tf, G=G: tf(G, ten))
                if vf is not None:
                    for G in groups:
                        self.v(lambda vec, vf=vf, G=G: vf(G, vec))
                if sf is not None:
                    for G in groups:
                        self.s(lambda sc, sf=sf, G=G: sf(G, sc))

        for c in range(NCH):
            def yq(sync, c=c):
                if c >= 2:
                    for g in range(NGR):
                        sync.wait_ge(self.tm.h, chunk_cons[(c - 2, g)])
                slot = c % 2
                ssem = st_qkv.begin_item()
                for p, src in (("q", self.qs), ("k", self.ks), ("v", self.vs)):
                    for oc in range(NHC):
                        dd = sync.dma_start(
                            out=qb[p][:, slot * H + oc * 128:slot * H + (oc + 1) * 128],
                            in_=src[c, oc])
                        ssem.inc(dd, 16)
                st_qkv.commit()
            self.y(yq)
            q_item = st_qkv.n - 1

            def tbt(ten, c=c, q_item=q_item):
                u, bank = pool.begin(ten)
                st_qkv.wait(ten, q_item)
                self.tm.inc(self.tr(ten, bank[:, 0:NH],
                                    self.beta_sb[:, c * 128:(c + 1) * 128]))
                pool.took(u)
                self._aa_bt = (u, self.tm.n)
            self.t(tbt)

            def vbt(vec, c=c):
                u, tmv = self._aa_bt
                vec.wait_ge(self.tm.h, tmv)
                cp = vec.tensor_copy(betaT[:, :], pool.bank(u)[:, 0:NH])
                pool.done_mark_u(u, self.vm, self.vm.inc(cp))
            self.v(vbt)

            for pair in range(NGR // 2):
                emit_group_pair(c, q_item, pair)

            v_rq = self.vm.n

            def srq(sc, c=c, v_rq=v_rq):
                sc.wait_ge(self.vm.h, v_rq)
                dd = sc.dma_start(out=self.locr[c], in_=rq_all[:, :])
                st_rq.mark(dd)
            self.s(srq)
        es.close()


    def attn_cc(self):
        sa_final = self.vm.n

        def gcc(gp):
            gp.wait_ge(self.vm.h, sa_final)
            gp.dma_start(out=self.cc_s_in[:, :], in_=self.SA[:, :]).then_inc(self.qg.h, 16)
            self.qg.n += 16
            gp.wait_ge(self.qg.h, self.qg.n)
            gp.collective_compute(
                "AllGather", OP.bypass,
                replica_groups=[[0, 1], [2, 3], [4, 5], [6, 7]],
                ins=[self.cc_s_in.opt()],
                outs=[self.cc_s_out.opt()],
            ).then_inc(self.gm.h, 1)
            self.gm.n += 1
            gp.wait_ge(self.gm.h, self.gm.n)
            gp.dma_start(out=self.SB[:, :], in_=self.cc_s_out[0:128, :]
                         ).then_inc(self.qg.h, 16)
            self.qg.n += 16
            gp.wait_ge(self.qg.h, self.qg.n)
            self.gm.inc(gp.nop())
        self.g(gcc)
        gm_v = self.gm.n

        def vflag(vec):
            vec.wait_ge(self.gm.h, gm_v)
            self.vm.inc(vec.tensor_scalar(self.SB[:, :], self.SB[:, :],
                                          self.flag_sb[:, 0:1], None, op0=OP.mult))
        self.v(vflag)

    def attn_b(self, o_fm):
        nc = self.nc
        es = ExitStack()
        lb = [es.enter_context(nc.sbuf_tensor(f"ab_lb{i}", [128, 5 * 128], F32))
              for i in range(2)]
        rqb = [es.enter_context(nc.sbuf_tensor(f"ab_rq{i}", [128, NH], F32))
               for i in range(2)]
        t_ou = [es.enter_context(nc.sbuf_tensor(f"ab_ou{i}", [128, 128], F32))
                for i in range(2)]
        t_u = es.enter_context(nc.sbuf_tensor("ab_u", [128, 128], F32))
        sso = es.enter_context(nc.sbuf_tensor("ab_sso", [128, 1], F32))
        rq2 = es.enter_context(nc.sbuf_tensor("ab_rq2", [128, 1], F32))
        tr3 = es.enter_context(nc.sbuf_tensor("ab_tr3", [128, 128], F32))

        st_lb = DmaStream(self, "ab_slb")
        st_rq = DmaStream(self, "ab_srq2")
        pool = PsumPool(self, [0, 1, 2, 3])
        sb_mark = {}
        ou_cons = {}
        hu = [0]

        for c in range(NCH):
            def yrq(sync, c=c):
                if c >= 2:
                    sync.wait_ge(self.vm.h, self._ab_rq_cons.get(c - 2, 0))
                dd = sync.dma_start(out=rqb[c % 2][:, :], in_=self.locr[c])
                st_rq.mark(dd)
            self.y(yrq)
            rq_item = st_rq.n - 1
            if not hasattr(self, "_ab_rq_cons"):
                self._ab_rq_cons = {}

            for h in range(NH):
                hs = hu[0] % 2
                def ylb(sync, c=c, h=h, hs=hs, it=hu[0]):
                    if it >= 2:
                        prev = ou_cons.get(it - 2)
                        if prev is not None:
                            sync.wait_ge(prev[0].h, prev[1])
                    ssem = st_lb.begin_item()
                    for j in range(5):
                        dd = sync.dma_start(out=lb[hs][:, j * 128:(j + 1) * 128],
                                            in_=self.loc[c, h, j])
                        ssem.inc(dd, 16)
                    st_lb.commit()
                self.y(ylb)
                lb_item = st_lb.n - 1
                W_ = lb[hs][:, 0:128]
                Uv = lb[hs][:, 128:256]
                Kst = lb[hs][:, 256:384]
                G_ = lb[hs][:, 384:512]
                Oint = lb[hs][:, 512:640]
                prev_sb = sb_mark.get(h)

                def t1(ten, hs=hs, h=h, lb_item=lb_item, prev_sb=prev_sb):
                    u, bank = pool.begin(ten)
                    st_lb.wait(ten, lb_item)
                    if prev_sb is not None:
                        ten.wait_ge(self.vm.h, prev_sb)
                    ten.matmul(bank[:, 0:128], lb[hs][:, 384:512],
                               self.SB[:, h * 128:(h + 1) * 128], start=True, stop=True)
                    self.tm.inc(ten.matmul(bank[:, 128:256], lb[hs][:, 0:128],
                                           self.SB[:, h * 128:(h + 1) * 128],
                                           start=True, stop=True))
                    pool.took(u)
                self.t(t1)
                tmv = self.tm.n
                uu = pool.u - 1

                def v2(vec, hs=hs, uu=uu, tmv=tmv):
                    vec.wait_ge(self.tm.h, tmv)
                    bank = self.pb[pool.banks[uu % len(pool.banks)]]
                    self.dve_sync(vec, vec.tensor_tensor(t_ou[hs][:, :], bank[:, 0:128],
                                                         lb[hs][:, 512:640], op=OP.add))
                    cp = vec.tensor_tensor(t_u[:, :], lb[hs][:, 128:256], bank[:, 128:256],
                                           op=OP.subtract)
                    pool.done_mark_u(uu, self.vm, self.vm.inc(cp))
                self.v(v2)
                vmv = self.vm.n

                def t3(ten, hs=hs, vmv=vmv):
                    u, bank = pool.begin(ten)
                    ten.wait_ge(self.vm.h, vmv)
                    self.tm.inc(ten.matmul(bank[:, 0:128], lb[hs][:, 256:384], t_u[:, :],
                                           start=True, stop=True))
                    pool.took(u)
                self.t(t3)
                tmv = self.tm.n
                uu = pool.u - 1

                def v3(vec, h=h, uu=uu, tmv=tmv):
                    vec.wait_ge(self.tm.h, tmv)
                    bank = self.pb[pool.banks[uu % len(pool.banks)]]
                    cp = vec.tensor_tensor(self.SB[:, h * 128:(h + 1) * 128],
                                           self.SB[:, h * 128:(h + 1) * 128],
                                           bank[:, 0:128], op=OP.add)
                    val = self.vm.inc(cp)
                    sb_mark[h] = val
                    pool.done_mark_u(uu, self.vm, val)
                self.v(v3)
                vmv = self.vm.n

                def v6(vec, hs=hs, rq_item=rq_item, h=h, c=c):
                    st_rq.wait(vec, rq_item)
                    self.dve_sync(vec, vec.tensor_tensor(tr3[:, :], t_ou[hs][:, :],
                                                         t_ou[hs][:, :], op=OP.mult))
                    self.dve_sync(vec, vec.reduce_sum(sso[:, 0:1], tr3[:, :], axis=AX.X))
                    self.dve_sync(vec, vec.tensor_tensor(rq2[:, :], rqb[c % 2][:, h:h + 1],
                                                         rqb[c % 2][:, h:h + 1], op=OP.mult))
                    self.dve_sync(vec, vec.tensor_scalar(sso[:, :], sso[:, :],
                                                         1.0 / HD, None, op0=OP.mult))
                    self.dve_sync(vec, vec.tensor_tensor(sso[:, :], sso[:, :], rq2[:, :],
                                                         op=OP.mult))
                    self.vm.inc(vec.tensor_scalar(sso[:, :], sso[:, :], RMS_EPS,
                                                  None, op0=OP.add))
                self.v(v6)
                vmv = self.vm.n

                def s6(sc, vmv=vmv):
                    sc.wait_ge(self.vm.h, vmv)
                    self.sm.inc(sc.activation(sso[:, :], sso[:, :], AF.Sqrt))
                self.s(s6)
                smv = self.sm.n

                def v7(vec, hs=hs, smv=smv, h=h, c=c):
                    vec.wait_ge(self.sm.h, smv)
                    self.dve_sync(vec, vec.reciprocal(sso[:, :], sso[:, :]))
                    self.dve_sync(vec, vec.tensor_tensor(sso[:, :], sso[:, :],
                                                         rqb[c % 2][:, h:h + 1], op=OP.mult))
                    self.vm.inc(vec.tensor_scalar(t_ou[hs][:, :], t_ou[hs][:, :],
                                                  sso[:, 0:1], None, op0=OP.mult))
                    self._ab_rq_cons[c] = self.vm.n
                self.v(v7)
                vmv = self.vm.n

                def t8(ten, hs=hs, vmv=vmv):
                    u, bank = pool.begin(ten)
                    ten.wait_ge(self.vm.h, vmv)
                    self.tm.inc(self.tr(ten, bank[:, 0:128], t_ou[hs][:, :]))
                    pool.took(u)
                self.t(t8)
                tmv = self.tm.n
                uu = pool.u - 1

                def s8(sc, h=h, c=c, uu=uu, tmv=tmv):
                    sc.wait_ge(self.tm.h, tmv)
                    bank = self.pb[pool.banks[uu % len(pool.banks)]]
                    cp = sc.copy(o_fm[:, h * T + c * 128:h * T + (c + 1) * 128],
                                 bank[:, 0:128])
                    val = self.sm.inc(cp)
                    ou_cons[hu[0]] = (self.sm, val)
                    pool.done_mark_u(uu, self.sm, val)
                self.s(s8)
                hu[0] += 1
        es.close()

    def attn_out(self, o_fm):
        nc = self.nc
        es = ExitStack()
        woT = es.enter_context(nc.sbuf_tensor("a2_woT", [128, 2 * H], F32))
        x2tm = es.enter_context(nc.sbuf_tensor("a2_x2tm", [128, 2 * H], F32))
        yb = es.enter_context(nc.sbuf_tensor("a2_yb", [128, TS_A2], F32))
        st_wo = DmaStream(self, "a2_swo")
        st_x = DmaStream(self, "a2_sx")
        st_st = DmaStream(self, "a2_sst")
        pool = PsumPool(self, [0, 1])
        pool_t = PsumPool(self, [2, 3])
        o_ready_v = self.vm.n
        o_ready_s = self.sm.n
        NTS = T // TS_A2
        wo_cons = {}
        x2_done = {}

        for ts in range(NTS):
            for tc in range(2):
                def yx(sync, ts=ts, tc=tc):
                    it = ts * 2 + tc
                    if it >= 2:
                        prev = x2_done.get(it - 2)
                        if prev is not None:
                            sync.wait_ge(prev[0].h, prev[1])
                    r0 = ts * TS_A2 + tc * 128
                    dd = sync.dma_start(out=x2tm[:, tc * H:(tc + 1) * H],
                                        in_=self.x1s[r0:r0 + 128, :])
                    st_x.mark(dd)
                self.y(yx)
            x_item = st_x.n - 1

            for osl in range(NHC):
                widx = ts * NHC + osl
                slot = widx % 2

                def ywo(sync, osl=osl, widx=widx, slot=slot):
                    if widx >= 2:
                        sync.wait_ge(self.tm.h, wo_cons[widx - 2])
                    dd = sync.dma_start(out=woT[:, slot * H:(slot + 1) * H],
                                        in_=self.wot[osl])
                    st_wo.mark(dd)
                self.y(ywo)
                w_item = st_wo.n - 1

                def tmm(ten, ts=ts, slot=slot, w_item=w_item, osl=osl, widx=widx):
                    u, bank = pool.begin(ten)
                    st_wo.wait(ten, w_item)
                    if widx == 0:
                        ten.wait_ge(self.vm.h, o_ready_v)
                        ten.wait_ge(self.sm.h, o_ready_s)
                    for k in range(NHC):
                        mm = ten.matmul(bank[:, 0:TS_A2],
                                        woT[:, slot * H + k * 128:slot * H + (k + 1) * 128],
                                        o_fm[:, k * T + ts * TS_A2:k * T + (ts + 1) * TS_A2],
                                        start=(k == 0), stop=(k == NHC - 1))
                    self.tm.inc(mm)
                    wo_cons[widx] = self.tm.n
                    pool.took(u)
                self.t(tmm)
                tmv = self.tm.n
                uu = pool.u - 1

                def vev(vec, uu=uu, tmv=tmv):
                    vec.wait_ge(self.tm.h, tmv)
                    bank = self.pb[pool.banks[uu % len(pool.banks)]]
                    cp = vec.tensor_copy(yb[:, :], bank[:, 0:TS_A2])
                    pool.done_mark_u(uu, self.vm, self.vm.inc(cp))
                self.v(vev)
                vmv = self.vm.n

                def ttr(ten, vmv=vmv):
                    u, bank = pool_t.begin(ten)
                    ten.wait_ge(self.vm.h, vmv)
                    self.tr(ten, bank[:, 0:128], yb[:, 0:128])
                    self.tm.inc(self.tr(ten, bank[:, 128:256], yb[:, 128:256]))
                    pool_t.took(u)
                self.t(ttr)
                tmv = self.tm.n
                uu = pool_t.u - 1

                def vadd(vec, osl=osl, uu=uu, tmv=tmv, x_item=x_item):
                    vec.wait_ge(self.tm.h, tmv)
                    st_x.wait(vec, x_item - 1)
                    st_x.wait(vec, x_item)
                    bank = self.pb[pool_t.banks[uu % len(pool_t.banks)]]
                    for tc in range(2):
                        dst = x2tm[:, tc * H + osl * 128:tc * H + (osl + 1) * 128]
                        cp = vec.tensor_tensor(dst, dst, bank[:, tc * 128:(tc + 1) * 128],
                                               op=OP.add)
                    pool_t.done_mark_u(uu, self.vm, self.vm.inc(cp))
                self.v(vadd)
            v_x2 = self.vm.n

            def yst(sync, ts=ts, v_x2=v_x2):
                sync.wait_ge(self.vm.h, v_x2)
                for tc in range(2):
                    r0 = ts * TS_A2 + tc * 128
                    dd = sync.dma_start(out=self.x2s[r0:r0 + 128, :],
                                        in_=x2tm[:, tc * H:(tc + 1) * H])
                    it = st_st.mark(dd)
                    x2_done[ts * 2 + tc] = st_st.vals[it]
            self.y(yst)
        es.close()

    def finish(self):
        final = self.qw.n

        def yend(sync):
            sync.wait_ge(self.qw.h, final)
        self.y(yend)
        self.es.close()
        return self.nc


def _build_nc(phases=("f1", "attn", "f2")):
    mk = MK()
    mk.p0()
    mk.phase_barrier()
    if "f1" in phases:
        mk.ffn(mk.x0, mk.x1s, "f1")
        mk.phase_barrier()
    if "f1out" in phases:
        mk.ffn(mk.x0, mk.out, "f1")
        mk.phase_barrier()
    if "attn" in phases:
        mk.attn_proj()
        mk.phase_barrier()
        mk.attn_a()
        mk.phase_barrier()
        mk.attn_cc()
        mk.phase_barrier()
        o_es = ExitStack()
        o_fm = o_es.enter_context(mk.nc.sbuf_tensor("o_fm", [128, NHC * T], F32))
        mk.attn_b(o_fm)
        mk.phase_barrier()
        mk.attn_out(o_fm)
        mk.phase_barrier()
        o_es.close()
    if "attnout" in phases:
        pass
    if "f2" in phases:
        mk.ffn(mk.x2s, mk.out, "f2")
        mk.phase_barrier()
    return mk.finish()


SEL8 = np.zeros((NG, NG * 128), np.float32)
for _g in range(NG):
    SEL8[_g, _g * 128:(_g + 1) * 128] = 1.0


def _make_in_maps(inputs):
    x = np.ascontiguousarray(inputs["L0_attn_x"])
    common = {
        "wq": np.ascontiguousarray(inputs["L0_attn_Wq"]),
        "wk": np.ascontiguousarray(inputs["L0_attn_Wk"]),
        "wv": np.ascontiguousarray(inputs["L0_attn_Wv"]),
        "wb": np.ascontiguousarray(inputs["L0_attn_Wb"]),
        "wo": np.ascontiguousarray(inputs["L0_attn_Wo"]),
        "cq": np.ascontiguousarray(inputs["L0_attn_conv_q"]),
        "ck": np.ascontiguousarray(inputs["L0_attn_conv_k"]),
        "cv": np.ascontiguousarray(inputs["L0_attn_conv_v"]),
        "rmsw": np.ascontiguousarray(
            np.asarray(inputs["L0_attn_rms_w"]).reshape(HD, 1)),
        "gw": np.ascontiguousarray(inputs["L0_ffn_gate_weight"]),
        "gup": np.ascontiguousarray(
            np.asarray(inputs["L0_ffn_gate_up_proj"]).reshape(NE * 2 * FI, H)),
        "dwn": np.ascontiguousarray(
            np.asarray(inputs["L0_ffn_down_proj"]).reshape(NE * H, FI)),
        "mg": np.ascontiguousarray(
            np.asarray(inputs["L0_ffn_merge_groups"]).reshape(NE, 1).astype(np.int32)),
        "dom": np.ascontiguousarray(
            np.asarray(inputs["L0_ffn_dominant_experts"]).reshape(NG, 1).astype(np.int32)),
    }
    in_maps = []
    for c in range(N_CORES):
        b, hf = divmod(c, 2)
        m = dict(common)
        m["x"] = np.ascontiguousarray(x[b, hf * T:(hf + 1) * T, :])
        m["flag"] = np.full((128, 1), float(hf), np.float32)
        m["sel8"] = SEL8
        in_maps.append(m)
    return in_maps


def kernel(**inputs):
    in_maps = _make_in_maps(inputs)
    if "nc" not in _NC_CACHE:
        _NC_CACHE["nc"] = _build_nc()
    nc = _NC_CACHE["nc"]
    res = run_bass_kernel_spmd(nc, in_maps, core_ids=list(range(N_CORES)))
    out = np.empty((4, 4096, H), np.float32)
    for c in range(N_CORES):
        b, hf = divmod(c, 2)
        out[b, hf * T:(hf + 1) * T, :] = res.results[c]["out"]
    return out


# version 04
# speedup vs baseline: 1.3560x; 1.1232x over previous
import numpy as np
from contextlib import ExitStack

import concourse.bass as bass
import concourse.mybir as mybir
from concourse.bass_utils import run_bass_kernel_spmd

F32 = mybir.dt.float32
I32 = mybir.dt.int32
AF = mybir.ActivationFunctionType
OP = mybir.AluOpType
AX = mybir.AxisListType

H = 2048
T = 2048
NHC = 16
NH, HD = 16, 128
C = 128
NCH = 16
TBLK = 512
NBLK = 4
NE, NG = 32, 8
FI = 768
NJ = 12
NI = 6
LN_EPS = 1e-5
L2_EPS = 1e-6
RMS_EPS = 1e-5
HD_SCALE = float(HD) ** -0.5
TS_A2 = 256
N_CORES = 8

_NC_CACHE = {}


class Sem:
    __slots__ = ("h", "n")

    def __init__(self, h):
        self.h = h
        self.n = 0

    def inc(self, instr, k=1):
        instr.then_inc(self.h, k)
        self.n += k
        return self.n


class DmaStream:

    def __init__(self, mk, name):
        self.sems = [Sem(mk.es.enter_context(mk.nc.semaphore(f"{name}_a"))),
                     Sem(mk.es.enter_context(mk.nc.semaphore(f"{name}_b")))]
        mk.all_sems.append(self.sems[0])
        mk.all_sems.append(self.sems[1])
        self.n = 0
        self.vals = {}

    def begin_item(self):
        return self.sems[self.n % 2]

    def commit(self):
        slot = self.n % 2
        self.vals[self.n] = (self.sems[slot], self.sems[slot].n)
        self.n += 1
        return self.n - 1

    def mark(self, instr):
        self.begin_item().inc(instr, 16)
        return self.commit()

    def wait(self, eng, i):
        so, v = self.vals[i]
        eng.wait_ge(so.h, v)


class PsumPool:

    def __init__(self, mk, bank_ids):
        self.mk = mk
        self.banks = bank_ids
        self.u = 0
        self.done = {}

    def begin(self, ten):
        u = self.u
        prev = self.done.get(u - len(self.banks))
        if prev is not None:
            ten.wait_ge(prev[0].h, prev[1])
        return u, self.mk.pb[self.banks[u % len(self.banks)]]

    def took(self, u):
        self.u = u + 1

    def done_mark(self, sem, val):
        self.done[self.u - 1] = (sem, val)

    def done_mark_u(self, u, sem, val):
        self.done[u] = (sem, val)

    def bank(self, u):
        return self.mk.pb[self.banks[u % len(self.banks)]]


class MK:
    def __init__(self):
        nc = bass.Bass("TRN2", target_bir_lowering=False, debug=False)
        self.nc = nc
        self.es = ExitStack()
        es = self.es

        def din(name, shape, dt=F32):
            return nc.dram_tensor(name, shape, dt, kind="ExternalInput").ap()

        self.x0 = din("x", [T, H])
        self.wq = din("wq", [H, H])
        self.wk = din("wk", [H, H])
        self.wv = din("wv", [H, H])
        self.wb = din("wb", [NH, H])
        self.wo = din("wo", [H, H])
        self.cqw = din("cq", [H, 4])
        self.ckw = din("ck", [H, 4])
        self.cvw = din("cv", [H, 4])
        self.rmsw = din("rmsw", [HD, 1])
        self.gw = din("gw", [NE, H])
        self.gup = din("gup", [NE * 2 * FI, H])
        self.dwn = din("dwn", [NE * H, FI])
        self.mg = din("mg", [NE, 1], I32)
        self.dom = din("dom", [NG, 1], I32)
        self.flag = din("flag", [128, 1])
        self.sel8 = din("sel8", [NG, NG * 128])
        self.out = nc.dram_tensor("out", [T, H], F32, kind="ExternalOutput").ap()

        dr = lambda name, shape: nc.dram_tensor(name, shape, F32).ap()
        self.x1s = dr("x1s", [T, H])
        self.x2s = dr("x2s", [T, H])
        drb = lambda name, shape: nc.dram_tensor(name, shape, mybir.dt.bfloat16).ap()
        self.gut_h = drb("gut_h", [NG, NJ, 128, H])
        self.gut_l = drb("gut_l", [NG, NJ, 128, H])
        self.dnt_h = drb("dnt_h", [NG, NHC, 128, FI])
        self.dnt_l = drb("dnt_l", [NG, NHC, 128, FI])
        self.wot = dr("wot", [NHC, 128, H])
        self.qs = dr("qs_s", [NCH, NHC, 128, C])
        self.ks = dr("ks_s", [NCH, NHC, 128, C])
        self.vs = dr("vs_s", [NCH, NHC, 128, C])
        self.loc = dr("loc", [NCH, NH, 5, 128, 128])
        self.locr = dr("locr", [NCH, 128, NH])
        self.cc_s_in = dr("cc_s_in", [128, NH * HD])
        self.cc_s_out = dr("cc_s_out", [256, NH * HD])
        self.cc_x_in = dr("cc_x_in", [3, H])
        self.cc_x_out = dr("cc_x_out", [6, H])

        self.all_sems = []

        def sem(name):
            s = Sem(es.enter_context(nc.semaphore(name)))
            self.all_sems.append(s)
            return s
        self.qw = sem("qw")
        self.qg = sem("qg")
        self.qc = sem("qc")
        self.tm = sem("tm")
        self.vm = sem("vm")
        self.sm = sem("sm")
        self.gm = sem("gm")

        sb = lambda name, shape, dt=F32: es.enter_context(nc.sbuf_tensor(name, shape, dt))
        self.ident = sb("ident", [128, 128])
        self.stril = sb("stril", [128, 128])
        self.tril0 = sb("tril0", [128, 128])
        self.ones_row = sb("ones_row", [1, 128])
        self.cw = {p: sb(f"cw_{p}", [128, 4 * NHC]) for p in "qkv"}
        self.rmsw_sb = sb("rmsw_sb", [128, 1])
        self.flag_sb = sb("flag_sb", [128, 1])
        self.gwT_h = es.enter_context(nc.sbuf_tensor("gwT_h", [128, NE * NHC], mybir.dt.bfloat16))
        self.gwT_l = es.enter_context(nc.sbuf_tensor("gwT_l", [128, NE * NHC], mybir.dt.bfloat16))
        self.mgrp = sb("mgrp", [NE, NG])
        self.mg_sb = sb("mg_sb", [NE, 1], I32)
        self.mgf = sb("mgf", [NE, 1])
        self.dom_sb = sb("dom_sb", [NG, 1], I32)
        self.iota8 = sb("iota8", [NE, NG])
        self._sel8_sb = sb("sel8_sb", [NG, NG * 128])
        self.beta_sb = sb("beta_sb", [NH, T])
        self.SA = sb("SA", [128, NH * HD])
        self.SB = sb("SB", [128, NH * HD])
        self.tril0w = sb("tril0w", [128, 512])
        self.trash = sb("trash", [128, H])
        self.ln1 = sb("ln_s1", [128, 1])
        self.ln2 = sb("ln_s2", [128, 1])

        self.pb = [es.enter_context(nc.psum_tensor(f"pb{i}", [128, 512], F32))
                   for i in range(8)]
        self.block = es.enter_context(nc.Block())

    def t(self, f): self.block.tensor(f)

    def v(self, f):
        def w(vec):
            if self.vm.n:
                vec.wait_ge(self.vm.h, self.vm.n)
            f(vec)
        self.block.vector(w)

    def s(self, f):
        def w(sc):
            if self.sm.n:
                sc.wait_ge(self.sm.h, self.sm.n)
            f(sc)
        self.block.scalar(w)

    def g(self, f):
        def w(gp):
            if self.gm.n:
                gp.wait_ge(self.gm.h, self.gm.n)
            f(gp)
        self.block.gpsimd(w)

    def y(self, f): self.block.sync(f)

    def emit_ln(self, xa, slot, stream, item):
        def vln(vec):
            stream.wait(vec, item)
            xt = xa[:, slot * H:(slot + 1) * H]
            self.dve_sync(vec, vec.reduce_sum(self.ln1[:, :], xt, axis=AX.X))
            self.dve_sync(vec, vec.tensor_scalar(self.ln1[:, :], self.ln1[:, :],
                                                 1.0 / H, None, op0=OP.mult))
            self.vm.inc(vec.tensor_scalar(xt, xt, self.ln1[:, 0:1], None,
                                          op0=OP.subtract))
        self.v(vln)
        vmv = self.vm.n

        def sln(sc, vmv=vmv):
            sc.wait_ge(self.vm.h, vmv)
            self.sm.inc(sc.activation(self.trash[:, :], xa[:, slot * H:(slot + 1) * H],
                                      AF.Square, accum_out=self.ln2[:, 0:1]))
        self.s(sln)
        smv = self.sm.n

        def vv(vec, smv=smv):
            vec.wait_ge(self.sm.h, smv)
            self.dve_sync(vec, vec.tensor_scalar(self.ln2[:, :], self.ln2[:, :],
                                                 1.0 / H, None, op0=OP.mult))
            self.vm.inc(vec.tensor_scalar(self.ln2[:, :], self.ln2[:, :],
                                          LN_EPS, None, op0=OP.add))
        self.v(vv)
        vmv2 = self.vm.n

        def ss(sc, vmv2=vmv2):
            sc.wait_ge(self.vm.h, vmv2)
            self.sm.inc(sc.activation(self.ln2[:, :], self.ln2[:, :], AF.Sqrt))
        self.s(ss)
        smv2 = self.sm.n

        def vv2(vec, smv2=smv2):
            sc_ = None
            vec.wait_ge(self.sm.h, smv2)
            self.dve_sync(vec, vec.reciprocal(self.ln2[:, :], self.ln2[:, :]))
            xt = xa[:, slot * H:(slot + 1) * H]
            self.vm.inc(vec.tensor_scalar(xt, xt, self.ln2[:, 0:1], None, op0=OP.mult))
        self.v(vv2)

    def dve_sync(self, vec, instr):
        val = self.vm.inc(instr)
        vec.wait_ge(self.vm.h, val)

    def tr(self, eng, out_ps, in_ap):
        p = in_ap.shape[0]
        bp = in_ap.base_partition()
        return eng.transpose(out_ps, in_ap, self.ident[bp:bp + p, bp:bp + p])

    def phase_barrier(self):
        marks = [(s, s.n) for s in self.all_sems]

        def w(eng):
            for s, v in marks:
                if v > 0:
                    eng.wait_ge(s.h, v)
        self.t(w)
        self.v(w)
        self.s(w)
        self.g(w)
        self.y(w)

    def p0(self):
        nc = self.nc

        def g0(gp):
            gp.memset(self.ident[:, :], 0.0).then_inc(self.gm.h, 1)
            self.gm.n += 1
            gp.memset(self.stril[:, :], 1.0).then_inc(self.gm.h, 1)
            self.gm.n += 1
            gp.memset(self.tril0[:, :], 1.0).then_inc(self.gm.h, 1)
            self.gm.n += 1
            gp.wait_ge(self.gm.h, self.gm.n)
            gp.affine_select(out=self.ident[:, :], in_=self.ident[:, :],
                             compare_op=OP.not_equal, fill=1.0, base=0,
                             pattern=[[-1, 128]], channel_multiplier=1)
            gp.affine_select(out=self.stril[:, :], in_=self.stril[:, :],
                             compare_op=OP.is_gt, fill=0.0, base=0,
                             pattern=[[-1, 128]], channel_multiplier=1)
            gp.affine_select(out=self.tril0[:, :], in_=self.tril0[:, :],
                             compare_op=OP.is_ge, fill=0.0, base=0,
                             pattern=[[-1, 128]], channel_multiplier=1
                             ).then_inc(self.gm.h, 1)
            self.gm.n += 1
            gp.wait_ge(self.gm.h, self.gm.n)
            for _j in range(4):
                gp.tensor_copy(self.tril0w[:, _j * 128:(_j + 1) * 128], self.tril0[:, :])
            gp.memset(self.ones_row[:, :], 1.0)
            gp.iota(self.iota8[:, :], pattern=[[1, NG]], base=0,
                    channel_multiplier=0, allow_small_or_imprecise_dtypes=True)
            gp.memset(self.SA[:, :], 0.0)
            gp.dma_start(out=self.rmsw_sb[:HD, :], in_=self.rmsw).then_inc(self.qg.h, 16)
            gp.dma_start(out=self.flag_sb[:, :], in_=self.flag).then_inc(self.qg.h, 16)
            gp.dma_start(out=self.mg_sb[:, :], in_=self.mg).then_inc(self.qg.h, 16)
            gp.dma_start(out=self.dom_sb[:, :], in_=self.dom).then_inc(self.qg.h, 16)
            gp.dma_start(out=self._sel8_sb[:, :], in_=self.sel8).then_inc(self.qg.h, 16)
            self.qg.n += 80
            gp.wait_ge(self.qg.h, self.qg.n)
            self.gm.inc(gp.nop())
        self.g(g0)
        gm0 = self.gm.n

        def v0(vec):
            vec.wait_ge(self.gm.h, gm0)
            cp = vec.tensor_copy(self.mgf[:, :], self.mg_sb[:, :])
            v = self.vm.inc(cp)
            vec.wait_ge(self.vm.h, v)
            self.vm.inc(vec.tensor_tensor(self.mgrp[:, :], self.iota8[:, :],
                                          self.mgf[:, 0:1].to_broadcast([NE, NG]),
                                          op=OP.is_equal))
        self.v(v0)

        p0es = ExitStack()
        gw_raw = p0es.enter_context(nc.sbuf_tensor("gw_raw", [NE, H], F32))

        def y0(sync):
            for p, src in (("q", self.cqw), ("k", self.ckw), ("v", self.cvw)):
                for oc in range(NHC):
                    sync.dma_start(out=self.cw[p][:, oc * 4:(oc + 1) * 4],
                                   in_=src[oc * 128:(oc + 1) * 128, :]).then_inc(self.qw.h, 16)
                    self.qw.n += 16
            sync.dma_start(out=gw_raw[:, :], in_=self.gw).then_inc(self.qw.h, 16)
            self.qw.n += 16
            sync.wait_ge(self.qw.h, self.qw.n)
        self.y(y0)

        qw0, gm1 = self.qw.n, self.gm.n
        prev_vm = [None]
        for k in range(NHC):
            def t0(ten, k=k):
                if k == 0:
                    ten.wait_ge(self.qw.h, qw0)
                    ten.wait_ge(self.gm.h, gm1)
                else:
                    ten.wait_ge(self.vm.h, prev_vm[0])
                self.tm.inc(self.tr(ten, self.pb[6][:, 0:NE], gw_raw[:, k * 128:(k + 1) * 128]))
            self.t(t0)
            tmv = self.tm.n

            def v1(vec, k=k, tmv=tmv):
                vec.wait_ge(self.tm.h, tmv)
                sl = slice(k * NE, (k + 1) * NE)
                self.dve_sync(vec, vec.tensor_copy(self.gwT_h[:, sl], self.pb[6][:, 0:NE]))
                self.dve_sync(vec, vec.tensor_tensor(self.trash[:, 0:NE], self.pb[6][:, 0:NE],
                                                     self.gwT_h[:, sl], op=OP.subtract))
                cp = vec.tensor_copy(self.gwT_l[:, sl], self.trash[:, 0:NE])
                prev_vm[0] = self.vm.inc(cp)
            self.v(v1)

        BF16 = mybir.dt.bfloat16
        raw_gu = p0es.enter_context(nc.sbuf_tensor("p0_rawgu", [128, 2 * H], F32))
        stg_gu = p0es.enter_context(nc.sbuf_tensor("p0_stggu", [128, 2 * H], F32))
        raw_dn = p0es.enter_context(nc.sbuf_tensor("p0_rawdn", [128, 2 * FI], F32))
        stg_dn = p0es.enter_context(nc.sbuf_tensor("p0_stgdn", [128, 2 * FI], F32))
        hst_gu = p0es.enter_context(nc.sbuf_tensor("p0_hgu", [128, 2 * H], BF16))
        lst_gu = p0es.enter_context(nc.sbuf_tensor("p0_lgu", [128, 2 * H], BF16))
        hst_dn = p0es.enter_context(nc.sbuf_tensor("p0_hdn", [128, 2 * FI], BF16))
        lst_dn = p0es.enter_context(nc.sbuf_tensor("p0_ldn", [128, 2 * FI], BF16))

        slabs = ([("gu", gi, j) for gi in range(NG) for j in range(NJ)]
                 + [("dn", gi, f) for gi in range(NG) for f in range(NHC)]
                 + [("wo", osl, 0) for osl in range(NHC)])
        geom = {
            "gu": dict(raw=raw_gu, stg=stg_gu, hst=hst_gu, lst=lst_gu, w=H, nt=16, buf="A"),
            "dn": dict(raw=raw_dn, stg=stg_dn, hst=hst_dn, lst=lst_dn, w=FI, nt=6, buf="B"),
            "wo": dict(raw=raw_gu, stg=stg_gu, hst=None, lst=None, w=H, nt=16, buf="A"),
        }
        buf_use = {"A": 0, "B": 0}
        slab_bufuse = {}
        bufuse_to_slab = {}
        for n, (kind, a, b) in enumerate(slabs):
            bk = geom[kind]["buf"]
            u = buf_use[bk]
            buf_use[bk] = u + 1
            slab_bufuse[n] = u
            bufuse_to_slab[(bk, u)] = n
        tm_base = self.tm.n
        qg_base = self.qg.n
        load_v = {n: qg_base + 16 * (n + 1) for n in range(len(slabs))}
        done_marks = {}

        gug = self.nc.dram_tensor("gug", [NG * 2 * FI, H], F32).ap()
        dng = self.nc.dram_tensor("dng", [NG * H, FI], F32).ap()

        def ggather(gp):
            with gp.register("eo_p0") as eo:
                for gi in range(NG):
                    gp.reg_load(eo, self.dom_sb[gi:gi + 1, 0:1])
                    gp.reg_mul(eo, eo, 2 * FI * H)
                    gp.dma_start(out=gug[gi * 2 * FI:(gi + 1) * 2 * FI, :],
                                 in_=bass.AP(self.gup.tensor, eo,
                                             [[H, 2 * FI], [1, H]])
                                 ).then_inc(self.qg.h, 16)
                    self.qg.n += 16
                for gi in range(NG):
                    gp.reg_load(eo, self.dom_sb[gi:gi + 1, 0:1])
                    gp.reg_mul(eo, eo, H * FI)
                    gp.dma_start(out=dng[gi * H:(gi + 1) * H, :],
                                 in_=bass.AP(self.dwn.tensor, eo,
                                             [[FI, H], [1, FI]])
                                 ).then_inc(self.qg.h, 16)
                    self.qg.n += 16
        gath_base = self.qg.n
        self.g(ggather)
        assert self.qg.n == gath_base + 16 * 16
        gath_total = self.qg.n
        stA = DmaStream(self, "p0A")
        stB = DmaStream(self, "p0B")
        streams = {"A": stA, "B": stB}
        stores = {"A": DmaStream(self, "p0sA"), "B": DmaStream(self, "p0sB")}
        store_item = {}
        load_item = {}

        def gall(gp):
            gp.wait_ge(self.qg.h, gath_total)
            for n, (kind, a, b) in enumerate(slabs):
                gg = geom[kind]
                st = streams[gg["buf"]]
                u = slab_bufuse[n]
                slot = u % 2
                prev_slab = bufuse_to_slab.get((gg["buf"], u - 2))
                if prev_slab is not None:
                    gp.wait_ge(self.tm.h, tm_base + prev_slab + 1)
                dst = gg["raw"][:, slot * gg["w"]:(slot + 1) * gg["w"]]
                if kind == "gu":
                    r0 = a * 2 * FI + b * 128
                    dd = gp.dma_start(out=dst, in_=gug[r0:r0 + 128, :])
                elif kind == "dn":
                    r0 = a * H + b * 128
                    dd = gp.dma_start(out=dst, in_=dng[r0:r0 + 128, :])
                else:
                    dd = gp.dma_start(out=dst, in_=self.wo[a * 128:(a + 1) * 128, :])
                load_item[n] = (st, st.mark(dd))
        self.g(gall)

        for n in range(len(slabs)):
            kind, a, b = slabs[n]
            gg = geom[kind]
            u = slab_bufuse[n]
            slot = u % 2
            nt = gg["nt"]
            ngrp = (nt + 3) // 4

            def tt(ten, n=n, gg=gg, slot=slot, nt=nt, ngrp=ngrp, u=u):
                st_, it_ = load_item[n]
                st_.wait(ten, it_)
                prev = done_marks.get(n - 1)
                if prev is not None:
                    for so, val in prev:
                        ten.wait_ge(so.h, val)
                src = gg["raw"][:, slot * gg["w"]:(slot + 1) * gg["w"]]
                for grp in range(ngrp):
                    bank = self.pb[grp]
                    for q in range(min(4, nt - grp * 4)):
                        k = grp * 4 + q
                        mm = self.tr(ten, bank[:, q * 128:(q + 1) * 128],
                                     src[:, k * 128:(k + 1) * 128])
                self.tm.inc(mm)
                assert self.tm.n == tm_base + n + 1
            self.t(tt)
            tmv = self.tm.n
            marks = []

            prev_store = store_item.get((gg["buf"], u - 2))

            def emit_v(vec, kind=kind, gg=gg, slot=slot, nt=nt, ngrp=ngrp, tmv=tmv,
                       prev_store=prev_store):
                vec.wait_ge(self.tm.h, tmv)
                if prev_store is not None:
                    vec.wait_ge(prev_store[0].h, prev_store[1])
                cp = None
                for grp in range(ngrp):
                    bank = self.pb[grp]
                    w = min(4, nt - grp * 4) * 128
                    o0 = slot * gg["w"] + grp * 512
                    if kind == "wo":
                        cp = vec.tensor_scalar(gg["stg"][:, o0:o0 + w], bank[:, :w],
                                               self.rmsw_sb[:, 0:1], None, op0=OP.mult)
                    else:
                        self.dve_sync(vec, vec.tensor_copy(gg["hst"][:, o0:o0 + w],
                                                           bank[:, :w]))
                        cp = vec.tensor_tensor(gg["stg"][:, o0:o0 + w], bank[:, :w],
                                               gg["hst"][:, o0:o0 + w], op=OP.subtract)
                marks.append((self.vm, self.vm.inc(cp)))

            def emit_s(sc, kind=kind, gg=gg, slot=slot, nt=nt, ngrp=ngrp,
                       vmark=None):
                sc.wait_ge(vmark[0].h, vmark[1])
                cp = None
                for grp in range(ngrp):
                    w = min(4, nt - grp * 4) * 128
                    o0 = slot * gg["w"] + grp * 512
                    cp = sc.copy(gg["lst"][:, o0:o0 + w], gg["stg"][:, o0:o0 + w])
                marks.append((self.sm, self.sm.inc(cp)))
            self.v(emit_v)
            if kind != "wo":
                vmark = marks[-1]
                self.s(lambda sc: emit_s(sc, vmark=vmark))
            done_marks[n] = list(marks)

            def yw(sync, kind=kind, a=a, b=b, gg=gg, slot=slot, marks=tuple(marks), u=u):
                for so, val in marks:
                    sync.wait_ge(so.h, val)
                sl = slice(slot * gg["w"], (slot + 1) * gg["w"])
                ssem = stores[gg["buf"]].begin_item()
                if kind == "wo":
                    ssem.inc(sync.dma_start(out=self.wot[a], in_=gg["stg"][:, sl]), 16)
                else:
                    dst_h = self.gut_h[a, b] if kind == "gu" else self.dnt_h[a, b]
                    dst_l = self.gut_l[a, b] if kind == "gu" else self.dnt_l[a, b]
                    ssem.inc(sync.dma_start(out=dst_h, in_=gg["hst"][:, sl]), 16)
                    ssem.inc(sync.dma_start(out=dst_l, in_=gg["lst"][:, sl]), 16)
                it = stores[gg["buf"]].commit()
                store_item[(gg["buf"], u)] = stores[gg["buf"]].vals[it]
            self.y(yw)
        p0es.close()

    def ffn(self, x_src, x_dst, tag):
        nc = self.nc
        es = ExitStack()
        BF16 = mybir.dt.bfloat16
        xlnT_h = es.enter_context(nc.sbuf_tensor(f"xlnTh_{tag}", [128, NHC * TBLK], BF16))
        xlnT_l = es.enter_context(nc.sbuf_tensor(f"xlnTl_{tag}", [128, NHC * TBLK], BF16))
        oacc = es.enter_context(nc.sbuf_tensor(f"oacc_{tag}", [128, NHC * TBLK], F32))
        act_h = es.enter_context(nc.sbuf_tensor(f"acth_{tag}", [128, NI * TBLK], BF16))
        act_l = es.enter_context(nc.sbuf_tensor(f"actl_{tag}", [128, NI * TBLK], BF16))
        tmpf = es.enter_context(nc.sbuf_tensor(f"tmpf_{tag}", [128, TBLK], F32))
        s_t = es.enter_context(nc.sbuf_tensor(f"s_t_{tag}", [128, TBLK], F32))
        wbc = es.enter_context(nc.sbuf_tensor(f"wbc_{tag}", [128, TBLK], F32))
        xa = es.enter_context(nc.sbuf_tensor(f"xa_{tag}", [128, 2 * H], F32))
        xo = es.enter_context(nc.sbuf_tensor(f"xo_{tag}", [128, 2 * H], F32))
        gub_h = es.enter_context(nc.sbuf_tensor(f"gubh_{tag}", [128, 2 * H], BF16))
        gub_l = es.enter_context(nc.sbuf_tensor(f"gubl_{tag}", [128, 2 * H], BF16))
        dnb_h = es.enter_context(nc.sbuf_tensor(f"dnbh_{tag}", [128, 2 * FI], BF16))
        dnb_l = es.enter_context(nc.sbuf_tensor(f"dnbl_{tag}", [128, 2 * FI], BF16))
        wrows = es.enter_context(nc.sbuf_tensor(f"wrows_{tag}", [NG, TBLK], F32))
        gsc = es.enter_context(nc.sbuf_tensor(f"gsc_{tag}", [128, 5 * NE], F32))
        gs1 = es.enter_context(nc.sbuf_tensor(f"gs1_{tag}", [128, 4], F32))
        m8 = es.enter_context(nc.sbuf_tensor(f"m8_{tag}", [128, 8], F32))
        lgf = es.enter_context(nc.sbuf_tensor(f"lgf_{tag}", [NE, 128], F32))
        wtg = es.enter_context(nc.sbuf_tensor(f"wtg_{tag}", [128, NG], F32))

        seq = []
        for tb in range(NBLK):
            for gi in range(NG):
                for jp in range(NI):
                    seq.append(("gu", gi, jp))
                    seq.append(("gu", gi, jp + NI))
                for f in range(NHC):
                    seq.append(("dn", gi, f))

        kctr = {"gu": 0, "dn": 0}
        kuse = {"gu": 0, "dn": 0}
        cons_v = {}
        st_slab = {"gu": DmaStream(self, f"f{tag}_gu"), "dn": DmaStream(self, f"f{tag}_dn")}

        def emit_slab_load(idx):
            kind, gi, j = seq[idx]
            ki = kctr[kind]
            kctr[kind] = ki + 1
            slot = ki % 2
            cred = cons_v.get((kind, ki - 2))
            bh, bl, w = ((gub_h, gub_l, H) if kind == "gu" else (dnb_h, dnb_l, FI))
            src_h = self.gut_h[gi, j] if kind == "gu" else self.dnt_h[gi, j]
            src_l = self.gut_l[gi, j] if kind == "gu" else self.dnt_l[gi, j]

            def yl(sync, bh=bh, bl=bl, w=w, src_h=src_h, src_l=src_l, slot=slot,
                   cred=cred, kind=kind, ki=ki):
                if cred is not None:
                    sync.wait_ge(self.tm.h, cred)
                ssem = st_slab[kind].begin_item()
                ssem.inc(sync.dma_start(out=bh[:, slot * w:(slot + 1) * w], in_=src_h), 16)
                ssem.inc(sync.dma_start(out=bl[:, slot * w:(slot + 1) * w], in_=src_l), 16)
                it = st_slab[kind].commit()
                assert it == ki
            self.y(yl)

        pbu = {"n": 0}
        evac_done = {}

        def bank_begin(ten):
            u = pbu["n"]
            prev = evac_done.get(u - 2)
            if prev is not None:
                ten.wait_ge(prev[0].h, prev[1])
            return u, self.pb[u % 2]

        def bank_end(u):
            pbu["n"] = u + 1

        tpbu = {"n": 0}
        tevac_done = {}

        def tbank_begin(ten):
            u = tpbu["n"]
            prev = tevac_done.get(u - 4)
            if prev is not None:
                ten.wait_ge(prev[0].h, prev[1])
            return u, self.pb[2 + (u % 4)]

        def tbank_end(u):
            tpbu["n"] = u + 1

        gpbu = {"n": 0}
        gevac_done = {}

        def gbank_begin(ten):
            u = gpbu["n"]
            prev = gevac_done.get(u - 1)
            if prev is not None:
                ten.wait_ge(prev[0].h, prev[1])
            return u, self.pb[6]

        def gbank_end(u):
            gpbu["n"] = u + 1

        def gmark(val):
            gevac_done[gpbu["n"] - 1] = (self.vm, val)

        xa_credit = {}
        xo_credit = {}
        st_xa = DmaStream(self, f"f{tag}_xa")
        st_xo = DmaStream(self, f"f{tag}_xo")
        st_xs = DmaStream(self, f"f{tag}_xs")
        last_ep_tm = [None]
        silu_done = [0]
        up_done = [0]
        seq_pos = [0]
        emit_slab_load(0)
        emit_slab_load(1)

        for tb in range(NBLK):
            for tc in range(4):
                slot = tc % 2
                cred = xa_credit.get(tb * 4 + tc - 2)

                def yx(sync, tb=tb, tc=tc, slot=slot, cred=cred):
                    if cred is not None:
                        sync.wait_ge(cred[0].h, cred[1])
                    r0 = tb * TBLK + tc * 128
                    dd = sync.dma_start(out=xa[:, slot * H:(slot + 1) * H],
                                        in_=x_src[r0:r0 + 128, :])
                    st_xa.mark(dd)
                self.y(yx)
                xa_item = st_xa.n - 1

                def vln(vec, slot=slot, xa_item=xa_item):
                    st_xa.wait(vec, xa_item)
                    xt = xa[:, slot * H:(slot + 1) * H]
                    self.dve_sync(vec, vec.reduce_sum(self.ln1[:, :], xt, axis=AX.X))
                    self.dve_sync(vec, vec.tensor_scalar(self.ln1[:, :], self.ln1[:, :],
                                                         1.0 / H, None, op0=OP.mult))
                    self.vm.inc(vec.tensor_scalar(xt, xt, self.ln1[:, 0:1], None,
                                                  op0=OP.subtract))
                self.v(vln)
                vmv = self.vm.n

                def sln(sc, slot=slot, vmv=vmv):
                    sc.wait_ge(self.vm.h, vmv)
                    self.sm.inc(sc.activation(self.trash[:, :], xa[:, slot * H:(slot + 1) * H],
                                              AF.Square, accum_out=self.ln2[:, 0:1]))
                self.s(sln)
                smv = self.sm.n

                def vv(vec, smv=smv):
                    vec.wait_ge(self.sm.h, smv)
                    self.dve_sync(vec, vec.tensor_scalar(self.ln2[:, :], self.ln2[:, :],
                                                         1.0 / H, None, op0=OP.mult))
                    self.vm.inc(vec.tensor_scalar(self.ln2[:, :], self.ln2[:, :],
                                                  LN_EPS, None, op0=OP.add))
                self.v(vv)
                vmv = self.vm.n

                def ss(sc, vmv=vmv):
                    sc.wait_ge(self.vm.h, vmv)
                    self.sm.inc(sc.activation(self.ln2[:, :], self.ln2[:, :], AF.Sqrt))
                self.s(ss)
                smv = self.sm.n

                def vv2(vec, slot=slot, smv=smv):
                    vec.wait_ge(self.sm.h, smv)
                    self.dve_sync(vec, vec.reciprocal(self.ln2[:, :], self.ln2[:, :]))
                    xt = xa[:, slot * H:(slot + 1) * H]
                    self.vm.inc(vec.tensor_scalar(xt, xt, self.ln2[:, 0:1], None,
                                                  op0=OP.mult))
                self.v(vv2)
                v_ln = self.vm.n

                tgrp = []
                for grp in range(4):
                    def tx(ten, grp=grp, slot=slot, v_ln=v_ln):
                        u, bank = tbank_begin(ten)
                        if grp == 0:
                            ten.wait_ge(self.vm.h, v_ln)
                        for q in range(4):
                            k = grp * 4 + q
                            mm = self.tr(ten, bank[:, q * 128:(q + 1) * 128],
                                         xa[:, slot * H + k * 128:slot * H + (k + 1) * 128])
                        self.tm.inc(mm)
                        tgrp.append((u, bank, self.tm.n))
                        tbank_end(u)
                    self.t(tx)
                xa_credit[tb * 4 + tc] = (self.tm, tgrp[3][2])

                ev_marks = []
                for grp in range(4):
                    u, bank, tmv = tgrp[grp]

                    def ev(vec, grp=grp, bank=bank, tmv=tmv, tc=tc, u=u):
                        vec.wait_ge(self.tm.h, tmv)
                        for q in range(4):
                            k = grp * 4 + q
                            o0 = k * TBLK + tc * 128
                            bsl = bank[:, q * 128:(q + 1) * 128]
                            self.dve_sync(vec, vec.tensor_copy(
                                xlnT_h[:, o0:o0 + 128], bsl))
                            self.dve_sync(vec, vec.tensor_tensor(
                                self.trash[:, q * 128:(q + 1) * 128], bsl,
                                xlnT_h[:, o0:o0 + 128], op=OP.subtract))
                            cp = vec.tensor_copy(xlnT_l[:, o0:o0 + 128],
                                                 self.trash[:, q * 128:(q + 1) * 128])
                        val = self.vm.inc(cp)
                        ev_marks.append((self.vm, val))
                        tevac_done[u] = (self.vm, val)
                    self.v(ev)

                def tlg(ten, tc=tc, ev_marks=tuple(ev_marks)):
                    u, bank = gbank_begin(ten)
                    for so, val in ev_marks:
                        ten.wait_ge(so.h, val)
                    for k in range(NHC):
                        wsl = slice(k * NE, (k + 1) * NE)
                        xsl = slice(k * TBLK + tc * 128, k * TBLK + tc * 128 + 128)
                        ten.matmul(bank[:NE, 0:128], self.gwT_h[:, wsl], xlnT_h[:, xsl],
                                   start=(k == 0), stop=False)
                        ten.matmul(bank[:NE, 0:128], self.gwT_h[:, wsl], xlnT_l[:, xsl],
                                   start=False, stop=False)
                        mm = ten.matmul(bank[:NE, 0:128], self.gwT_l[:, wsl], xlnT_h[:, xsl],
                                        start=False, stop=(k == NHC - 1))
                    self.tm.inc(mm)
                    gbank_end(u)
                self.t(tlg)
                tmv = self.tm.n

                def vlg(vec, tmv=tmv):
                    vec.wait_ge(self.tm.h, tmv)
                    gmark(self.vm.inc(vec.tensor_copy(lgf[:, :], self.pb[6][:NE, 0:128])))
                self.v(vlg)
                vmv = self.vm.n

                def tlg2(ten, vmv=vmv):
                    u, bank = gbank_begin(ten)
                    ten.wait_ge(self.vm.h, vmv)
                    self.tm.inc(self.tr(ten, bank[:, 128:128 + NE], lgf[:, :]))
                    gbank_end(u)
                self.t(tlg2)
                tmv = self.tm.n

                def vsm(vec, tmv=tmv):
                    vec.wait_ge(self.tm.h, tmv)
                    cp = vec.tensor_copy(gsc[:, 0:NE], self.pb[6][:, 128:128 + NE])
                    gmark(self.vm.inc(cp))
                    vec.wait_ge(self.vm.h, self.vm.n)
                    self.dve_sync(vec, vec.reduce_max(gs1[:, 0:1], gsc[:, 0:NE], axis=AX.X))
                    self.vm.inc(vec.tensor_scalar(gsc[:, 0:NE], gsc[:, 0:NE], gs1[:, 0:1],
                                                  None, op0=OP.subtract))
                self.v(vsm)
                vmv = self.vm.n

                def sexp(sc, vmv=vmv):
                    sc.wait_ge(self.vm.h, vmv)
                    self.sm.inc(sc.activation(gsc[:, NE:2 * NE], gsc[:, 0:NE], AF.Exp))
                self.s(sexp)
                smv = self.sm.n

                def vtop(vec, smv=smv):
                    vec.wait_ge(self.sm.h, smv)
                    pex = gsc[:, NE:2 * NE]
                    probs = gsc[:, 2 * NE:3 * NE]
                    self.dve_sync(vec, vec.reduce_sum(gs1[:, 1:2], pex, axis=AX.X))
                    self.dve_sync(vec, vec.reciprocal(gs1[:, 1:2], gs1[:, 1:2]))
                    self.dve_sync(vec, vec.tensor_scalar(probs, pex, gs1[:, 1:2], None,
                                                         op0=OP.mult))
                    mx8 = vec.max(out=m8[:, :], in_=probs)
                    v = self.vm.inc(mx8)
                    for _ in range(8):
                        vec.engine_nop()
                    vec.wait_ge(self.vm.h, v)
                    self.dve_sync(vec, vec.reduce_sum(gs1[:, 2:3], m8[:, :], axis=AX.X))
                    self.dve_sync(vec, vec.reciprocal(gs1[:, 2:3], gs1[:, 2:3]))
                    mask = gsc[:, 3 * NE:4 * NE]
                    self.dve_sync(vec, vec.tensor_tensor(
                        mask, probs, m8[:, 7:8].to_broadcast([128, NE]), op=OP.is_ge))
                    pm = gsc[:, 4 * NE:5 * NE]
                    self.dve_sync(vec, vec.tensor_tensor(pm, probs, mask, op=OP.mult))
                    self.vm.inc(vec.tensor_scalar(pm, pm, gs1[:, 2:3], None, op0=OP.mult))
                self.v(vtop)
                vmv = self.vm.n

                def tpm(ten, vmv=vmv):
                    u, bank = gbank_begin(ten)
                    ten.wait_ge(self.vm.h, vmv)
                    self.tm.inc(self.tr(ten, bank[:NE, 0:128], gsc[:, 4 * NE:5 * NE]))
                    gbank_end(u)
                self.t(tpm)
                tmv = self.tm.n

                def vpm(vec, tmv=tmv):
                    vec.wait_ge(self.tm.h, tmv)
                    gmark(self.vm.inc(vec.tensor_copy(lgf[:, :], self.pb[6][:NE, 0:128])))
                self.v(vpm)
                vmv = self.vm.n

                def twtg(ten, vmv=vmv):
                    u, bank = gbank_begin(ten)
                    ten.wait_ge(self.vm.h, vmv)
                    mm = ten.matmul(bank[:, 128:128 + NG], lgf[:, :], self.mgrp[:, :],
                                    start=True, stop=True)
                    self.tm.inc(mm)
                    gbank_end(u)
                self.t(twtg)
                tmv = self.tm.n

                def vwtg(vec, tmv=tmv):
                    vec.wait_ge(self.tm.h, tmv)
                    gmark(self.vm.inc(vec.tensor_copy(wtg[:, :], self.pb[6][:, 128:128 + NG])))
                self.v(vwtg)
                vmv = self.vm.n

                def twr(ten, vmv=vmv):
                    u, bank = gbank_begin(ten)
                    ten.wait_ge(self.vm.h, vmv)
                    self.tm.inc(self.tr(ten, bank[:NG, 0:128], wtg[:, :]))
                    gbank_end(u)
                self.t(twr)
                tmv = self.tm.n

                def vwr(vec, tmv=tmv, tc=tc):
                    vec.wait_ge(self.tm.h, tmv)
                    cp = vec.tensor_copy(wrows[:, tc * 128:(tc + 1) * 128],
                                         self.pb[6][:NG, 0:128])
                    gmark(self.vm.inc(cp))
                self.v(vwr)

            wrows_ready = self.vm.n

            for gi in range(NG):
                def twb(ten, gi=gi, wrows_ready=wrows_ready):
                    u, bank = gbank_begin(ten)
                    if gi == 0:
                        ten.wait_ge(self.vm.h, wrows_ready)
                    mm = ten.matmul(bank[:, 0:TBLK],
                                    self.sel8[gi] if False else self._sel8_sb[:, gi * 128:(gi + 1) * 128],
                                    wrows[0:NG, :], start=True, stop=True)
                    self.tm.inc(mm)
                    gbank_end(u)
                self.t(twb)
                tmv = self.tm.n

                def vwb(vec, tmv=tmv, gi=gi):
                    vec.wait_ge(self.tm.h, tmv)
                    if gi > 0:
                        vec.wait_ge(self.vm.h, up_done[0])
                    gmark(self.vm.inc(vec.tensor_copy(wbc[:, :], self.pb[6][:, 0:TBLK])))
                self.v(vwb)
                wbc_ready = self.vm.n

                for jp in range(NI):
                    for half, j in ((0, jp), (1, jp + NI)):
                        ki = kuse["gu"]
                        kuse["gu"] = ki + 1
                        slot = ki % 2

                        def tgu(ten, slot=slot, ki=ki):
                            u, bank = bank_begin(ten)
                            st_slab["gu"].wait(ten, ki)
                            for k in range(NHC):
                                wsl = slice(slot * H + k * 128, slot * H + (k + 1) * 128)
                                xsl = slice(k * TBLK, (k + 1) * TBLK)
                                ten.matmul(bank[:, 0:TBLK], gub_h[:, wsl], xlnT_h[:, xsl],
                                           start=(k == 0), stop=False)
                                ten.matmul(bank[:, 0:TBLK], gub_h[:, wsl], xlnT_l[:, xsl],
                                           start=False, stop=False)
                                mm = ten.matmul(bank[:, 0:TBLK], gub_l[:, wsl], xlnT_h[:, xsl],
                                                start=False, stop=(k == NHC - 1))
                            self.tm.inc(mm)
                            cons_v[("gu", ki)] = self.tm.n
                            bank_end(u)
                        self.t(tgu)
                        u_used = pbu["n"] - 1
                        tmv = self.tm.n
                        seq_pos[0] += 1
                        if seq_pos[0] + 1 < len(seq):
                            emit_slab_load(seq_pos[0] + 1)

                        if half == 0:
                            def sgate(sc, u=u_used, tmv=tmv):
                                sc.wait_ge(self.tm.h, tmv)
                                if up_done[0]:
                                    sc.wait_ge(self.vm.h, up_done[0])
                                cp = sc.activation(s_t[:, :], self.pb[u % 2][:, 0:TBLK],
                                                   AF.Silu)
                                evac_done[u] = (self.sm, self.sm.inc(cp))
                            self.s(sgate)
                            silu_done[0] = self.sm.n
                        else:
                            def vup(vec, u=u_used, tmv=tmv, jp=jp,
                                    wbc_ready=wbc_ready, sd=silu_done[0]):
                                vec.wait_ge(self.tm.h, tmv)
                                vec.wait_ge(self.sm.h, sd)
                                if jp == 0:
                                    vec.wait_ge(self.vm.h, wbc_ready)
                                hsl = act_h[:, jp * TBLK:(jp + 1) * TBLK]
                                lsl = act_l[:, jp * TBLK:(jp + 1) * TBLK]
                                self.dve_sync(vec, vec.tensor_tensor(
                                    tmpf[:, :], self.pb[u % 2][:, 0:TBLK], s_t[:, :],
                                    op=OP.mult))
                                self.dve_sync(vec, vec.tensor_tensor(
                                    tmpf[:, :], tmpf[:, :], wbc[:, :], op=OP.mult))
                                self.dve_sync(vec, vec.tensor_copy(hsl, tmpf[:, :]))
                                self.dve_sync(vec, vec.tensor_tensor(
                                    tmpf[:, :], tmpf[:, :], hsl, op=OP.subtract))
                                cp = vec.tensor_copy(lsl, tmpf[:, :])
                                evac_done[u] = (self.vm, self.vm.inc(cp))
                            self.v(vup)
                            up_done[0] = self.vm.n

                act_ready = self.vm.n
                for f in range(NHC):
                    ki = kuse["dn"]
                    kuse["dn"] = ki + 1
                    slot = ki % 2

                    def tdn(ten, f=f, slot=slot, ki=ki, act_ready=act_ready):
                        u, bank = bank_begin(ten)
                        st_slab["dn"].wait(ten, ki)
                        if f == 0:
                            ten.wait_ge(self.vm.h, act_ready)
                        for i in range(NI):
                            wsl = slice(slot * FI + i * 128, slot * FI + (i + 1) * 128)
                            asl = slice(i * TBLK, (i + 1) * TBLK)
                            ten.matmul(bank[:, 0:TBLK], dnb_h[:, wsl], act_h[:, asl],
                                       start=(i == 0), stop=False)
                            ten.matmul(bank[:, 0:TBLK], dnb_h[:, wsl], act_l[:, asl],
                                       start=False, stop=False)
                            mm = ten.matmul(bank[:, 0:TBLK], dnb_l[:, wsl], act_h[:, asl],
                                            start=False, stop=(i == NI - 1))
                        self.tm.inc(mm)
                        cons_v[("dn", ki)] = self.tm.n
                        bank_end(u)
                    self.t(tdn)
                    u_used = pbu["n"] - 1
                    tmv = self.tm.n
                    seq_pos[0] += 1
                    if seq_pos[0] + 1 < len(seq):
                        emit_slab_load(seq_pos[0] + 1)

                    def vdn(vec, u=u_used, tmv=tmv, f=f, gi=gi, tb=tb):
                        vec.wait_ge(self.tm.h, tmv)
                        if gi == 0 and f == 0 and tb > 0 and last_ep_tm[0]:
                            vec.wait_ge(self.tm.h, last_ep_tm[0])
                        dst = oacc[:, f * TBLK:(f + 1) * TBLK]
                        if gi == 0:
                            cp = vec.tensor_copy(dst, self.pb[u % 2][:, 0:TBLK])
                        else:
                            cp = vec.tensor_tensor(dst, dst, self.pb[u % 2][:, 0:TBLK],
                                                   op=OP.add)
                        evac_done[u] = (self.vm, self.vm.inc(cp))
                    self.v(vdn)
            oacc_ready = self.vm.n

            for tc in range(4):
                slot = tc % 2
                cred = xo_credit.get(tb * 4 + tc - 2)

                def yx2(sync, tb=tb, tc=tc, slot=slot, cred=cred):
                    if cred is not None:
                        sync.wait_ge(cred[0].h, cred[1])
                    r0 = tb * TBLK + tc * 128
                    dd = sync.dma_start(out=xo[:, slot * H:(slot + 1) * H],
                                        in_=x_src[r0:r0 + 128, :])
                    st_xo.mark(dd)
                self.y(yx2)
                xo_item = st_xo.n - 1

                ep = []
                for grp in range(4):
                    def tep(ten, grp=grp, tc=tc, oacc_ready=oacc_ready):
                        u, bank = tbank_begin(ten)
                        if grp == 0 and tc == 0:
                            ten.wait_ge(self.vm.h, oacc_ready)
                        for q in range(4):
                            f = grp * 4 + q
                            mm = self.tr(ten, bank[:, q * 128:(q + 1) * 128],
                                         oacc[:, f * TBLK + tc * 128:f * TBLK + tc * 128 + 128])
                        self.tm.inc(mm)
                        ep.append((u, bank, self.tm.n))
                        tbank_end(u)
                    self.t(tep)
                last_ep_tm[0] = self.tm.n

                st_marks = []
                for grp in range(4):
                    u, bank, tmv = ep[grp]

                    def vev(vec, grp=grp, bank=bank, tmv=tmv, slot=slot, xo_item=xo_item, u=u):
                        vec.wait_ge(self.tm.h, tmv)
                        st_xo.wait(vec, xo_item)
                        cp = None
                        for q in range(4):
                            f = grp * 4 + q
                            dst = xo[:, slot * H + f * 128:slot * H + (f + 1) * 128]
                            cp = vec.scalar_tensor_tensor(
                                dst, bank[:, q * 128:(q + 1) * 128], 0.5, dst,
                                op0=OP.mult, op1=OP.add)
                        val = self.vm.inc(cp)
                        st_marks.append((self.vm, val))
                        tevac_done[u] = (self.vm, val)
                    self.v(vev)

                def yst(sync, tb=tb, tc=tc, slot=slot, st_marks=tuple(st_marks)):
                    for so, val in st_marks:
                        sync.wait_ge(so.h, val)
                    r0 = tb * TBLK + tc * 128
                    dd = sync.dma_start(out=x_dst[r0:r0 + 128, :],
                                        in_=xo[:, slot * H:(slot + 1) * H])
                    it = st_xs.mark(dd)
                    xo_credit[tb * 4 + tc] = st_xs.vals[it]
                self.y(yst)

        es.close()

    def attn_proj(self):
        nc = self.nc
        es = ExitStack()
        xlnT = es.enter_context(nc.sbuf_tensor("a1_xlnT", [128, NHC * T], F32))
        xa = es.enter_context(nc.sbuf_tensor("a1_xa", [128, H], F32))
        wraw = es.enter_context(nc.sbuf_tensor("a1_wraw", [128, H], F32))
        wT = es.enter_context(nc.sbuf_tensor("a1_wT", [128, H], F32))
        ext = es.enter_context(nc.sbuf_tensor("a1_ext", [128, 520], F32))
        qcA = es.enter_context(nc.sbuf_tensor("a1_qcA", [128, TBLK], F32))
        qcB = es.enter_context(nc.sbuf_tensor("a1_qcB", [128, TBLK], F32))
        qcv = es.enter_context(nc.sbuf_tensor("a1_qcv", [128, 2 * TBLK], F32))
        pr3 = es.enter_context(nc.sbuf_tensor("a1_pr3", [128, 3], F32))
        pr3i = es.enter_context(nc.sbuf_tensor("a1_pr3i", [128, 3], F32))
        x3row = self.trash[0:3, :]
        xln3T = es.enter_context(nc.sbuf_tensor("a1_xln3T", [128, 3 * NHC], F32))
        wbraw = self.trash[32:48, :]
        wbT = es.enter_context(nc.sbuf_tensor("a1_wbT", [128, NH * NHC], F32))
        l3a = es.enter_context(nc.sbuf_tensor("a1_l3a", [3, 1], F32))
        l3b = es.enter_context(nc.sbuf_tensor("a1_l3b", [3, 1], F32))

        st_xa = DmaStream(self, "a1_sxa")
        st_w = DmaStream(self, "a1_sw")
        st_qcv = DmaStream(self, "a1_sqcv")
        self._a1_xa_cons = {}
        self._a1_w_cons = {}

        def gcc(gp):
            gp.dma_start(out=self.cc_x_in[:, :], in_=self.x1s[T - 3:T, :]
                         ).then_inc(self.qg.h, 16)
            self.qg.n += 16
            gp.wait_ge(self.qg.h, self.qg.n)
            gp.collective_compute(
                "AllGather", OP.bypass,
                replica_groups=[[0, 1], [2, 3], [4, 5], [6, 7]],
                ins=[self.cc_x_in.opt()],
                outs=[self.cc_x_out.opt()],
            ).then_inc(self.gm.h, 1)
            self.gm.n += 1
            gp.wait_ge(self.gm.h, self.gm.n)
            gp.dma_start(out=x3row[:, :], in_=self.cc_x_out[0:3, :]
                         ).then_inc(self.qg.h, 16)
            self.qg.n += 16
            gp.wait_ge(self.qg.h, self.qg.n)
            self.gm.inc(gp.nop())
        self.g(gcc)
        gm_x3 = self.gm.n

        def v3a(vec):
            vec.wait_ge(self.gm.h, gm_x3)
            self.dve_sync(vec, vec.reduce_sum(l3a[:, :], x3row[:, :], axis=AX.X))
            self.dve_sync(vec, vec.tensor_scalar(l3a[:, :], l3a[:, :], 1.0 / H, None,
                                                 op0=OP.mult))
            self.vm.inc(vec.tensor_scalar(x3row[:, :], x3row[:, :], l3a[:, 0:1], None,
                                          op0=OP.subtract))
        self.v(v3a)
        vmv = self.vm.n

        def s3a(sc, vmv=vmv):
            sc.wait_ge(self.vm.h, vmv)
            self.sm.inc(sc.activation(self.trash[64:67, :], x3row[:, :], AF.Square,
                                      accum_out=l3b[:, 0:1]))
        self.s(s3a)
        smv = self.sm.n

        def v3b(vec, smv=smv):
            vec.wait_ge(self.sm.h, smv)
            self.dve_sync(vec, vec.tensor_scalar(l3b[:, :], l3b[:, :], 1.0 / H, None,
                                                 op0=OP.mult))
            self.vm.inc(vec.tensor_scalar(l3b[:, :], l3b[:, :], LN_EPS, None, op0=OP.add))
        self.v(v3b)
        smv2 = self.vm.n

        def s3b(sc, smv2=smv2):
            sc.wait_ge(self.vm.h, smv2)
            self.sm.inc(sc.activation(l3b[:, :], l3b[:, :], AF.Sqrt))
        self.s(s3b)
        smv3 = self.sm.n

        def v3c(vec, smv3=smv3):
            vec.wait_ge(self.sm.h, smv3)
            self.dve_sync(vec, vec.reciprocal(l3b[:, :], l3b[:, :]))
            self.vm.inc(vec.tensor_scalar(x3row[:, :], x3row[:, :], l3b[:, 0:1], None,
                                          op0=OP.mult))
        self.v(v3c)
        v_x3 = self.vm.n

        def yb(sync):
            sync.dma_start(out=wbraw[:, :], in_=self.wb).then_inc(self.qw.h, 16)
            self.qw.n += 16
            sync.wait_ge(self.qw.h, self.qw.n)
        self.y(yb)
        qw_wb = self.qw.n

        pool_t = PsumPool(self, [2, 3, 4, 5])
        pool_m = PsumPool(self, [0, 1])
        pool_x = PsumPool(self, [7])

        for k in range(NHC):
            def t3(ten, k=k, v_x3=v_x3):
                u, bank = pool_x.begin(ten)
                if k == 0:
                    ten.wait_ge(self.vm.h, v_x3)
                self.tm.inc(self.tr(ten, bank[:, 0:3], x3row[:, k * 128:(k + 1) * 128]))
                pool_x.took(u)
            self.t(t3)
            tmv = self.tm.n

            def v3d(vec, k=k, tmv=tmv):
                vec.wait_ge(self.tm.h, tmv)
                cp = vec.tensor_copy(xln3T[:, k * 3:(k + 1) * 3], self.pb[7][:, 0:3])
                pool_x.done_mark(self.vm, self.vm.inc(cp))
            self.v(v3d)

        for k in range(NHC):
            def tb(ten, k=k, qw_wb=qw_wb):
                u, bank = pool_x.begin(ten)
                if k == 0:
                    ten.wait_ge(self.qw.h, qw_wb)
                self.tm.inc(self.tr(ten, bank[:, 0:NH], wbraw[:, k * 128:(k + 1) * 128]))
                pool_x.took(u)
            self.t(tb)
            tmv = self.tm.n

            def vb(vec, k=k, tmv=tmv):
                vec.wait_ge(self.tm.h, tmv)
                cp = vec.tensor_copy(wbT[:, k * NH:(k + 1) * NH], self.pb[7][:, 0:NH])
                pool_x.done_mark(self.vm, self.vm.inc(cp))
            self.v(vb)

        for tc in range(NCH):
            slot = 0

            def yx(sync, tc=tc):
                if tc >= 1:
                    sync.wait_ge(self.tm.h, self._a1_xa_cons[tc - 1])
                dd = sync.dma_start(out=xa[:, :],
                                    in_=self.x1s[tc * 128:(tc + 1) * 128, :])
                st_xa.mark(dd)
            self.y(yx)
            xa_item = st_xa.n - 1
            self.emit_ln(xa, slot, st_xa, xa_item)
            v_ln = self.vm.n

            tg = []
            for grp in range(4):
                def tx(ten, grp=grp, v_ln=v_ln):
                    u, bank = pool_t.begin(ten)
                    if grp == 0:
                        ten.wait_ge(self.vm.h, v_ln)
                    for q in range(4):
                        k = grp * 4 + q
                        mm = self.tr(ten, bank[:, q * 128:(q + 1) * 128],
                                     xa[:, k * 128:(k + 1) * 128])
                    self.tm.inc(mm)
                    tg.append((u, bank, self.tm.n))
                    pool_t.took(u)
                self.t(tx)
            self._a1_xa_cons[tc] = tg[3][2]
            for grp in range(4):
                u, bank, tmv = tg[grp]

                def ev(eng, semo, is_v, grp=grp, bank=bank, tmv=tmv, tc=tc, u=u):
                    eng.wait_ge(self.tm.h, tmv)
                    for q in range(4):
                        k = grp * 4 + q
                        dst = xlnT[:, k * T + tc * 128:k * T + tc * 128 + 128]
                        if is_v:
                            cp = eng.tensor_copy(dst, bank[:, q * 128:(q + 1) * 128])
                        else:
                            cp = eng.copy(dst, bank[:, q * 128:(q + 1) * 128])
                    pool_t.done_mark_u(u, semo, semo.inc(cp))
                if grp % 2 == 0:
                    self.v(lambda vec, grp=grp, bank=bank, tmv=tmv, u=u:
                           ev(vec, self.vm, True, grp, bank, tmv, tc, u))
                else:
                    self.s(lambda sc, grp=grp, bank=bank, tmv=tmv, u=u:
                           ev(sc, self.sm, False, grp, bank, tmv, tc, u))
        xlnT_ready_v = self.vm.n
        xlnT_ready_s = self.sm.n

        for ts in range(4):
            def tbt(ten, ts=ts):
                u, bank = pool_m.begin(ten)
                if ts == 0:
                    ten.wait_ge(self.vm.h, xlnT_ready_v)
                    ten.wait_ge(self.sm.h, xlnT_ready_s)
                for k in range(NHC):
                    mm = ten.matmul(bank[:NH, 0:TBLK],
                                    wbT[:, k * NH:(k + 1) * NH],
                                    xlnT[:, k * T + ts * TBLK:k * T + (ts + 1) * TBLK],
                                    start=(k == 0), stop=(k == NHC - 1))
                self.tm.inc(mm)
                pool_m.took(u)
            self.t(tbt)
            tmv = self.tm.n
            u_used = pool_m.u - 1

            def sbt(sc, ts=ts, tmv=tmv, u=u_used):
                sc.wait_ge(self.tm.h, tmv)
                cp = sc.activation(self.beta_sb[:, ts * TBLK:(ts + 1) * TBLK],
                                   self.pb[u % 2][:NH, 0:TBLK], AF.Sigmoid)
                pool_m.done_mark_u(u, self.sm, self.sm.inc(cp))
            self.s(sbt)

        conv_sm = [0]
        for pi, (p, wsrc, dsts) in enumerate((("q", self.wq, self.qs),
                                              ("k", self.wk, self.ks),
                                              ("v", self.wv, self.vs))):
            for osl in range(NHC):
                widx = pi * NHC + osl
                slot = 0

                def yw(sync, wsrc=wsrc, osl=osl, widx=widx):
                    if widx >= 1:
                        sync.wait_ge(self.tm.h, self._a1_w_cons[widx - 1])
                    dd = sync.dma_start(out=wraw[:, :],
                                        in_=wsrc[osl * 128:(osl + 1) * 128, :])
                    st_w.mark(dd)
                self.y(yw)
                w_item = st_w.n - 1

                tg = []
                for grp in range(4):
                    def twt(ten, grp=grp, w_item=w_item):
                        u, bank = pool_t.begin(ten)
                        if grp == 0:
                            st_w.wait(ten, w_item)
                        for q in range(4):
                            k = grp * 4 + q
                            mm = self.tr(ten, bank[:, q * 128:(q + 1) * 128],
                                         wraw[:, k * 128:(k + 1) * 128])
                        self.tm.inc(mm)
                        tg.append((u, bank, self.tm.n))
                        pool_t.took(u)
                    self.t(twt)
                wT_marks = []
                for grp in range(4):
                    u, bank, tmv = tg[grp]

                    def evw(eng, semo, is_v, grp=grp, bank=bank, tmv=tmv, u=u):
                        eng.wait_ge(self.tm.h, tmv)
                        for q in range(4):
                            k = grp * 4 + q
                            dst = wT[:, k * 128:(k + 1) * 128]
                            if is_v:
                                cp = eng.tensor_copy(dst, bank[:, q * 128:(q + 1) * 128])
                            else:
                                cp = eng.copy(dst, bank[:, q * 128:(q + 1) * 128])
                        val = semo.inc(cp)
                        wT_marks.append((semo, val))
                        pool_t.done_mark_u(u, semo, val)
                    if grp % 2 == 0:
                        self.v(lambda vec, grp=grp, bank=bank, tmv=tmv, u=u:
                               evw(vec, self.vm, True, grp, bank, tmv, u))
                    else:
                        self.s(lambda sc, grp=grp, bank=bank, tmv=tmv, u=u:
                               evw(sc, self.sm, False, grp, bank, tmv, u))

                def tp3(ten, wT_marks=tuple(wT_marks)):
                    u, bank = pool_x.begin(ten)
                    for so, val in wT_marks:
                        ten.wait_ge(so.h, val)
                    for k in range(NHC):
                        mm = ten.matmul(bank[:, 0:3],
                                        wT[:, k * 128:(k + 1) * 128],
                                        xln3T[:, k * 3:(k + 1) * 3],
                                        start=(k == 0), stop=(k == NHC - 1))
                    self.tm.inc(mm)
                    pool_x.took(u)
                self.t(tp3)
                tmv = self.tm.n

                def vp3(vec, tmv=tmv):
                    vec.wait_ge(self.tm.h, tmv)
                    cp = vec.tensor_scalar(pr3i[:, :], self.pb[7][:, 0:3],
                                           self.flag_sb[:, 0:1], None, op0=OP.mult)
                    pool_x.done_mark(self.vm, self.vm.inc(cp))
                self.v(vp3)

                for ts in range(4):
                    def tmm(ten, ts=ts):
                        u, bank = pool_m.begin(ten)
                        for k in range(NHC):
                            mm = ten.matmul(bank[:, 0:TBLK],
                                            wT[:, k * 128:(k + 1) * 128],
                                            xlnT[:, k * T + ts * TBLK:k * T + (ts + 1) * TBLK],
                                            start=(k == 0), stop=(k == NHC - 1))
                        self.tm.inc(mm)
                        pool_m.took(u)
                    self.t(tmm)
                    tmv = self.tm.n
                    self._a1_w_cons[widx] = tmv
                    u_used = pool_m.u - 1

                    def vcv(vec, ts=ts, u=u_used, tmv=tmv, p=p, osl=osl):
                        vec.wait_ge(self.tm.h, tmv)
                        if conv_sm[0]:
                            vec.wait_ge(self.sm.h, conv_sm[0])
                        self.dve_sync(vec, vec.tensor_copy(ext[:, 3:515],
                                                           self.pb[u % 2][:, 0:TBLK]))
                        src3 = pr3i if ts == 0 else pr3
                        self.dve_sync(vec, vec.tensor_copy(ext[:, 0:3], src3[:, :]))
                        cw = self.cw[p]
                        self.dve_sync(vec, vec.tensor_scalar(
                            qcA[:, :], ext[:, 0:512], cw[:, osl * 4:osl * 4 + 1],
                            None, op0=OP.mult))
                        self.dve_sync(vec, vec.scalar_tensor_tensor(
                            qcB[:, :], ext[:, 1:513], cw[:, osl * 4 + 1:osl * 4 + 2],
                            qcA[:, :], op0=OP.mult, op1=OP.add))
                        self.dve_sync(vec, vec.scalar_tensor_tensor(
                            qcA[:, :], ext[:, 2:514], cw[:, osl * 4 + 2:osl * 4 + 3],
                            qcB[:, :], op0=OP.mult, op1=OP.add))
                        self.dve_sync(vec, vec.scalar_tensor_tensor(
                            qcB[:, :], ext[:, 3:515], cw[:, osl * 4 + 3:osl * 4 + 4],
                            qcA[:, :], op0=OP.mult, op1=OP.add))
                        cp = vec.tensor_copy(pr3[:, :], ext[:, 512:515])
                        val = self.vm.inc(cp)
                        pool_m.done_mark_u(u, self.vm, val)
                    self.v(vcv)
                    vmv = self.vm.n
                    qitem = st_qcv.n

                    def scv(sc, vmv=vmv, qslot=qitem % 2, qitem=qitem):
                        sc.wait_ge(self.vm.h, vmv)
                        if qitem >= 2:
                            st_qcv.wait(sc, qitem - 2)
                        self.sm.inc(sc.activation(qcv[:, qslot * TBLK:(qslot + 1) * TBLK],
                                                  qcB[:, :], AF.Silu))
                        conv_sm[0] = self.sm.n
                    self.s(scv)
                    smv = self.sm.n

                    def ycv(sync, dsts=dsts, ts=ts, osl=osl, smv=smv, qslot=qitem % 2):
                        sync.wait_ge(self.sm.h, smv)
                        ssem = st_qcv.begin_item()
                        for q in range(4):
                            c = ts * 4 + q
                            dd = sync.dma_start(
                                out=dsts[c, osl],
                                in_=qcv[:, qslot * TBLK + q * 128:qslot * TBLK + (q + 1) * 128])
                            ssem.inc(dd, 16)
                        st_qcv.commit()
                    self.y(ycv)
        es.close()

    def attn_a(self):
        nc = self.nc
        es = ExitStack()
        HG = 4
        NGR = NH // HG
        W = HG * 128
        qb = {p: es.enter_context(nc.sbuf_tensor(f"aa_qb_{p}", [128, 2 * H], F32))
              for p in "qkv"}
        DW = lambda name, w=W: es.enter_context(nc.sbuf_tensor(name, [128, w], F32))
        qst = [DW("aa_qst0"), DW("aa_qst1")]
        kst = [DW("aa_kst0"), DW("aa_kst1")]
        kfn = [DW("aa_kfn0"), DW("aa_kfn1")]
        Bw = [DW("aa_B0"), DW("aa_B1")]
        tqw = [DW("aa_tq0"), DW("aa_tq1")]
        TBw = [[DW(f"aa_TB{p}_{i}") for i in range(7)] for p in range(2)]
        Bpw = [[DW(f"aa_Bp{p}_{i}") for i in range(2)] for p in range(2)]
        ZK = [DW("aa_ZK0"), DW("aa_ZK1")]
        ZV = [DW("aa_ZV0"), DW("aa_ZV1")]
        Uw = [DW("aa_U0"), DW("aa_U1")]
        Gw = [DW("aa_G0"), DW("aa_G1")]
        OIw = [DW("aa_OI0"), DW("aa_OI1")]
        WFw = [DW("aa_WF0"), DW("aa_WF1")]
        sqw = [DW("aa_sq0"), DW("aa_sq1")]
        eall = [DW("aa_e0", 8), DW("aa_e1", 8)]
        rq_all = DW("aa_rq", NH)
        betaT = DW("aa_betaT", NH)

        st_qkv = DmaStream(self, "aa_sqkv")
        st_loc = DmaStream(self, "aa_sloc")
        st_rq = DmaStream(self, "aa_srq")
        pool = PsumPool(self, [0, 1, 2, 3, 4, 5, 6, 7])

        chunk_cons = {}
        sa_mark = {}
        gidx = [0]

        def emit_group_pair(c, q_item, pair):
            cslot = c % 2
            groups = []
            for gsub in range(2):
                g = pair * 2 + gsub
                h0 = g * HG
                gi = gidx[0] + gsub
                p = gi % 2
                groups.append(dict(g=g, h0=h0, p=p, gi=gi,
                                   qf=qb["q"][:, cslot * H + h0 * 128:cslot * H + (h0 + HG) * 128],
                                   kf=qb["k"][:, cslot * H + h0 * 128:cslot * H + (h0 + HG) * 128],
                                   vf=qb["v"][:, cslot * H + h0 * 128:cslot * H + (h0 + HG) * 128],
                                   st={}))
            gidx[0] += 2

            def seg(wt, j, w=128):
                return wt[:, j * w:(j + 1) * w]

            stages = []

            def t1(G, ten):
                p = G["p"]
                banks = []
                for src in (G["qf"], G["kf"], G["vf"]):
                    u, bank = pool.begin(ten)
                    for j in range(HG):
                        mm = self.tr(ten, seg(bank, j), seg(src, j))
                    self.tm.inc(mm)
                    pool.took(u)
                    banks.append(u)
                G["st"]["b1"] = banks
                if G["gi"] >= 2:
                    pass
                G["tmv1"] = self.tm.n

            def v1(G, vec):
                p = G["p"]
                vec.wait_ge(self.tm.h, G["tmv1"])
                prev_loc = st_loc.vals.get(G["gi"] - 2)
                if prev_loc is not None:
                    vec.wait_ge(prev_loc[0].h, prev_loc[1])
                bq, bk, bv = [pool.bank(u) for u in G["st"]["b1"]]
                self.dve_sync(vec, vec.tensor_copy(qst[p][:, :], bq[:, 0:W]))
                self.dve_sync(vec, vec.tensor_copy(kst[p][:, :], bk[:, 0:W]))
                last = None
                for j in range(HG):
                    last = vec.tensor_scalar(seg(ZV[p], j), seg(bv, j),
                                             betaT[:, G["h0"] + j:G["h0"] + j + 1],
                                             None, op0=OP.mult)
                val = self.vm.inc(last)
                for u in G["st"]["b1"]:
                    pool.done_mark_u(u, self.vm, val)
            stages.append((t1, v1, None))

            def v2(G, vec):
                p = G["p"]
                self.dve_sync(vec, vec.tensor_tensor(sqw[p][:, :], qst[p][:, :],
                                                     qst[p][:, :], op=OP.mult))
                for j in range(HG):
                    self.dve_sync(vec, vec.reduce_sum(eall[p][:, j:j + 1],
                                                      seg(sqw[p], j), axis=AX.X))
                self.dve_sync(vec, vec.tensor_tensor(sqw[p][:, :], kst[p][:, :],
                                                     kst[p][:, :], op=OP.mult))
                for j in range(HG):
                    self.dve_sync(vec, vec.reduce_sum(eall[p][:, 4 + j:5 + j],
                                                      seg(sqw[p], j), axis=AX.X))
                self.vm.inc(vec.tensor_scalar(eall[p][:, :], eall[p][:, :],
                                              L2_EPS, None, op0=OP.add))
                G["v2"] = self.vm.n

            def s2(G, sc):
                sc.wait_ge(self.vm.h, G["v2"])
                self.sm.inc(sc.activation(eall[G["p"]][:, :], eall[G["p"]][:, :], AF.Sqrt))
                G["s2"] = self.sm.n
            stages.append((None, v2, s2))

            def v3(G, vec):
                p = G["p"]
                vec.wait_ge(self.sm.h, G["s2"])
                if G["g"] == 0 and c >= 1:
                    st_rq.wait(vec, c - 1)
                self.dve_sync(vec, vec.reciprocal(eall[p][:, :], eall[p][:, :]))
                self.dve_sync(vec, vec.tensor_scalar(
                    rq_all[:, G["h0"]:G["h0"] + HG], eall[p][:, 0:4], HD_SCALE,
                    None, op0=OP.mult))
                for j in range(HG):
                    self.dve_sync(vec, vec.tensor_scalar(
                        seg(kst[p], j), seg(kst[p], j), eall[p][:, 4 + j:5 + j],
                        None, op0=OP.mult))
                last = None
                for j in range(HG):
                    last = vec.tensor_scalar(seg(ZK[p], j), seg(kst[p], j),
                                             betaT[:, G["h0"] + j:G["h0"] + j + 1],
                                             None, op0=OP.mult)
                val = self.vm.inc(last)
                G["v3"] = val
            stages.append((None, v3, None))

            def t4(G, ten):
                p = G["p"]
                u, bank = pool.begin(ten)
                ten.wait_ge(self.vm.h, G["v3"])
                for j in range(HG):
                    mm = self.tr(ten, seg(bank, j), seg(kst[p], j))
                self.tm.inc(mm)
                pool.took(u)
                G["st"]["b4"] = u
                G["tmv4"] = self.tm.n

            def v4(G, vec):
                p = G["p"]
                vec.wait_ge(self.tm.h, G["tmv4"])
                bank = pool.bank(G["st"]["b4"])
                cp = vec.tensor_copy(kfn[p][:, :], bank[:, 0:W])
                pool.done_mark_u(G["st"]["b4"], self.vm, self.vm.inc(cp))
                G["v4"] = self.vm.n
            stages.append((t4, v4, None))

            def t5(G, ten):
                p = G["p"]
                uk, bk = pool.begin(ten)
                ten.wait_ge(self.vm.h, G["v4"])
                for j in range(HG):
                    mm = ten.matmul(seg(bk, j), seg(kfn[p], j), seg(kfn[p], j),
                                    start=True, stop=True)
                self.tm.inc(mm)
                pool.took(uk)
                uq, bq = pool.begin(ten)
                for j in range(HG):
                    mm = ten.matmul(seg(bq, j), seg(G["qf"], j), seg(kfn[p], j),
                                    start=True, stop=True)
                self.tm.inc(mm)
                pool.took(uq)
                G["st"]["b5"] = (uk, uq)
                G["tmv5"] = self.tm.n

            def v5(G, vec):
                p = G["p"]
                vec.wait_ge(self.tm.h, G["tmv5"])
                uk, uq = G["st"]["b5"]
                bk, bq = pool.bank(uk), pool.bank(uq)
                last = None
                for j in range(HG):
                    last = vec.scalar_tensor_tensor(
                        seg(Bw[p], j), seg(bk, j),
                        betaT[:, G["h0"] + j:G["h0"] + j + 1], self.stril[:, :],
                        op0=OP.mult, op1=OP.mult)
                self.dve_sync(vec, last)
                cp = vec.tensor_tensor(tqw[p][:, :], bq[:, 0:W], self.tril0w[:, :],
                                       op=OP.mult)
                val = self.vm.inc(cp)
                pool.done_mark_u(uk, self.vm, val)
                pool.done_mark_u(uq, self.vm, val)
                G["v5"] = val
            stages.append((t5, v5, None))

            def t6(G, ten):
                p = G["p"]
                ub, bb = pool.begin(ten)
                ten.wait_ge(self.vm.h, G["v5"])
                for j in range(HG):
                    mm = self.tr(ten, seg(bb, j), seg(Bw[p], j))
                self.tm.inc(mm)
                pool.took(ub)
                ut, bt = pool.begin(ten)
                for j in range(HG):
                    mm = self.tr(ten, seg(bt, j), seg(tqw[p], j))
                self.tm.inc(mm)
                pool.took(ut)
                G["st"]["b6"] = (ub, ut)
                G["tmv6"] = self.tm.n

            def v6(G, vec):
                p = G["p"]
                vec.wait_ge(self.tm.h, G["tmv6"])
                ub, ut = G["st"]["b6"]
                self.dve_sync(vec, vec.tensor_copy(TBw[p][0][:, :],
                                                   pool.bank(ub)[:, 0:W]))
                cp = vec.tensor_copy(tqw[p][:, :], pool.bank(ut)[:, 0:W])
                val = self.vm.inc(cp)
                pool.done_mark_u(ub, self.vm, val)
                pool.done_mark_u(ut, self.vm, val)
                G["v6"] = val
            stages.append((t6, v6, None))

            for i in range(1, 7):
                def tsq(G, ten, i=i):
                    p = G["p"]
                    Bp_cur = Bw[p] if i == 1 else Bpw[p][(i - 1) % 2]
                    ten.wait_ge(self.vm.h, G["v6"] if i == 1 else G[f"vsq{i-1}"])
                    if i >= 2:
                        ten.wait_ge(self.sm.h, G[f"ssq{i-1}"])
                    uT, bT = pool.begin(ten)
                    for j in range(HG):
                        mm = ten.matmul(seg(bT, j), seg(Bp_cur, j), seg(TBw[p][i - 1], j),
                                        start=True, stop=True)
                    self.tm.inc(mm)
                    pool.took(uT)
                    uB = None
                    if i < 6:
                        uB, bB = pool.begin(ten)
                        for j in range(HG):
                            mm = ten.matmul(seg(bB, j), seg(TBw[p][i - 1], j),
                                            seg(Bp_cur, j), start=True, stop=True)
                        self.tm.inc(mm)
                        pool.took(uB)
                    G["st"][f"bsq{i}"] = (uT, uB)
                    G[f"tmsq{i}"] = self.tm.n

                def vsq(G, vec, i=i):
                    p = G["p"]
                    vec.wait_ge(self.tm.h, G[f"tmsq{i}"])
                    uT, uB = G["st"][f"bsq{i}"]
                    cp = vec.tensor_copy(TBw[p][i][:, :], pool.bank(uT)[:, 0:W])
                    val = self.vm.inc(cp)
                    pool.done_mark_u(uT, self.vm, val)
                    G[f"vsq{i}"] = val

                def ssq(G, sc, i=i):
                    p = G["p"]
                    uT, uB = G["st"][f"bsq{i}"]
                    if uB is None:
                        G[f"ssq{i}"] = self.sm.n
                        return
                    sc.wait_ge(self.tm.h, G[f"tmsq{i}"])
                    cp = sc.copy(Bpw[p][i % 2][:, :], pool.bank(uB)[:, 0:W])
                    val = self.sm.inc(cp)
                    pool.done_mark_u(uB, self.sm, val)
                    G[f"ssq{i}"] = val
                stages.append((tsq, vsq, ssq))

            for ai, i in enumerate([6, 5, 4, 3, 2, 1, 0]):
                def tap(G, ten, i=i, ai=ai):
                    p = G["p"]
                    ten.wait_ge(self.vm.h, G["vsq6"] if ai == 0 else G[f"vap{ai-1}"])
                    TBi = Bw[p] if i == 0 else TBw[p][i]
                    TBi = TBw[p][0] if i == 0 else TBw[p][i]
                    uK, bK = pool.begin(ten)
                    for j in range(HG):
                        mm = ten.matmul(seg(bK, j), seg(TBi, j), seg(ZK[p], j),
                                        start=True, stop=True)
                    self.tm.inc(mm)
                    pool.took(uK)
                    uV, bV = pool.begin(ten)
                    for j in range(HG):
                        mm = ten.matmul(seg(bV, j), seg(TBi, j), seg(ZV[p], j),
                                        start=True, stop=True)
                    self.tm.inc(mm)
                    pool.took(uV)
                    G["st"][f"bap{ai}"] = (uK, uV)
                    G[f"tmap{ai}"] = self.tm.n

                def vap(G, vec, i=i, ai=ai):
                    p = G["p"]
                    vec.wait_ge(self.tm.h, G[f"tmap{ai}"])
                    uK, uV = G["st"][f"bap{ai}"]
                    op = OP.subtract if i == 0 else OP.add
                    self.dve_sync(vec, vec.tensor_tensor(
                        ZK[p][:, :], ZK[p][:, :], pool.bank(uK)[:, 0:W], op=op))
                    cp = vec.tensor_tensor(ZV[p][:, :], ZV[p][:, :],
                                           pool.bank(uV)[:, 0:W], op=op)
                    val = self.vm.inc(cp)
                    pool.done_mark_u(uK, self.vm, val)
                    pool.done_mark_u(uV, self.vm, val)
                    G[f"vap{ai}"] = val
                stages.append((tap, vap, None))

            def t20(G, ten):
                p = G["p"]
                ten.wait_ge(self.vm.h, G["vap6"])
                ug, bg = pool.begin(ten)
                for j in range(HG):
                    mm = ten.matmul(seg(bg, j), seg(ZK[p], j), seg(tqw[p], j),
                                    start=True, stop=True)
                self.tm.inc(mm)
                pool.took(ug)
                uo, bo = pool.begin(ten)
                for j in range(HG):
                    mm = ten.matmul(seg(bo, j), seg(tqw[p], j), seg(ZV[p], j),
                                    start=True, stop=True)
                self.tm.inc(mm)
                pool.took(uo)
                uw, bw = pool.begin(ten)
                for j in range(HG):
                    mm = self.tr(ten, seg(bw, j), seg(ZK[p], j))
                self.tm.inc(mm)
                pool.took(uw)
                G["st"]["b20"] = (ug, uo, uw)
                G["tmv20"] = self.tm.n

            def v20(G, vec):
                p = G["p"]
                vec.wait_ge(self.tm.h, G["tmv20"])
                ug, uo, uw = G["st"]["b20"]
                self.dve_sync(vec, vec.tensor_tensor(Gw[p][:, :], G["qf"],
                                                     pool.bank(ug)[:, 0:W],
                                                     op=OP.subtract))
                self.dve_sync(vec, vec.tensor_copy(OIw[p][:, :],
                                                   pool.bank(uo)[:, 0:W]))
                cp = vec.tensor_copy(WFw[p][:, :], pool.bank(uw)[:, 0:W])
                val = self.vm.inc(cp)
                for u in (ug, uo, uw):
                    pool.done_mark_u(u, self.vm, val)
                G["v20"] = val
            stages.append((t20, v20, None))

            def s21(G, sc):
                p = G["p"]
                sc.wait_ge(self.vm.h, G["v20"])
                ssem = st_loc.begin_item()
                for j in range(HG):
                    h = G["h0"] + j
                    ssem.inc(sc.dma_start(out=self.loc[c, h, 0], in_=seg(WFw[p], j)), 16)
                    ssem.inc(sc.dma_start(out=self.loc[c, h, 1], in_=seg(ZV[p], j)), 16)
                    ssem.inc(sc.dma_start(out=self.loc[c, h, 2], in_=seg(kst[p], j)), 16)
                    ssem.inc(sc.dma_start(out=self.loc[c, h, 3], in_=seg(Gw[p], j)), 16)
                    ssem.inc(sc.dma_start(out=self.loc[c, h, 4], in_=seg(OIw[p], j)), 16)
                it = st_loc.commit()
                assert it == G["gi"]
            stages.append((None, None, s21))

            def t22(G, ten):
                p = G["p"]
                prev = sa_mark.get(G["g"])
                if prev is not None:
                    ten.wait_ge(self.vm.h, prev)
                ten.wait_ge(self.vm.h, G["v20"])
                u, bank = pool.begin(ten)
                for j in range(HG):
                    mm = ten.matmul(seg(bank, j), seg(WFw[p], j),
                                    self.SA[:, (G["h0"] + j) * 128:(G["h0"] + j + 1) * 128],
                                    start=True, stop=True)
                self.tm.inc(mm)
                pool.took(u)
                G["st"]["b22"] = u
                G["tmv22"] = self.tm.n

            def v22(G, vec):
                p = G["p"]
                vec.wait_ge(self.tm.h, G["tmv22"])
                u = G["st"]["b22"]
                cp = vec.tensor_tensor(Uw[p][:, :], ZV[p][:, :],
                                       pool.bank(u)[:, 0:W], op=OP.subtract)
                val = self.vm.inc(cp)
                pool.done_mark_u(u, self.vm, val)
                G["v22"] = val
            stages.append((t22, v22, None))

            def t23(G, ten):
                p = G["p"]
                ten.wait_ge(self.vm.h, G["v22"])
                u, bank = pool.begin(ten)
                for j in range(HG):
                    mm = ten.matmul(seg(bank, j), seg(kst[p], j), seg(Uw[p], j),
                                    start=True, stop=True)
                self.tm.inc(mm)
                pool.took(u)
                G["st"]["b23"] = u
                G["tmv23"] = self.tm.n
                chunk_cons[(c, G["g"])] = self.tm.n

            def v23(G, vec):
                p = G["p"]
                vec.wait_ge(self.tm.h, G["tmv23"])
                u = G["st"]["b23"]
                sl = self.SA[:, G["h0"] * 128:(G["h0"] + HG) * 128]
                cp = vec.tensor_tensor(sl, sl, pool.bank(u)[:, 0:W], op=OP.add)
                val = self.vm.inc(cp)
                pool.done_mark_u(u, self.vm, val)
                sa_mark[G["g"]] = val
            stages.append((t23, v23, None))

            for (tf, vf, sf) in stages:
                if tf is not None:
                    for G in groups:
                        self.t(lambda ten, tf=tf, G=G: tf(G, ten))
                if vf is not None:
                    for G in groups:
                        self.v(lambda vec, vf=vf, G=G: vf(G, vec))
                if sf is not None:
                    for G in groups:
                        self.s(lambda sc, sf=sf, G=G: sf(G, sc))

        for c in range(NCH):
            def yq(sync, c=c):
                if c >= 2:
                    for g in range(NGR):
                        sync.wait_ge(self.tm.h, chunk_cons[(c - 2, g)])
                slot = c % 2
                ssem = st_qkv.begin_item()
                for p, src in (("q", self.qs), ("k", self.ks), ("v", self.vs)):
                    for oc in range(NHC):
                        dd = sync.dma_start(
                            out=qb[p][:, slot * H + oc * 128:slot * H + (oc + 1) * 128],
                            in_=src[c, oc])
                        ssem.inc(dd, 16)
                st_qkv.commit()
            self.y(yq)
            q_item = st_qkv.n - 1

            def tbt(ten, c=c, q_item=q_item):
                u, bank = pool.begin(ten)
                st_qkv.wait(ten, q_item)
                self.tm.inc(self.tr(ten, bank[:, 0:NH],
                                    self.beta_sb[:, c * 128:(c + 1) * 128]))
                pool.took(u)
                self._aa_bt = (u, self.tm.n)
            self.t(tbt)

            def vbt(vec, c=c):
                u, tmv = self._aa_bt
                vec.wait_ge(self.tm.h, tmv)
                cp = vec.tensor_copy(betaT[:, :], pool.bank(u)[:, 0:NH])
                pool.done_mark_u(u, self.vm, self.vm.inc(cp))
            self.v(vbt)

            for pair in range(NGR // 2):
                emit_group_pair(c, q_item, pair)

            v_rq = self.vm.n

            def srq(sc, c=c, v_rq=v_rq):
                sc.wait_ge(self.vm.h, v_rq)
                dd = sc.dma_start(out=self.locr[c], in_=rq_all[:, :])
                st_rq.mark(dd)
            self.s(srq)
        es.close()


    def attn_cc(self):
        sa_final = self.vm.n

        def gcc(gp):
            gp.wait_ge(self.vm.h, sa_final)
            gp.dma_start(out=self.cc_s_in[:, :], in_=self.SA[:, :]).then_inc(self.qg.h, 16)
            self.qg.n += 16
            gp.wait_ge(self.qg.h, self.qg.n)
            gp.collective_compute(
                "AllGather", OP.bypass,
                replica_groups=[[0, 1], [2, 3], [4, 5], [6, 7]],
                ins=[self.cc_s_in.opt()],
                outs=[self.cc_s_out.opt()],
            ).then_inc(self.gm.h, 1)
            self.gm.n += 1
            gp.wait_ge(self.gm.h, self.gm.n)
            gp.dma_start(out=self.SB[:, :], in_=self.cc_s_out[0:128, :]
                         ).then_inc(self.qg.h, 16)
            self.qg.n += 16
            gp.wait_ge(self.qg.h, self.qg.n)
            self.gm.inc(gp.nop())
        self.g(gcc)
        gm_v = self.gm.n

        def vflag(vec):
            vec.wait_ge(self.gm.h, gm_v)
            self.vm.inc(vec.tensor_scalar(self.SB[:, :], self.SB[:, :],
                                          self.flag_sb[:, 0:1], None, op0=OP.mult))
        self.v(vflag)

    def attn_b(self, o_fm):
        nc = self.nc
        es = ExitStack()
        lb = [es.enter_context(nc.sbuf_tensor(f"ab_lb{i}", [128, 5 * 128], F32))
              for i in range(2)]
        rqb = [es.enter_context(nc.sbuf_tensor(f"ab_rq{i}", [128, NH], F32))
               for i in range(2)]
        t_ou = [es.enter_context(nc.sbuf_tensor(f"ab_ou{i}", [128, 128], F32))
                for i in range(2)]
        t_u = es.enter_context(nc.sbuf_tensor("ab_u", [128, 128], F32))
        sso = es.enter_context(nc.sbuf_tensor("ab_sso", [128, 1], F32))
        rq2 = es.enter_context(nc.sbuf_tensor("ab_rq2", [128, 1], F32))
        tr3 = es.enter_context(nc.sbuf_tensor("ab_tr3", [128, 128], F32))

        st_lb = DmaStream(self, "ab_slb")
        st_rq = DmaStream(self, "ab_srq2")
        pool = PsumPool(self, [0, 1, 2, 3])
        sb_mark = {}
        ou_cons = {}
        hu = [0]

        for c in range(NCH):
            def yrq(sync, c=c):
                if c >= 2:
                    sync.wait_ge(self.vm.h, self._ab_rq_cons.get(c - 2, 0))
                dd = sync.dma_start(out=rqb[c % 2][:, :], in_=self.locr[c])
                st_rq.mark(dd)
            self.y(yrq)
            rq_item = st_rq.n - 1
            if not hasattr(self, "_ab_rq_cons"):
                self._ab_rq_cons = {}

            for h in range(NH):
                hs = hu[0] % 2
                def ylb(sync, c=c, h=h, hs=hs, it=hu[0]):
                    if it >= 2:
                        prev = ou_cons.get(it - 2)
                        if prev is not None:
                            sync.wait_ge(prev[0].h, prev[1])
                    ssem = st_lb.begin_item()
                    for j in range(5):
                        dd = sync.dma_start(out=lb[hs][:, j * 128:(j + 1) * 128],
                                            in_=self.loc[c, h, j])
                        ssem.inc(dd, 16)
                    st_lb.commit()
                self.y(ylb)
                lb_item = st_lb.n - 1
                W_ = lb[hs][:, 0:128]
                Uv = lb[hs][:, 128:256]
                Kst = lb[hs][:, 256:384]
                G_ = lb[hs][:, 384:512]
                Oint = lb[hs][:, 512:640]
                prev_sb = sb_mark.get(h)

                def t1(ten, hs=hs, h=h, lb_item=lb_item, prev_sb=prev_sb):
                    u, bank = pool.begin(ten)
                    st_lb.wait(ten, lb_item)
                    if prev_sb is not None:
                        ten.wait_ge(self.vm.h, prev_sb)
                    ten.matmul(bank[:, 0:128], lb[hs][:, 384:512],
                               self.SB[:, h * 128:(h + 1) * 128], start=True, stop=True)
                    self.tm.inc(ten.matmul(bank[:, 128:256], lb[hs][:, 0:128],
                                           self.SB[:, h * 128:(h + 1) * 128],
                                           start=True, stop=True))
                    pool.took(u)
                self.t(t1)
                tmv = self.tm.n
                uu = pool.u - 1

                def v2(vec, hs=hs, uu=uu, tmv=tmv):
                    vec.wait_ge(self.tm.h, tmv)
                    bank = self.pb[pool.banks[uu % len(pool.banks)]]
                    self.dve_sync(vec, vec.tensor_tensor(t_ou[hs][:, :], bank[:, 0:128],
                                                         lb[hs][:, 512:640], op=OP.add))
                    cp = vec.tensor_tensor(t_u[:, :], lb[hs][:, 128:256], bank[:, 128:256],
                                           op=OP.subtract)
                    pool.done_mark_u(uu, self.vm, self.vm.inc(cp))
                self.v(v2)
                vmv = self.vm.n

                def t3(ten, hs=hs, vmv=vmv):
                    u, bank = pool.begin(ten)
                    ten.wait_ge(self.vm.h, vmv)
                    self.tm.inc(ten.matmul(bank[:, 0:128], lb[hs][:, 256:384], t_u[:, :],
                                           start=True, stop=True))
                    pool.took(u)
                self.t(t3)
                tmv = self.tm.n
                uu = pool.u - 1

                def v3(vec, h=h, uu=uu, tmv=tmv):
                    vec.wait_ge(self.tm.h, tmv)
                    bank = self.pb[pool.banks[uu % len(pool.banks)]]
                    cp = vec.tensor_tensor(self.SB[:, h * 128:(h + 1) * 128],
                                           self.SB[:, h * 128:(h + 1) * 128],
                                           bank[:, 0:128], op=OP.add)
                    val = self.vm.inc(cp)
                    sb_mark[h] = val
                    pool.done_mark_u(uu, self.vm, val)
                self.v(v3)
                vmv = self.vm.n

                def v6(vec, hs=hs, rq_item=rq_item, h=h, c=c):
                    st_rq.wait(vec, rq_item)
                    self.dve_sync(vec, vec.tensor_tensor(tr3[:, :], t_ou[hs][:, :],
                                                         t_ou[hs][:, :], op=OP.mult))
                    self.dve_sync(vec, vec.reduce_sum(sso[:, 0:1], tr3[:, :], axis=AX.X))
                    self.dve_sync(vec, vec.tensor_tensor(rq2[:, :], rqb[c % 2][:, h:h + 1],
                                                         rqb[c % 2][:, h:h + 1], op=OP.mult))
                    self.dve_sync(vec, vec.tensor_scalar(sso[:, :], sso[:, :],
                                                         1.0 / HD, None, op0=OP.mult))
                    self.dve_sync(vec, vec.tensor_tensor(sso[:, :], sso[:, :], rq2[:, :],
                                                         op=OP.mult))
                    self.vm.inc(vec.tensor_scalar(sso[:, :], sso[:, :], RMS_EPS,
                                                  None, op0=OP.add))
                self.v(v6)
                vmv = self.vm.n

                def s6(sc, vmv=vmv):
                    sc.wait_ge(self.vm.h, vmv)
                    self.sm.inc(sc.activation(sso[:, :], sso[:, :], AF.Sqrt))
                self.s(s6)
                smv = self.sm.n

                def v7(vec, hs=hs, smv=smv, h=h, c=c):
                    vec.wait_ge(self.sm.h, smv)
                    self.dve_sync(vec, vec.reciprocal(sso[:, :], sso[:, :]))
                    self.dve_sync(vec, vec.tensor_tensor(sso[:, :], sso[:, :],
                                                         rqb[c % 2][:, h:h + 1], op=OP.mult))
                    self.vm.inc(vec.tensor_scalar(t_ou[hs][:, :], t_ou[hs][:, :],
                                                  sso[:, 0:1], None, op0=OP.mult))
                    self._ab_rq_cons[c] = self.vm.n
                self.v(v7)
                vmv = self.vm.n

                def t8(ten, hs=hs, vmv=vmv):
                    u, bank = pool.begin(ten)
                    ten.wait_ge(self.vm.h, vmv)
                    self.tm.inc(self.tr(ten, bank[:, 0:128], t_ou[hs][:, :]))
                    pool.took(u)
                self.t(t8)
                tmv = self.tm.n
                uu = pool.u - 1

                def s8(sc, h=h, c=c, uu=uu, tmv=tmv):
                    sc.wait_ge(self.tm.h, tmv)
                    bank = self.pb[pool.banks[uu % len(pool.banks)]]
                    cp = sc.copy(o_fm[:, h * T + c * 128:h * T + (c + 1) * 128],
                                 bank[:, 0:128])
                    val = self.sm.inc(cp)
                    ou_cons[hu[0]] = (self.sm, val)
                    pool.done_mark_u(uu, self.sm, val)
                self.s(s8)
                hu[0] += 1
        es.close()

    def attn_out(self, o_fm):
        nc = self.nc
        es = ExitStack()
        woT = es.enter_context(nc.sbuf_tensor("a2_woT", [128, 2 * H], F32))
        x2tm = es.enter_context(nc.sbuf_tensor("a2_x2tm", [128, 2 * H], F32))
        yb = es.enter_context(nc.sbuf_tensor("a2_yb", [128, TS_A2], F32))
        st_wo = DmaStream(self, "a2_swo")
        st_x = DmaStream(self, "a2_sx")
        st_st = DmaStream(self, "a2_sst")
        pool = PsumPool(self, [0, 1])
        pool_t = PsumPool(self, [2, 3])
        o_ready_v = self.vm.n
        o_ready_s = self.sm.n
        NTS = T // TS_A2
        wo_cons = {}
        x2_done = {}

        for ts in range(NTS):
            for tc in range(2):
                def yx(sync, ts=ts, tc=tc):
                    it = ts * 2 + tc
                    if it >= 2:
                        prev = x2_done.get(it - 2)
                        if prev is not None:
                            sync.wait_ge(prev[0].h, prev[1])
                    r0 = ts * TS_A2 + tc * 128
                    dd = sync.dma_start(out=x2tm[:, tc * H:(tc + 1) * H],
                                        in_=self.x1s[r0:r0 + 128, :])
                    st_x.mark(dd)
                self.y(yx)
            x_item = st_x.n - 1

            for osl in range(NHC):
                widx = ts * NHC + osl
                slot = widx % 2

                def ywo(sync, osl=osl, widx=widx, slot=slot):
                    if widx >= 2:
                        sync.wait_ge(self.tm.h, wo_cons[widx - 2])
                    dd = sync.dma_start(out=woT[:, slot * H:(slot + 1) * H],
                                        in_=self.wot[osl])
                    st_wo.mark(dd)
                self.y(ywo)
                w_item = st_wo.n - 1

                def tmm(ten, ts=ts, slot=slot, w_item=w_item, osl=osl, widx=widx):
                    u, bank = pool.begin(ten)
                    st_wo.wait(ten, w_item)
                    if widx == 0:
                        ten.wait_ge(self.vm.h, o_ready_v)
                        ten.wait_ge(self.sm.h, o_ready_s)
                    for k in range(NHC):
                        mm = ten.matmul(bank[:, 0:TS_A2],
                                        woT[:, slot * H + k * 128:slot * H + (k + 1) * 128],
                                        o_fm[:, k * T + ts * TS_A2:k * T + (ts + 1) * TS_A2],
                                        start=(k == 0), stop=(k == NHC - 1))
                    self.tm.inc(mm)
                    wo_cons[widx] = self.tm.n
                    pool.took(u)
                self.t(tmm)
                tmv = self.tm.n
                uu = pool.u - 1

                def vev(vec, uu=uu, tmv=tmv):
                    vec.wait_ge(self.tm.h, tmv)
                    bank = self.pb[pool.banks[uu % len(pool.banks)]]
                    cp = vec.tensor_copy(yb[:, :], bank[:, 0:TS_A2])
                    pool.done_mark_u(uu, self.vm, self.vm.inc(cp))
                self.v(vev)
                vmv = self.vm.n

                def ttr(ten, vmv=vmv):
                    u, bank = pool_t.begin(ten)
                    ten.wait_ge(self.vm.h, vmv)
                    self.tr(ten, bank[:, 0:128], yb[:, 0:128])
                    self.tm.inc(self.tr(ten, bank[:, 128:256], yb[:, 128:256]))
                    pool_t.took(u)
                self.t(ttr)
                tmv = self.tm.n
                uu = pool_t.u - 1

                def vadd(vec, osl=osl, uu=uu, tmv=tmv, x_item=x_item):
                    vec.wait_ge(self.tm.h, tmv)
                    st_x.wait(vec, x_item - 1)
                    st_x.wait(vec, x_item)
                    bank = self.pb[pool_t.banks[uu % len(pool_t.banks)]]
                    for tc in range(2):
                        dst = x2tm[:, tc * H + osl * 128:tc * H + (osl + 1) * 128]
                        cp = vec.tensor_tensor(dst, dst, bank[:, tc * 128:(tc + 1) * 128],
                                               op=OP.add)
                    pool_t.done_mark_u(uu, self.vm, self.vm.inc(cp))
                self.v(vadd)
            v_x2 = self.vm.n

            def yst(sync, ts=ts, v_x2=v_x2):
                sync.wait_ge(self.vm.h, v_x2)
                for tc in range(2):
                    r0 = ts * TS_A2 + tc * 128
                    dd = sync.dma_start(out=self.x2s[r0:r0 + 128, :],
                                        in_=x2tm[:, tc * H:(tc + 1) * H])
                    it = st_st.mark(dd)
                    x2_done[ts * 2 + tc] = st_st.vals[it]
            self.y(yst)
        es.close()

    def finish(self):
        final = self.qw.n

        def yend(sync):
            sync.wait_ge(self.qw.h, final)
        self.y(yend)
        self.es.close()
        return self.nc


def _build_nc(phases=("f1", "attn", "f2")):
    mk = MK()
    mk.p0()
    mk.phase_barrier()
    if "f1" in phases:
        mk.ffn(mk.x0, mk.x1s, "f1")
        mk.phase_barrier()
    if "f1out" in phases:
        mk.ffn(mk.x0, mk.out, "f1")
        mk.phase_barrier()
    if "attn" in phases:
        mk.attn_proj()
        mk.phase_barrier()
        mk.attn_a()
        mk.phase_barrier()
        mk.attn_cc()
        mk.phase_barrier()
        o_es = ExitStack()
        o_fm = o_es.enter_context(mk.nc.sbuf_tensor("o_fm", [128, NHC * T], F32))
        mk.attn_b(o_fm)
        mk.phase_barrier()
        mk.attn_out(o_fm)
        mk.phase_barrier()
        o_es.close()
    if "attnout" in phases:
        pass
    if "f2" in phases:
        mk.ffn(mk.x2s, mk.out, "f2")
        mk.phase_barrier()
    return mk.finish()


SEL8 = np.zeros((NG, NG * 128), np.float32)
for _g in range(NG):
    SEL8[_g, _g * 128:(_g + 1) * 128] = 1.0


def _make_in_maps(inputs):
    x = np.ascontiguousarray(inputs["L0_attn_x"])
    common = {
        "wq": np.ascontiguousarray(inputs["L0_attn_Wq"]),
        "wk": np.ascontiguousarray(inputs["L0_attn_Wk"]),
        "wv": np.ascontiguousarray(inputs["L0_attn_Wv"]),
        "wb": np.ascontiguousarray(inputs["L0_attn_Wb"]),
        "wo": np.ascontiguousarray(inputs["L0_attn_Wo"]),
        "cq": np.ascontiguousarray(inputs["L0_attn_conv_q"]),
        "ck": np.ascontiguousarray(inputs["L0_attn_conv_k"]),
        "cv": np.ascontiguousarray(inputs["L0_attn_conv_v"]),
        "rmsw": np.ascontiguousarray(
            np.asarray(inputs["L0_attn_rms_w"]).reshape(HD, 1)),
        "gw": np.ascontiguousarray(inputs["L0_ffn_gate_weight"]),
        "gup": np.ascontiguousarray(
            np.asarray(inputs["L0_ffn_gate_up_proj"]).reshape(NE * 2 * FI, H)),
        "dwn": np.ascontiguousarray(
            np.asarray(inputs["L0_ffn_down_proj"]).reshape(NE * H, FI)),
        "mg": np.ascontiguousarray(
            np.asarray(inputs["L0_ffn_merge_groups"]).reshape(NE, 1).astype(np.int32)),
        "dom": np.ascontiguousarray(
            np.asarray(inputs["L0_ffn_dominant_experts"]).reshape(NG, 1).astype(np.int32)),
    }
    in_maps = []
    for c in range(N_CORES):
        b, hf = divmod(c, 2)
        m = dict(common)
        m["x"] = np.ascontiguousarray(x[b, hf * T:(hf + 1) * T, :])
        m["flag"] = np.full((128, 1), float(hf), np.float32)
        m["sel8"] = SEL8
        in_maps.append(m)
    return in_maps


def kernel(**inputs):
    in_maps = _make_in_maps(inputs)
    if "nc" not in _NC_CACHE:
        _NC_CACHE["nc"] = _build_nc()
    nc = _NC_CACHE["nc"]
    res = run_bass_kernel_spmd(nc, in_maps, core_ids=list(range(N_CORES)))
    out = np.empty((4, 4096, H), np.float32)
    for c in range(N_CORES):
        b, hf = divmod(c, 2)
        out[b, hf * T:(hf + 1) * T, :] = res.results[c]["out"]
    return out


# version 06 (final)
# speedup vs baseline: 1.3831x; 1.0200x over previous
import numpy as np
from contextlib import ExitStack

import concourse.bass as bass
import concourse.mybir as mybir
from concourse.bass_utils import run_bass_kernel_spmd

F32 = mybir.dt.float32
I32 = mybir.dt.int32
AF = mybir.ActivationFunctionType
OP = mybir.AluOpType
AX = mybir.AxisListType

H = 2048
T = 2048
NHC = 16
NH, HD = 16, 128
C = 128
NCH = 16
TBLK = 512
NBLK = 4
NE, NG = 32, 8
FI = 768
NJ = 12
NI = 6
LN_EPS = 1e-5
L2_EPS = 1e-6
RMS_EPS = 1e-5
HD_SCALE = float(HD) ** -0.5
TS_A2 = 256
N_CORES = 8

_NC_CACHE = {}


class Sem:
    __slots__ = ("h", "n")

    def __init__(self, h):
        self.h = h
        self.n = 0

    def inc(self, instr, k=1):
        instr.then_inc(self.h, k)
        self.n += k
        return self.n


class DmaStream:

    def __init__(self, mk, name):
        self.sems = [Sem(mk.es.enter_context(mk.nc.semaphore(f"{name}_a"))),
                     Sem(mk.es.enter_context(mk.nc.semaphore(f"{name}_b")))]
        mk.all_sems.append(self.sems[0])
        mk.all_sems.append(self.sems[1])
        self.n = 0
        self.vals = {}

    def begin_item(self):
        return self.sems[self.n % 2]

    def commit(self):
        slot = self.n % 2
        self.vals[self.n] = (self.sems[slot], self.sems[slot].n)
        self.n += 1
        return self.n - 1

    def mark(self, instr):
        self.begin_item().inc(instr, 16)
        return self.commit()

    def wait(self, eng, i):
        so, v = self.vals[i]
        eng.wait_ge(so.h, v)


class PsumPool:

    def __init__(self, mk, bank_ids):
        self.mk = mk
        self.banks = bank_ids
        self.u = 0
        self.done = {}

    def begin(self, ten):
        u = self.u
        prev = self.done.get(u - len(self.banks))
        if prev is not None:
            ten.wait_ge(prev[0].h, prev[1])
        return u, self.mk.pb[self.banks[u % len(self.banks)]]

    def took(self, u):
        self.u = u + 1

    def done_mark(self, sem, val):
        self.done[self.u - 1] = (sem, val)

    def done_mark_u(self, u, sem, val):
        self.done[u] = (sem, val)

    def bank(self, u):
        return self.mk.pb[self.banks[u % len(self.banks)]]


class MK:
    def __init__(self):
        nc = bass.Bass("TRN2", target_bir_lowering=False, debug=False)
        self.nc = nc
        self.es = ExitStack()
        es = self.es

        def din(name, shape, dt=F32):
            return nc.dram_tensor(name, shape, dt, kind="ExternalInput").ap()

        self.x0 = din("x", [T, H])
        self.wq = din("wq", [H, H])
        self.wk = din("wk", [H, H])
        self.wv = din("wv", [H, H])
        self.wb = din("wb", [NH, H])
        self.wo = din("wo", [H, H])
        self.cqw = din("cq", [H, 4])
        self.ckw = din("ck", [H, 4])
        self.cvw = din("cv", [H, 4])
        self.rmsw = din("rmsw", [HD, 1])
        self.gw = din("gw", [NE, H])
        self.gup = din("gup", [NE * 2 * FI, H])
        self.dwn = din("dwn", [NE * H, FI])
        self.mg = din("mg", [NE, 1], I32)
        self.dom = din("dom", [NG, 1], I32)
        self.flag = din("flag", [128, 1])
        self.sel8 = din("sel8", [NG, NG * 128])
        self.out = nc.dram_tensor("out", [T, H], F32, kind="ExternalOutput").ap()

        dr = lambda name, shape: nc.dram_tensor(name, shape, F32).ap()
        self.x1s = dr("x1s", [T, H])
        self.x2s = dr("x2s", [T, H])
        drb = lambda name, shape: nc.dram_tensor(name, shape, mybir.dt.bfloat16).ap()
        self.gut_h = drb("gut_h", [NG, NJ, 128, H])
        self.gut_l = drb("gut_l", [NG, NJ, 128, H])
        self.dnt_h = drb("dnt_h", [NG, NHC, 128, FI])
        self.dnt_l = drb("dnt_l", [NG, NHC, 128, FI])
        self.wot = dr("wot", [NHC, 128, H])
        self.qs = dr("qs_s", [NCH, NHC, 128, C])
        self.ks = dr("ks_s", [NCH, NHC, 128, C])
        self.vs = dr("vs_s", [NCH, NHC, 128, C])
        self.loc = dr("loc", [NCH, NH, 5, 128, 128])
        self.locr = dr("locr", [NCH, 128, NH])
        self.cc_s_in = dr("cc_s_in", [128, NH * HD])
        self.cc_s_out = dr("cc_s_out", [256, NH * HD])
        self.cc_x_in = dr("cc_x_in", [3, H])
        self.cc_x_out = dr("cc_x_out", [6, H])

        self.all_sems = []

        def sem(name):
            s = Sem(es.enter_context(nc.semaphore(name)))
            self.all_sems.append(s)
            return s
        self.qw = sem("qw")
        self.qg = sem("qg")
        self.qc = sem("qc")
        self.tm = sem("tm")
        self.vm = sem("vm")
        self.sm = sem("sm")
        self.gm = sem("gm")

        sb = lambda name, shape, dt=F32: es.enter_context(nc.sbuf_tensor(name, shape, dt))
        self.ident = sb("ident", [128, 128])
        self.stril = sb("stril", [128, 128])
        self.tril0 = sb("tril0", [128, 128])
        self.ones_row = sb("ones_row", [1, 128])
        self.cw = {p: sb(f"cw_{p}", [128, 4 * NHC]) for p in "qkv"}
        self.rmsw_sb = sb("rmsw_sb", [128, 1])
        self.flag_sb = sb("flag_sb", [128, 1])
        self.gwT = sb("gwT", [128, NE * NHC])
        self.mgrp = sb("mgrp", [NE, NG])
        self.mg_sb = sb("mg_sb", [NE, 1], I32)
        self.mgf = sb("mgf", [NE, 1])
        self.dom_sb = sb("dom_sb", [NG, 1], I32)
        self.iota8 = sb("iota8", [NE, NG])
        self._sel8_sb = sb("sel8_sb", [NG, NG * 128])
        self.beta_sb = sb("beta_sb", [NH, T])
        self.SA = sb("SA", [128, NH * HD])
        self.SB = sb("SB", [128, NH * HD])
        self.tril0w = sb("tril0w", [128, 512])
        self.trash = sb("trash", [128, H])
        self.ln1 = sb("ln_s1", [128, 1])
        self.ln2 = sb("ln_s2", [128, 1])

        self.pb = [es.enter_context(nc.psum_tensor(f"pb{i}", [128, 512], F32))
                   for i in range(8)]
        self.block = es.enter_context(nc.Block())

    def t(self, f): self.block.tensor(f)

    def v(self, f):
        def w(vec):
            if self.vm.n:
                vec.wait_ge(self.vm.h, self.vm.n)
            f(vec)
        self.block.vector(w)

    def s(self, f):
        def w(sc):
            if self.sm.n:
                sc.wait_ge(self.sm.h, self.sm.n)
            f(sc)
        self.block.scalar(w)

    def g(self, f):
        def w(gp):
            if self.gm.n:
                gp.wait_ge(self.gm.h, self.gm.n)
            f(gp)
        self.block.gpsimd(w)

    def y(self, f): self.block.sync(f)

    def emit_ln(self, xa, slot, stream, item):
        def vln(vec):
            stream.wait(vec, item)
            xt = xa[:, slot * H:(slot + 1) * H]
            self.dve_sync(vec, vec.reduce_sum(self.ln1[:, :], xt, axis=AX.X))
            self.dve_sync(vec, vec.tensor_scalar(self.ln1[:, :], self.ln1[:, :],
                                                 1.0 / H, None, op0=OP.mult))
            self.vm.inc(vec.tensor_scalar(xt, xt, self.ln1[:, 0:1], None,
                                          op0=OP.subtract))
        self.v(vln)
        vmv = self.vm.n

        def sln(sc, vmv=vmv):
            sc.wait_ge(self.vm.h, vmv)
            self.sm.inc(sc.activation(self.trash[:, :], xa[:, slot * H:(slot + 1) * H],
                                      AF.Square, accum_out=self.ln2[:, 0:1]))
        self.s(sln)
        smv = self.sm.n

        def vv(vec, smv=smv):
            vec.wait_ge(self.sm.h, smv)
            self.dve_sync(vec, vec.tensor_scalar(self.ln2[:, :], self.ln2[:, :],
                                                 1.0 / H, None, op0=OP.mult))
            self.vm.inc(vec.tensor_scalar(self.ln2[:, :], self.ln2[:, :],
                                          LN_EPS, None, op0=OP.add))
        self.v(vv)
        vmv2 = self.vm.n

        def ss(sc, vmv2=vmv2):
            sc.wait_ge(self.vm.h, vmv2)
            self.sm.inc(sc.activation(self.ln2[:, :], self.ln2[:, :], AF.Sqrt))
        self.s(ss)
        smv2 = self.sm.n

        def vv2(vec, smv2=smv2):
            sc_ = None
            vec.wait_ge(self.sm.h, smv2)
            self.dve_sync(vec, vec.reciprocal(self.ln2[:, :], self.ln2[:, :]))
            xt = xa[:, slot * H:(slot + 1) * H]
            self.vm.inc(vec.tensor_scalar(xt, xt, self.ln2[:, 0:1], None, op0=OP.mult))
        self.v(vv2)

    def dve_sync(self, vec, instr):
        val = self.vm.inc(instr)
        vec.wait_ge(self.vm.h, val)

    def tr(self, eng, out_ps, in_ap):
        p = in_ap.shape[0]
        bp = in_ap.base_partition()
        return eng.transpose(out_ps, in_ap, self.ident[bp:bp + p, bp:bp + p])

    def phase_barrier(self):
        marks = [(s, s.n) for s in self.all_sems]

        def w(eng):
            for s, v in marks:
                if v > 0:
                    eng.wait_ge(s.h, v)
        self.t(w)
        self.v(w)
        self.s(w)
        self.g(w)
        self.y(w)

    def p0(self):
        nc = self.nc

        def g0(gp):
            gp.memset(self.ident[:, :], 0.0).then_inc(self.gm.h, 1)
            self.gm.n += 1
            gp.memset(self.stril[:, :], 1.0).then_inc(self.gm.h, 1)
            self.gm.n += 1
            gp.memset(self.tril0[:, :], 1.0).then_inc(self.gm.h, 1)
            self.gm.n += 1
            gp.wait_ge(self.gm.h, self.gm.n)
            gp.affine_select(out=self.ident[:, :], in_=self.ident[:, :],
                             compare_op=OP.not_equal, fill=1.0, base=0,
                             pattern=[[-1, 128]], channel_multiplier=1)
            gp.affine_select(out=self.stril[:, :], in_=self.stril[:, :],
                             compare_op=OP.is_gt, fill=0.0, base=0,
                             pattern=[[-1, 128]], channel_multiplier=1)
            gp.affine_select(out=self.tril0[:, :], in_=self.tril0[:, :],
                             compare_op=OP.is_ge, fill=0.0, base=0,
                             pattern=[[-1, 128]], channel_multiplier=1
                             ).then_inc(self.gm.h, 1)
            self.gm.n += 1
            gp.wait_ge(self.gm.h, self.gm.n)
            for _j in range(4):
                gp.tensor_copy(self.tril0w[:, _j * 128:(_j + 1) * 128], self.tril0[:, :])
            gp.memset(self.ones_row[:, :], 1.0)
            gp.iota(self.iota8[:, :], pattern=[[1, NG]], base=0,
                    channel_multiplier=0, allow_small_or_imprecise_dtypes=True)
            gp.memset(self.SA[:, :], 0.0)
            gp.dma_start(out=self.rmsw_sb[:HD, :], in_=self.rmsw).then_inc(self.qg.h, 16)
            gp.dma_start(out=self.flag_sb[:, :], in_=self.flag).then_inc(self.qg.h, 16)
            gp.dma_start(out=self.mg_sb[:, :], in_=self.mg).then_inc(self.qg.h, 16)
            gp.dma_start(out=self.dom_sb[:, :], in_=self.dom).then_inc(self.qg.h, 16)
            gp.dma_start(out=self._sel8_sb[:, :], in_=self.sel8).then_inc(self.qg.h, 16)
            self.qg.n += 80
            gp.wait_ge(self.qg.h, self.qg.n)
            self.gm.inc(gp.nop())
        self.g(g0)
        gm0 = self.gm.n

        def v0(vec):
            vec.wait_ge(self.gm.h, gm0)
            cp = vec.tensor_copy(self.mgf[:, :], self.mg_sb[:, :])
            v = self.vm.inc(cp)
            vec.wait_ge(self.vm.h, v)
            self.vm.inc(vec.tensor_tensor(self.mgrp[:, :], self.iota8[:, :],
                                          self.mgf[:, 0:1].to_broadcast([NE, NG]),
                                          op=OP.is_equal))
        self.v(v0)

        p0es = ExitStack()
        gw_raw = p0es.enter_context(nc.sbuf_tensor("gw_raw", [NE, H], F32))

        def y0(sync):
            for p, src in (("q", self.cqw), ("k", self.ckw), ("v", self.cvw)):
                for oc in range(NHC):
                    sync.dma_start(out=self.cw[p][:, oc * 4:(oc + 1) * 4],
                                   in_=src[oc * 128:(oc + 1) * 128, :]).then_inc(self.qw.h, 16)
                    self.qw.n += 16
            sync.dma_start(out=gw_raw[:, :], in_=self.gw).then_inc(self.qw.h, 16)
            self.qw.n += 16
            sync.wait_ge(self.qw.h, self.qw.n)
        self.y(y0)

        qw0, gm1 = self.qw.n, self.gm.n
        prev_vm = [None]
        for k in range(NHC):
            def t0(ten, k=k):
                if k == 0:
                    ten.wait_ge(self.qw.h, qw0)
                    ten.wait_ge(self.gm.h, gm1)
                else:
                    ten.wait_ge(self.vm.h, prev_vm[0])
                self.tm.inc(self.tr(ten, self.pb[6][:, 0:NE], gw_raw[:, k * 128:(k + 1) * 128]))
            self.t(t0)
            tmv = self.tm.n

            def v1(vec, k=k, tmv=tmv):
                vec.wait_ge(self.tm.h, tmv)
                cp = vec.tensor_copy(self.gwT[:, k * NE:(k + 1) * NE], self.pb[6][:, 0:NE])
                prev_vm[0] = self.vm.inc(cp)
            self.v(v1)

        BF16 = mybir.dt.bfloat16
        raw_gu = p0es.enter_context(nc.sbuf_tensor("p0_rawgu", [128, 2 * H], F32))
        stg_gu = p0es.enter_context(nc.sbuf_tensor("p0_stggu", [128, 2 * H], F32))
        raw_dn = p0es.enter_context(nc.sbuf_tensor("p0_rawdn", [128, 2 * FI], F32))
        stg_dn = p0es.enter_context(nc.sbuf_tensor("p0_stgdn", [128, 2 * FI], F32))
        hst_gu = p0es.enter_context(nc.sbuf_tensor("p0_hgu", [128, 2 * H], BF16))
        lst_gu = p0es.enter_context(nc.sbuf_tensor("p0_lgu", [128, 2 * H], BF16))
        hst_dn = p0es.enter_context(nc.sbuf_tensor("p0_hdn", [128, 2 * FI], BF16))
        lst_dn = p0es.enter_context(nc.sbuf_tensor("p0_ldn", [128, 2 * FI], BF16))

        slabs = ([("gu", gi, j) for gi in range(NG) for j in range(NJ)]
                 + [("dn", gi, f) for gi in range(NG) for f in range(NHC)]
                 + [("wo", osl, 0) for osl in range(NHC)])
        geom = {
            "gu": dict(raw=raw_gu, stg=stg_gu, hst=hst_gu, lst=lst_gu, w=H, nt=16, buf="A"),
            "dn": dict(raw=raw_dn, stg=stg_dn, hst=hst_dn, lst=lst_dn, w=FI, nt=6, buf="B"),
            "wo": dict(raw=raw_gu, stg=stg_gu, hst=None, lst=None, w=H, nt=16, buf="A"),
        }
        buf_use = {"A": 0, "B": 0}
        slab_bufuse = {}
        bufuse_to_slab = {}
        for n, (kind, a, b) in enumerate(slabs):
            bk = geom[kind]["buf"]
            u = buf_use[bk]
            buf_use[bk] = u + 1
            slab_bufuse[n] = u
            bufuse_to_slab[(bk, u)] = n
        tm_base = self.tm.n
        qg_base = self.qg.n
        load_v = {n: qg_base + 16 * (n + 1) for n in range(len(slabs))}
        done_marks = {}

        gug = self.nc.dram_tensor("gug", [NG * 2 * FI, H], F32).ap()
        dng = self.nc.dram_tensor("dng", [NG * H, FI], F32).ap()

        def ggather(gp):
            with gp.register("eo_p0") as eo:
                for gi in range(NG):
                    gp.reg_load(eo, self.dom_sb[gi:gi + 1, 0:1])
                    gp.reg_mul(eo, eo, 2 * FI * H)
                    gp.dma_start(out=gug[gi * 2 * FI:(gi + 1) * 2 * FI, :],
                                 in_=bass.AP(self.gup.tensor, eo,
                                             [[H, 2 * FI], [1, H]])
                                 ).then_inc(self.qg.h, 16)
                    self.qg.n += 16
                for gi in range(NG):
                    gp.reg_load(eo, self.dom_sb[gi:gi + 1, 0:1])
                    gp.reg_mul(eo, eo, H * FI)
                    gp.dma_start(out=dng[gi * H:(gi + 1) * H, :],
                                 in_=bass.AP(self.dwn.tensor, eo,
                                             [[FI, H], [1, FI]])
                                 ).then_inc(self.qg.h, 16)
                    self.qg.n += 16
        gath_base = self.qg.n
        self.g(ggather)
        assert self.qg.n == gath_base + 16 * 16
        gath_total = self.qg.n
        stA = DmaStream(self, "p0A")
        stB = DmaStream(self, "p0B")
        streams = {"A": stA, "B": stB}
        stores = {"A": DmaStream(self, "p0sA"), "B": DmaStream(self, "p0sB")}
        store_item = {}
        load_item = {}

        def emit_slab_load(gp, n):
            kind, a, b = slabs[n]
            gg = geom[kind]
            st = streams[gg["buf"]]
            u = slab_bufuse[n]
            slot = u % 2
            prev_slab = bufuse_to_slab.get((gg["buf"], u - 2))
            if prev_slab is not None:
                gp.wait_ge(self.tm.h, tm_base + prev_slab + 1)
            dst = gg["raw"][:, slot * gg["w"]:(slot + 1) * gg["w"]]
            if kind == "gu":
                r0 = a * 2 * FI + b * 128
                dd = gp.dma_start(out=dst, in_=gug[r0:r0 + 128, :])
            elif kind == "dn":
                r0 = a * H + b * 128
                dd = gp.dma_start(out=dst, in_=dng[r0:r0 + 128, :])
            else:
                dd = gp.dma_start(out=dst, in_=self.wo[a * 128:(a + 1) * 128, :])
            load_item[n] = (st, st.mark(dd))

        def g_init(gp):
            gp.wait_ge(self.qg.h, gath_total)
            emit_slab_load(gp, 0)
            emit_slab_load(gp, 1)
        self.g(g_init)

        for n in range(len(slabs)):
            kind, a, b = slabs[n]
            gg = geom[kind]
            u = slab_bufuse[n]
            slot = u % 2
            nt = gg["nt"]
            ngrp = (nt + 3) // 4

            def tt(ten, n=n, gg=gg, slot=slot, nt=nt, ngrp=ngrp, u=u):
                st_, it_ = load_item[n]
                st_.wait(ten, it_)
                prev = done_marks.get(n - 2)
                if prev is not None:
                    for so, val in prev:
                        ten.wait_ge(so.h, val)
                src = gg["raw"][:, slot * gg["w"]:(slot + 1) * gg["w"]]
                boff = (n % 2) * ngrp
                for grp in range(ngrp):
                    bank = self.pb[boff + grp]
                    for q in range(min(4, nt - grp * 4)):
                        k = grp * 4 + q
                        mm = self.tr(ten, bank[:, q * 128:(q + 1) * 128],
                                     src[:, k * 128:(k + 1) * 128])
                self.tm.inc(mm)
                assert self.tm.n == tm_base + n + 1
            self.t(tt)
            tmv = self.tm.n
            marks = []

            prev_store = store_item.get((gg["buf"], u - 2))

            boff_n = (n % 2) * ngrp

            def emit_v(vec, kind=kind, gg=gg, slot=slot, nt=nt, ngrp=ngrp, tmv=tmv,
                       prev_store=prev_store, boff=boff_n):
                vec.wait_ge(self.tm.h, tmv)
                if prev_store is not None:
                    vec.wait_ge(prev_store[0].h, prev_store[1])
                cp = None
                for grp in range(ngrp):
                    bank = self.pb[boff + grp]
                    w = min(4, nt - grp * 4) * 128
                    o0 = slot * gg["w"] + grp * 512
                    if kind == "wo":
                        cp = vec.tensor_scalar(gg["stg"][:, o0:o0 + w], bank[:, :w],
                                               self.rmsw_sb[:, 0:1], None, op0=OP.mult)
                    else:
                        self.dve_sync(vec, vec.tensor_copy(gg["hst"][:, o0:o0 + w],
                                                           bank[:, :w]))
                        cp = vec.tensor_tensor(gg["stg"][:, o0:o0 + w], bank[:, :w],
                                               gg["hst"][:, o0:o0 + w], op=OP.subtract)
                marks.append((self.vm, self.vm.inc(cp)))

            def emit_s(sc, kind=kind, gg=gg, slot=slot, nt=nt, vmark=None):
                sc.wait_ge(vmark[0].h, vmark[1])
                o0 = slot * gg["w"]
                w = nt * 128
                cp = sc.copy(gg["lst"][:, o0:o0 + w], gg["stg"][:, o0:o0 + w])
                marks.append((self.sm, self.sm.inc(cp)))
            self.v(emit_v)
            if kind != "wo":
                vmark = marks[-1]
                self.s(lambda sc: emit_s(sc, vmark=vmark))

            def gsec2(gp, n=n):
                if n + 2 < len(slabs):
                    emit_slab_load(gp, n + 2)
            self.g(gsec2)
            done_marks[n] = list(marks)

            def yw(sync, kind=kind, a=a, b=b, gg=gg, slot=slot, marks=tuple(marks), u=u):
                for so, val in marks:
                    sync.wait_ge(so.h, val)
                sl = slice(slot * gg["w"], (slot + 1) * gg["w"])
                ssem = stores[gg["buf"]].begin_item()
                if kind == "wo":
                    ssem.inc(sync.dma_start(out=self.wot[a], in_=gg["stg"][:, sl]), 16)
                else:
                    dst_h = self.gut_h[a, b] if kind == "gu" else self.dnt_h[a, b]
                    dst_l = self.gut_l[a, b] if kind == "gu" else self.dnt_l[a, b]
                    ssem.inc(sync.dma_start(out=dst_h, in_=gg["hst"][:, sl]), 16)
                    ssem.inc(sync.dma_start(out=dst_l, in_=gg["lst"][:, sl]), 16)
                it = stores[gg["buf"]].commit()
                store_item[(gg["buf"], u)] = stores[gg["buf"]].vals[it]
            self.y(yw)
        p0es.close()

    def ffn(self, x_src, x_dst, tag):
        nc = self.nc
        es = ExitStack()
        BF16 = mybir.dt.bfloat16
        xlnT_h = es.enter_context(nc.sbuf_tensor(f"xlnTh_{tag}", [128, NHC * TBLK], BF16))
        xlnT_l = es.enter_context(nc.sbuf_tensor(f"xlnTl_{tag}", [128, NHC * TBLK], BF16))
        oacc = es.enter_context(nc.sbuf_tensor(f"oacc_{tag}", [128, NHC * TBLK], F32))
        act_h = es.enter_context(nc.sbuf_tensor(f"acth_{tag}", [128, NI * TBLK], BF16))
        act_l = es.enter_context(nc.sbuf_tensor(f"actl_{tag}", [128, NI * TBLK], BF16))
        tmpf = es.enter_context(nc.sbuf_tensor(f"tmpf_{tag}", [128, TBLK], F32))
        xg32 = es.enter_context(nc.sbuf_tensor(f"xg32_{tag}", [128, NHC * 128], F32))
        s_t = es.enter_context(nc.sbuf_tensor(f"s_t_{tag}", [128, TBLK], F32))
        wbc = es.enter_context(nc.sbuf_tensor(f"wbc_{tag}", [128, TBLK], F32))
        xa = es.enter_context(nc.sbuf_tensor(f"xa_{tag}", [128, 2 * H], F32))
        xo = es.enter_context(nc.sbuf_tensor(f"xo_{tag}", [128, 2 * H], F32))
        gub_h = es.enter_context(nc.sbuf_tensor(f"gubh_{tag}", [128, 2 * H], BF16))
        gub_l = es.enter_context(nc.sbuf_tensor(f"gubl_{tag}", [128, 2 * H], BF16))
        dnb_h = es.enter_context(nc.sbuf_tensor(f"dnbh_{tag}", [128, 2 * FI], BF16))
        dnb_l = es.enter_context(nc.sbuf_tensor(f"dnbl_{tag}", [128, 2 * FI], BF16))
        wrows = es.enter_context(nc.sbuf_tensor(f"wrows_{tag}", [NG, TBLK], F32))
        gsc = es.enter_context(nc.sbuf_tensor(f"gsc_{tag}", [128, 5 * NE], F32))
        gs1 = es.enter_context(nc.sbuf_tensor(f"gs1_{tag}", [128, 4], F32))
        m8 = es.enter_context(nc.sbuf_tensor(f"m8_{tag}", [128, 8], F32))
        lgf = es.enter_context(nc.sbuf_tensor(f"lgf_{tag}", [NE, 128], F32))
        wtg = es.enter_context(nc.sbuf_tensor(f"wtg_{tag}", [128, NG], F32))

        seq = []
        for tb in range(NBLK):
            for gi in range(NG):
                for jp in range(NI):
                    seq.append(("gu", gi, jp))
                    seq.append(("gu", gi, jp + NI))
                for f in range(NHC):
                    seq.append(("dn", gi, f))

        kctr = {"gu": 0, "dn": 0}
        kuse = {"gu": 0, "dn": 0}
        cons_v = {}
        st_slab = {"gu": DmaStream(self, f"f{tag}_gu"), "dn": DmaStream(self, f"f{tag}_dn")}

        def emit_slab_load(idx):
            kind, gi, j = seq[idx]
            ki = kctr[kind]
            kctr[kind] = ki + 1
            slot = ki % 2
            cred = cons_v.get((kind, ki - 2))
            bh, bl, w = ((gub_h, gub_l, H) if kind == "gu" else (dnb_h, dnb_l, FI))
            src_h = self.gut_h[gi, j] if kind == "gu" else self.dnt_h[gi, j]
            src_l = self.gut_l[gi, j] if kind == "gu" else self.dnt_l[gi, j]

            def yl(sync, bh=bh, bl=bl, w=w, src_h=src_h, src_l=src_l, slot=slot,
                   cred=cred, kind=kind, ki=ki):
                if cred is not None:
                    sync.wait_ge(self.tm.h, cred)
                ssem = st_slab[kind].begin_item()
                ssem.inc(sync.dma_start(out=bh[:, slot * w:(slot + 1) * w], in_=src_h), 16)
                ssem.inc(sync.dma_start(out=bl[:, slot * w:(slot + 1) * w], in_=src_l), 16)
                it = st_slab[kind].commit()
                assert it == ki
            self.y(yl)

        pbu = {"n": 0}
        evac_done = {}

        def bank_begin(ten):
            u = pbu["n"]
            prev = evac_done.get(u - 2)
            if prev is not None:
                ten.wait_ge(prev[0].h, prev[1])
            return u, self.pb[u % 2]

        def bank_end(u):
            pbu["n"] = u + 1

        tpbu = {"n": 0}
        tevac_done = {}

        def tbank_begin(ten):
            u = tpbu["n"]
            prev = tevac_done.get(u - 4)
            if prev is not None:
                ten.wait_ge(prev[0].h, prev[1])
            return u, self.pb[2 + (u % 4)]

        def tbank_end(u):
            tpbu["n"] = u + 1

        gpbu = {"n": 0}
        gevac_done = {}

        def gbank_begin(ten):
            u = gpbu["n"]
            prev = gevac_done.get(u - 1)
            if prev is not None:
                ten.wait_ge(prev[0].h, prev[1])
            return u, self.pb[6]

        def gbank_end(u):
            gpbu["n"] = u + 1

        def gmark(val):
            gevac_done[gpbu["n"] - 1] = (self.vm, val)

        xa_credit = {}
        xo_credit = {}
        st_xa = DmaStream(self, f"f{tag}_xa")
        st_xo = DmaStream(self, f"f{tag}_xo")
        st_xs = DmaStream(self, f"f{tag}_xs")
        last_ep_tm = [None]
        silu_done = [0]
        up_done = [0]
        seq_pos = [0]
        emit_slab_load(0)
        emit_slab_load(1)

        for tb in range(NBLK):
            for tc in range(4):
                slot = tc % 2
                cred = xa_credit.get(tb * 4 + tc - 2)

                def yx(sync, tb=tb, tc=tc, slot=slot, cred=cred):
                    if cred is not None:
                        sync.wait_ge(cred[0].h, cred[1])
                    r0 = tb * TBLK + tc * 128
                    dd = sync.dma_start(out=xa[:, slot * H:(slot + 1) * H],
                                        in_=x_src[r0:r0 + 128, :])
                    st_xa.mark(dd)
                self.y(yx)
                xa_item = st_xa.n - 1

                def vln(vec, slot=slot, xa_item=xa_item):
                    st_xa.wait(vec, xa_item)
                    xt = xa[:, slot * H:(slot + 1) * H]
                    self.dve_sync(vec, vec.reduce_sum(self.ln1[:, :], xt, axis=AX.X))
                    self.dve_sync(vec, vec.tensor_scalar(self.ln1[:, :], self.ln1[:, :],
                                                         1.0 / H, None, op0=OP.mult))
                    self.vm.inc(vec.tensor_scalar(xt, xt, self.ln1[:, 0:1], None,
                                                  op0=OP.subtract))
                self.v(vln)
                vmv = self.vm.n

                def sln(sc, slot=slot, vmv=vmv):
                    sc.wait_ge(self.vm.h, vmv)
                    self.sm.inc(sc.activation(self.trash[:, :], xa[:, slot * H:(slot + 1) * H],
                                              AF.Square, accum_out=self.ln2[:, 0:1]))
                self.s(sln)
                smv = self.sm.n

                def vv(vec, smv=smv):
                    vec.wait_ge(self.sm.h, smv)
                    self.dve_sync(vec, vec.tensor_scalar(self.ln2[:, :], self.ln2[:, :],
                                                         1.0 / H, None, op0=OP.mult))
                    self.vm.inc(vec.tensor_scalar(self.ln2[:, :], self.ln2[:, :],
                                                  LN_EPS, None, op0=OP.add))
                self.v(vv)
                vmv = self.vm.n

                def ss(sc, vmv=vmv):
                    sc.wait_ge(self.vm.h, vmv)
                    self.sm.inc(sc.activation(self.ln2[:, :], self.ln2[:, :], AF.Sqrt))
                self.s(ss)
                smv = self.sm.n

                def vv2(vec, slot=slot, smv=smv):
                    vec.wait_ge(self.sm.h, smv)
                    self.dve_sync(vec, vec.reciprocal(self.ln2[:, :], self.ln2[:, :]))
                    xt = xa[:, slot * H:(slot + 1) * H]
                    self.vm.inc(vec.tensor_scalar(xt, xt, self.ln2[:, 0:1], None,
                                                  op0=OP.mult))
                self.v(vv2)
                v_ln = self.vm.n

                tgrp = []
                for grp in range(4):
                    def tx(ten, grp=grp, slot=slot, v_ln=v_ln):
                        u, bank = tbank_begin(ten)
                        if grp == 0:
                            ten.wait_ge(self.vm.h, v_ln)
                        for q in range(4):
                            k = grp * 4 + q
                            mm = self.tr(ten, bank[:, q * 128:(q + 1) * 128],
                                         xa[:, slot * H + k * 128:slot * H + (k + 1) * 128])
                        self.tm.inc(mm)
                        tgrp.append((u, bank, self.tm.n))
                        tbank_end(u)
                    self.t(tx)
                xa_credit[tb * 4 + tc] = (self.tm, tgrp[3][2])

                ev_marks = []
                for grp in range(4):
                    u, bank, tmv = tgrp[grp]

                    def ev(vec, grp=grp, bank=bank, tmv=tmv, tc=tc, u=u):
                        vec.wait_ge(self.tm.h, tmv)
                        for q in range(4):
                            k = grp * 4 + q
                            o0 = k * TBLK + tc * 128
                            bsl = bank[:, q * 128:(q + 1) * 128]
                            self.dve_sync(vec, vec.tensor_copy(
                                xlnT_h[:, o0:o0 + 128], bsl))
                            self.dve_sync(vec, vec.tensor_tensor(
                                self.trash[:, q * 128:(q + 1) * 128], bsl,
                                xlnT_h[:, o0:o0 + 128], op=OP.subtract))
                            self.dve_sync(vec, vec.tensor_copy(
                                xlnT_l[:, o0:o0 + 128],
                                self.trash[:, q * 128:(q + 1) * 128]))
                            cp = vec.tensor_copy(xg32[:, k * 128:(k + 1) * 128], bsl)
                        val = self.vm.inc(cp)
                        ev_marks.append((self.vm, val))
                        tevac_done[u] = (self.vm, val)
                    self.v(ev)

                def tlg(ten, tc=tc, ev_marks=tuple(ev_marks)):
                    u, bank = gbank_begin(ten)
                    for so, val in ev_marks:
                        ten.wait_ge(so.h, val)
                    for k in range(NHC):
                        mm = ten.matmul(bank[:NE, 0:128],
                                        self.gwT[:, k * NE:(k + 1) * NE],
                                        xg32[:, k * 128:(k + 1) * 128],
                                        start=(k == 0), stop=(k == NHC - 1))
                    self.tm.inc(mm)
                    gbank_end(u)
                self.t(tlg)
                tmv = self.tm.n

                def vlg(vec, tmv=tmv):
                    vec.wait_ge(self.tm.h, tmv)
                    gmark(self.vm.inc(vec.tensor_copy(lgf[:, :], self.pb[6][:NE, 0:128])))
                self.v(vlg)
                vmv = self.vm.n

                def tlg2(ten, vmv=vmv):
                    u, bank = gbank_begin(ten)
                    ten.wait_ge(self.vm.h, vmv)
                    self.tm.inc(self.tr(ten, bank[:, 128:128 + NE], lgf[:, :]))
                    gbank_end(u)
                self.t(tlg2)
                tmv = self.tm.n

                def vsm(vec, tmv=tmv):
                    vec.wait_ge(self.tm.h, tmv)
                    cp = vec.tensor_copy(gsc[:, 0:NE], self.pb[6][:, 128:128 + NE])
                    gmark(self.vm.inc(cp))
                    vec.wait_ge(self.vm.h, self.vm.n)
                    self.dve_sync(vec, vec.reduce_max(gs1[:, 0:1], gsc[:, 0:NE], axis=AX.X))
                    self.vm.inc(vec.tensor_scalar(gsc[:, 0:NE], gsc[:, 0:NE], gs1[:, 0:1],
                                                  None, op0=OP.subtract))
                self.v(vsm)
                vmv = self.vm.n

                def sexp(sc, vmv=vmv):
                    sc.wait_ge(self.vm.h, vmv)
                    self.sm.inc(sc.activation(gsc[:, NE:2 * NE], gsc[:, 0:NE], AF.Exp))
                self.s(sexp)
                smv = self.sm.n

                def vtop(vec, smv=smv):
                    vec.wait_ge(self.sm.h, smv)
                    pex = gsc[:, NE:2 * NE]
                    probs = gsc[:, 2 * NE:3 * NE]
                    self.dve_sync(vec, vec.reduce_sum(gs1[:, 1:2], pex, axis=AX.X))
                    self.dve_sync(vec, vec.reciprocal(gs1[:, 1:2], gs1[:, 1:2]))
                    self.dve_sync(vec, vec.tensor_scalar(probs, pex, gs1[:, 1:2], None,
                                                         op0=OP.mult))
                    mx8 = vec.max(out=m8[:, :], in_=probs)
                    v = self.vm.inc(mx8)
                    for _ in range(8):
                        vec.engine_nop()
                    vec.wait_ge(self.vm.h, v)
                    self.dve_sync(vec, vec.reduce_sum(gs1[:, 2:3], m8[:, :], axis=AX.X))
                    self.dve_sync(vec, vec.reciprocal(gs1[:, 2:3], gs1[:, 2:3]))
                    mask = gsc[:, 3 * NE:4 * NE]
                    self.dve_sync(vec, vec.tensor_tensor(
                        mask, probs, m8[:, 7:8].to_broadcast([128, NE]), op=OP.is_ge))
                    pm = gsc[:, 4 * NE:5 * NE]
                    self.dve_sync(vec, vec.tensor_tensor(pm, probs, mask, op=OP.mult))
                    self.vm.inc(vec.tensor_scalar(pm, pm, gs1[:, 2:3], None, op0=OP.mult))
                self.v(vtop)
                vmv = self.vm.n

                def tpm(ten, vmv=vmv):
                    u, bank = gbank_begin(ten)
                    ten.wait_ge(self.vm.h, vmv)
                    self.tm.inc(self.tr(ten, bank[:NE, 0:128], gsc[:, 4 * NE:5 * NE]))
                    gbank_end(u)
                self.t(tpm)
                tmv = self.tm.n

                def vpm(vec, tmv=tmv):
                    vec.wait_ge(self.tm.h, tmv)
                    gmark(self.vm.inc(vec.tensor_copy(lgf[:, :], self.pb[6][:NE, 0:128])))
                self.v(vpm)
                vmv = self.vm.n

                def twtg(ten, vmv=vmv):
                    u, bank = gbank_begin(ten)
                    ten.wait_ge(self.vm.h, vmv)
                    mm = ten.matmul(bank[:, 128:128 + NG], lgf[:, :], self.mgrp[:, :],
                                    start=True, stop=True)
                    self.tm.inc(mm)
                    gbank_end(u)
                self.t(twtg)
                tmv = self.tm.n

                def vwtg(vec, tmv=tmv):
                    vec.wait_ge(self.tm.h, tmv)
                    gmark(self.vm.inc(vec.tensor_copy(wtg[:, :], self.pb[6][:, 128:128 + NG])))
                self.v(vwtg)
                vmv = self.vm.n

                def twr(ten, vmv=vmv):
                    u, bank = gbank_begin(ten)
                    ten.wait_ge(self.vm.h, vmv)
                    self.tm.inc(self.tr(ten, bank[:NG, 0:128], wtg[:, :]))
                    gbank_end(u)
                self.t(twr)
                tmv = self.tm.n

                def vwr(vec, tmv=tmv, tc=tc):
                    vec.wait_ge(self.tm.h, tmv)
                    cp = vec.tensor_copy(wrows[:, tc * 128:(tc + 1) * 128],
                                         self.pb[6][:NG, 0:128])
                    gmark(self.vm.inc(cp))
                self.v(vwr)

            wrows_ready = self.vm.n

            for gi in range(NG):
                def twb(ten, gi=gi, wrows_ready=wrows_ready):
                    u, bank = gbank_begin(ten)
                    if gi == 0:
                        ten.wait_ge(self.vm.h, wrows_ready)
                    mm = ten.matmul(bank[:, 0:TBLK],
                                    self.sel8[gi] if False else self._sel8_sb[:, gi * 128:(gi + 1) * 128],
                                    wrows[0:NG, :], start=True, stop=True)
                    self.tm.inc(mm)
                    gbank_end(u)
                self.t(twb)
                tmv = self.tm.n

                def vwb(vec, tmv=tmv, gi=gi):
                    vec.wait_ge(self.tm.h, tmv)
                    if gi > 0:
                        vec.wait_ge(self.vm.h, up_done[0])
                    gmark(self.vm.inc(vec.tensor_copy(wbc[:, :], self.pb[6][:, 0:TBLK])))
                self.v(vwb)
                wbc_ready = self.vm.n

                for jp in range(NI):
                    for half, j in ((0, jp), (1, jp + NI)):
                        ki = kuse["gu"]
                        kuse["gu"] = ki + 1
                        slot = ki % 2

                        def tgu(ten, slot=slot, ki=ki):
                            u, bank = bank_begin(ten)
                            st_slab["gu"].wait(ten, ki)
                            for k in range(NHC):
                                wsl = slice(slot * H + k * 128, slot * H + (k + 1) * 128)
                                xsl = slice(k * TBLK, (k + 1) * TBLK)
                                ten.matmul(bank[:, 0:TBLK], gub_h[:, wsl], xlnT_h[:, xsl],
                                           start=(k == 0), stop=False)
                                ten.matmul(bank[:, 0:TBLK], gub_h[:, wsl], xlnT_l[:, xsl],
                                           start=False, stop=False)
                                mm = ten.matmul(bank[:, 0:TBLK], gub_l[:, wsl], xlnT_h[:, xsl],
                                                start=False, stop=(k == NHC - 1))
                            self.tm.inc(mm)
                            cons_v[("gu", ki)] = self.tm.n
                            bank_end(u)
                        self.t(tgu)
                        u_used = pbu["n"] - 1
                        tmv = self.tm.n
                        seq_pos[0] += 1
                        if seq_pos[0] + 1 < len(seq):
                            emit_slab_load(seq_pos[0] + 1)

                        if half == 0:
                            def sgate(sc, u=u_used, tmv=tmv):
                                sc.wait_ge(self.tm.h, tmv)
                                if up_done[0]:
                                    sc.wait_ge(self.vm.h, up_done[0])
                                cp = sc.activation(s_t[:, :], self.pb[u % 2][:, 0:TBLK],
                                                   AF.Silu)
                                evac_done[u] = (self.sm, self.sm.inc(cp))
                            self.s(sgate)
                            silu_done[0] = self.sm.n
                        else:
                            def vup(vec, u=u_used, tmv=tmv, jp=jp,
                                    wbc_ready=wbc_ready, sd=silu_done[0]):
                                vec.wait_ge(self.tm.h, tmv)
                                vec.wait_ge(self.sm.h, sd)
                                if jp == 0:
                                    vec.wait_ge(self.vm.h, wbc_ready)
                                hsl = act_h[:, jp * TBLK:(jp + 1) * TBLK]
                                lsl = act_l[:, jp * TBLK:(jp + 1) * TBLK]
                                self.dve_sync(vec, vec.tensor_tensor(
                                    tmpf[:, :], self.pb[u % 2][:, 0:TBLK], s_t[:, :],
                                    op=OP.mult))
                                self.dve_sync(vec, vec.tensor_tensor(
                                    tmpf[:, :], tmpf[:, :], wbc[:, :], op=OP.mult))
                                self.dve_sync(vec, vec.tensor_copy(hsl, tmpf[:, :]))
                                self.dve_sync(vec, vec.tensor_tensor(
                                    tmpf[:, :], tmpf[:, :], hsl, op=OP.subtract))
                                cp = vec.tensor_copy(lsl, tmpf[:, :])
                                evac_done[u] = (self.vm, self.vm.inc(cp))
                            self.v(vup)
                            up_done[0] = self.vm.n

                act_ready = self.vm.n
                for f in range(NHC):
                    ki = kuse["dn"]
                    kuse["dn"] = ki + 1
                    slot = ki % 2

                    def tdn(ten, f=f, slot=slot, ki=ki, act_ready=act_ready):
                        u, bank = bank_begin(ten)
                        st_slab["dn"].wait(ten, ki)
                        if f == 0:
                            ten.wait_ge(self.vm.h, act_ready)
                        for i in range(NI):
                            wsl = slice(slot * FI + i * 128, slot * FI + (i + 1) * 128)
                            asl = slice(i * TBLK, (i + 1) * TBLK)
                            ten.matmul(bank[:, 0:TBLK], dnb_h[:, wsl], act_h[:, asl],
                                       start=(i == 0), stop=False)
                            ten.matmul(bank[:, 0:TBLK], dnb_h[:, wsl], act_l[:, asl],
                                       start=False, stop=False)
                            mm = ten.matmul(bank[:, 0:TBLK], dnb_l[:, wsl], act_h[:, asl],
                                            start=False, stop=(i == NI - 1))
                        self.tm.inc(mm)
                        cons_v[("dn", ki)] = self.tm.n
                        bank_end(u)
                    self.t(tdn)
                    u_used = pbu["n"] - 1
                    tmv = self.tm.n
                    seq_pos[0] += 1
                    if seq_pos[0] + 1 < len(seq):
                        emit_slab_load(seq_pos[0] + 1)

                    def vdn(vec, u=u_used, tmv=tmv, f=f, gi=gi, tb=tb):
                        vec.wait_ge(self.tm.h, tmv)
                        if gi == 0 and f == 0 and tb > 0 and last_ep_tm[0]:
                            vec.wait_ge(self.tm.h, last_ep_tm[0])
                        dst = oacc[:, f * TBLK:(f + 1) * TBLK]
                        if gi == 0:
                            cp = vec.tensor_copy(dst, self.pb[u % 2][:, 0:TBLK])
                        else:
                            cp = vec.tensor_tensor(dst, dst, self.pb[u % 2][:, 0:TBLK],
                                                   op=OP.add)
                        evac_done[u] = (self.vm, self.vm.inc(cp))
                    self.v(vdn)
            oacc_ready = self.vm.n

            for tc in range(4):
                slot = tc % 2
                cred = xo_credit.get(tb * 4 + tc - 2)

                def yx2(sync, tb=tb, tc=tc, slot=slot, cred=cred):
                    if cred is not None:
                        sync.wait_ge(cred[0].h, cred[1])
                    r0 = tb * TBLK + tc * 128
                    dd = sync.dma_start(out=xo[:, slot * H:(slot + 1) * H],
                                        in_=x_src[r0:r0 + 128, :])
                    st_xo.mark(dd)
                self.y(yx2)
                xo_item = st_xo.n - 1

                ep = []
                for grp in range(4):
                    def tep(ten, grp=grp, tc=tc, oacc_ready=oacc_ready):
                        u, bank = tbank_begin(ten)
                        if grp == 0 and tc == 0:
                            ten.wait_ge(self.vm.h, oacc_ready)
                        for q in range(4):
                            f = grp * 4 + q
                            mm = self.tr(ten, bank[:, q * 128:(q + 1) * 128],
                                         oacc[:, f * TBLK + tc * 128:f * TBLK + tc * 128 + 128])
                        self.tm.inc(mm)
                        ep.append((u, bank, self.tm.n))
                        tbank_end(u)
                    self.t(tep)
                last_ep_tm[0] = self.tm.n

                st_marks = []
                for grp in range(4):
                    u, bank, tmv = ep[grp]

                    def vev(vec, grp=grp, bank=bank, tmv=tmv, slot=slot, xo_item=xo_item, u=u):
                        vec.wait_ge(self.tm.h, tmv)
                        st_xo.wait(vec, xo_item)
                        cp = None
                        for q in range(4):
                            f = grp * 4 + q
                            dst = xo[:, slot * H + f * 128:slot * H + (f + 1) * 128]
                            cp = vec.scalar_tensor_tensor(
                                dst, bank[:, q * 128:(q + 1) * 128], 0.5, dst,
                                op0=OP.mult, op1=OP.add)
                        val = self.vm.inc(cp)
                        st_marks.append((self.vm, val))
                        tevac_done[u] = (self.vm, val)
                    self.v(vev)

                def yst(sync, tb=tb, tc=tc, slot=slot, st_marks=tuple(st_marks)):
                    for so, val in st_marks:
                        sync.wait_ge(so.h, val)
                    r0 = tb * TBLK + tc * 128
                    dd = sync.dma_start(out=x_dst[r0:r0 + 128, :],
                                        in_=xo[:, slot * H:(slot + 1) * H])
                    it = st_xs.mark(dd)
                    xo_credit[tb * 4 + tc] = st_xs.vals[it]
                self.y(yst)

        es.close()

    def attn_proj(self):
        nc = self.nc
        es = ExitStack()
        BF16 = mybir.dt.bfloat16
        xlnT_h = es.enter_context(nc.sbuf_tensor("a1_xlnTh", [128, NHC * T], BF16))
        xlnT_l = es.enter_context(nc.sbuf_tensor("a1_xlnTl", [128, NHC * T], BF16))
        xa = es.enter_context(nc.sbuf_tensor("a1_xa", [128, H], F32))
        wraw = es.enter_context(nc.sbuf_tensor("a1_wraw", [128, H], F32))
        wT_h = es.enter_context(nc.sbuf_tensor("a1_wTh", [128, H], BF16))
        wT_l = es.enter_context(nc.sbuf_tensor("a1_wTl", [128, H], BF16))
        ext = es.enter_context(nc.sbuf_tensor("a1_ext", [128, 520], F32))
        qcA = es.enter_context(nc.sbuf_tensor("a1_qcA", [128, TBLK], F32))
        qcB = es.enter_context(nc.sbuf_tensor("a1_qcB", [128, TBLK], F32))
        qcv = es.enter_context(nc.sbuf_tensor("a1_qcv", [128, 2 * TBLK], F32))
        pr3 = es.enter_context(nc.sbuf_tensor("a1_pr3", [128, 3], F32))
        pr3i = es.enter_context(nc.sbuf_tensor("a1_pr3i", [128, 3], F32))
        x3row = self.trash[0:3, :]
        xln3T_h = es.enter_context(nc.sbuf_tensor("a1_x3h", [128, 3 * NHC], BF16))
        xln3T_l = es.enter_context(nc.sbuf_tensor("a1_x3l", [128, 3 * NHC], BF16))
        wbT_h = es.enter_context(nc.sbuf_tensor("a1_wbTh", [128, NH * NHC], BF16))
        wbT_l = es.enter_context(nc.sbuf_tensor("a1_wbTl", [128, NH * NHC], BF16))
        wbraw = xa[0:16, :]
        l3a = es.enter_context(nc.sbuf_tensor("a1_l3a", [3, 1], F32))
        l3b = es.enter_context(nc.sbuf_tensor("a1_l3b", [3, 1], F32))

        st_xa = DmaStream(self, "a1_sxa")
        st_w = DmaStream(self, "a1_sw")
        st_qcv = DmaStream(self, "a1_sqcv")
        self._a1_xa_cons = {}
        self._a1_w_cons = {}

        def gcc(gp):
            gp.dma_start(out=self.cc_x_in[:, :], in_=self.x1s[T - 3:T, :]
                         ).then_inc(self.qg.h, 16)
            self.qg.n += 16
            gp.wait_ge(self.qg.h, self.qg.n)
            gp.collective_compute(
                "AllGather", OP.bypass,
                replica_groups=[[0, 1], [2, 3], [4, 5], [6, 7]],
                ins=[self.cc_x_in.opt()],
                outs=[self.cc_x_out.opt()],
            ).then_inc(self.gm.h, 1)
            self.gm.n += 1
            gp.wait_ge(self.gm.h, self.gm.n)
            gp.dma_start(out=x3row[:, :], in_=self.cc_x_out[0:3, :]
                         ).then_inc(self.qg.h, 16)
            self.qg.n += 16
            gp.wait_ge(self.qg.h, self.qg.n)
            self.gm.inc(gp.nop())
        self.g(gcc)
        gm_x3 = self.gm.n

        def v3a(vec):
            vec.wait_ge(self.gm.h, gm_x3)
            self.dve_sync(vec, vec.reduce_sum(l3a[:, :], x3row[:, :], axis=AX.X))
            self.dve_sync(vec, vec.tensor_scalar(l3a[:, :], l3a[:, :], 1.0 / H, None,
                                                 op0=OP.mult))
            self.vm.inc(vec.tensor_scalar(x3row[:, :], x3row[:, :], l3a[:, 0:1], None,
                                          op0=OP.subtract))
        self.v(v3a)
        vmv = self.vm.n

        def s3a(sc, vmv=vmv):
            sc.wait_ge(self.vm.h, vmv)
            self.sm.inc(sc.activation(self.trash[64:67, :], x3row[:, :], AF.Square,
                                      accum_out=l3b[:, 0:1]))
        self.s(s3a)
        smv = self.sm.n

        def v3b(vec, smv=smv):
            vec.wait_ge(self.sm.h, smv)
            self.dve_sync(vec, vec.tensor_scalar(l3b[:, :], l3b[:, :], 1.0 / H, None,
                                                 op0=OP.mult))
            self.vm.inc(vec.tensor_scalar(l3b[:, :], l3b[:, :], LN_EPS, None, op0=OP.add))
        self.v(v3b)
        smv2 = self.vm.n

        def s3b(sc, smv2=smv2):
            sc.wait_ge(self.vm.h, smv2)
            self.sm.inc(sc.activation(l3b[:, :], l3b[:, :], AF.Sqrt))
        self.s(s3b)
        smv3 = self.sm.n

        def v3c(vec, smv3=smv3):
            vec.wait_ge(self.sm.h, smv3)
            self.dve_sync(vec, vec.reciprocal(l3b[:, :], l3b[:, :]))
            self.vm.inc(vec.tensor_scalar(x3row[:, :], x3row[:, :], l3b[:, 0:1], None,
                                          op0=OP.mult))
        self.v(v3c)
        v_x3 = self.vm.n

        def yb(sync):
            sync.dma_start(out=wbraw[:, :], in_=self.wb).then_inc(self.qw.h, 16)
            self.qw.n += 16
            sync.wait_ge(self.qw.h, self.qw.n)
        self.y(yb)
        qw_wb = self.qw.n

        pool_t = PsumPool(self, [2, 3, 4, 5])
        pool_m = PsumPool(self, [0, 1])
        pool_x = PsumPool(self, [7])

        for k in range(NHC):
            def t3(ten, k=k, v_x3=v_x3):
                u, bank = pool_x.begin(ten)
                if k == 0:
                    ten.wait_ge(self.vm.h, v_x3)
                self.tm.inc(self.tr(ten, bank[:, 0:3], x3row[:, k * 128:(k + 1) * 128]))
                pool_x.took(u)
            self.t(t3)
            tmv = self.tm.n

            def v3d(vec, k=k, tmv=tmv):
                vec.wait_ge(self.tm.h, tmv)
                sl = slice(k * 3, (k + 1) * 3)
                self.dve_sync(vec, vec.tensor_copy(xln3T_h[:, sl], self.pb[7][:, 0:3]))
                self.dve_sync(vec, vec.tensor_tensor(self.trash[:, 64:67], self.pb[7][:, 0:3],
                                                     xln3T_h[:, sl], op=OP.subtract))
                cp = vec.tensor_copy(xln3T_l[:, sl], self.trash[:, 64:67])
                pool_x.done_mark(self.vm, self.vm.inc(cp))
            self.v(v3d)

        for k in range(NHC):
            def tb(ten, k=k, qw_wb=qw_wb):
                u, bank = pool_x.begin(ten)
                if k == 0:
                    ten.wait_ge(self.qw.h, qw_wb)
                self.tm.inc(self.tr(ten, bank[:, 0:NH], wbraw[:, k * 128:(k + 1) * 128]))
                pool_x.took(u)
            self.t(tb)
            tmv = self.tm.n

            def vb(vec, k=k, tmv=tmv):
                vec.wait_ge(self.tm.h, tmv)
                sl = slice(k * NH, (k + 1) * NH)
                self.dve_sync(vec, vec.tensor_copy(wbT_h[:, sl], self.pb[7][:, 0:NH]))
                self.dve_sync(vec, vec.tensor_tensor(self.trash[:, 64:80], self.pb[7][:, 0:NH],
                                                     wbT_h[:, sl], op=OP.subtract))
                cp = vec.tensor_copy(wbT_l[:, sl], self.trash[:, 64:80])
                pool_x.done_mark(self.vm, self.vm.inc(cp))
            self.v(vb)

        wb_cons = self.tm.n

        for tc in range(NCH):
            slot = 0

            def yx(sync, tc=tc):
                if tc >= 1:
                    sync.wait_ge(self.tm.h, self._a1_xa_cons[tc - 1])
                else:
                    sync.wait_ge(self.tm.h, wb_cons)
                dd = sync.dma_start(out=xa[:, :],
                                    in_=self.x1s[tc * 128:(tc + 1) * 128, :])
                st_xa.mark(dd)
            self.y(yx)
            xa_item = st_xa.n - 1
            self.emit_ln(xa, slot, st_xa, xa_item)
            v_ln = self.vm.n

            tg = []
            for grp in range(4):
                def tx(ten, grp=grp, v_ln=v_ln):
                    u, bank = pool_t.begin(ten)
                    if grp == 0:
                        ten.wait_ge(self.vm.h, v_ln)
                    for q in range(4):
                        k = grp * 4 + q
                        mm = self.tr(ten, bank[:, q * 128:(q + 1) * 128],
                                     xa[:, k * 128:(k + 1) * 128])
                    self.tm.inc(mm)
                    tg.append((u, bank, self.tm.n))
                    pool_t.took(u)
                self.t(tx)
            self._a1_xa_cons[tc] = tg[3][2]
            for grp in range(4):
                u, bank, tmv = tg[grp]

                def ev(vec, grp=grp, bank=bank, tmv=tmv, tc=tc, u=u):
                    vec.wait_ge(self.tm.h, tmv)
                    for q in range(4):
                        k = grp * 4 + q
                        o0 = k * T + tc * 128
                        bsl = bank[:, q * 128:(q + 1) * 128]
                        tsl = self.trash[:, q * 128:(q + 1) * 128]
                        self.dve_sync(vec, vec.tensor_copy(xlnT_h[:, o0:o0 + 128], bsl))
                        self.dve_sync(vec, vec.tensor_tensor(tsl, bsl,
                                                             xlnT_h[:, o0:o0 + 128],
                                                             op=OP.subtract))
                        cp = vec.tensor_copy(xlnT_l[:, o0:o0 + 128], tsl)
                    pool_t.done_mark_u(u, self.vm, self.vm.inc(cp))
                self.v(ev)
        xlnT_ready_v = self.vm.n
        xlnT_ready_s = self.sm.n

        vwb_dbg = self.vm.n

        self._dbg_wbh = wbT_h
        self._dbg_wbl = wbT_l

        for ts in range(4):
            def tbt(ten, ts=ts):
                u, bank = pool_m.begin(ten)
                if ts == 0:
                    ten.wait_ge(self.vm.h, xlnT_ready_v)
                    ten.wait_ge(self.sm.h, xlnT_ready_s)
                for k in range(NHC):
                    wsl = slice(k * NH, (k + 1) * NH)
                    xsl = slice(k * T + ts * TBLK, k * T + (ts + 1) * TBLK)
                    ten.matmul(bank[:NH, 0:TBLK], wbT_h[:, wsl], xlnT_h[:, xsl],
                               start=(k == 0), stop=False)
                    ten.matmul(bank[:NH, 0:TBLK], wbT_h[:, wsl], xlnT_l[:, xsl],
                               start=False, stop=False)
                    mm = ten.matmul(bank[:NH, 0:TBLK], wbT_l[:, wsl], xlnT_h[:, xsl],
                                    start=False, stop=(k == NHC - 1))
                self.tm.inc(mm)
                pool_m.took(u)
            self.t(tbt)
            tmv = self.tm.n
            u_used = pool_m.u - 1

            def sbt(sc, ts=ts, tmv=tmv, u=u_used):
                sc.wait_ge(self.tm.h, tmv)
                cp = sc.activation(self.beta_sb[:, ts * TBLK:(ts + 1) * TBLK],
                                   self.pb[u % 2][:NH, 0:TBLK], AF.Sigmoid)
                pool_m.done_mark_u(u, self.sm, self.sm.inc(cp))
            self.s(sbt)

        conv_sm = [0]
        for pi, (p, wsrc, dsts) in enumerate((("q", self.wq, self.qs),
                                              ("k", self.wk, self.ks),
                                              ("v", self.wv, self.vs))):
            for osl in range(NHC):
                widx = pi * NHC + osl
                slot = 0

                def yw(sync, wsrc=wsrc, osl=osl, widx=widx):
                    if widx >= 1:
                        sync.wait_ge(self.tm.h, self._a1_w_cons[widx - 1])
                    dd = sync.dma_start(out=wraw[:, :],
                                        in_=wsrc[osl * 128:(osl + 1) * 128, :])
                    st_w.mark(dd)
                self.y(yw)
                w_item = st_w.n - 1

                tg = []
                for grp in range(4):
                    def twt(ten, grp=grp, w_item=w_item):
                        u, bank = pool_t.begin(ten)
                        if grp == 0:
                            st_w.wait(ten, w_item)
                        for q in range(4):
                            k = grp * 4 + q
                            mm = self.tr(ten, bank[:, q * 128:(q + 1) * 128],
                                         wraw[:, k * 128:(k + 1) * 128])
                        self.tm.inc(mm)
                        tg.append((u, bank, self.tm.n))
                        pool_t.took(u)
                    self.t(twt)
                wT_marks = []
                for grp in range(4):
                    u, bank, tmv = tg[grp]

                    def evw(vec, grp=grp, bank=bank, tmv=tmv, u=u):
                        vec.wait_ge(self.tm.h, tmv)
                        for q in range(4):
                            k = grp * 4 + q
                            sl = slice(k * 128, (k + 1) * 128)
                            bsl = bank[:, q * 128:(q + 1) * 128]
                            tsl = self.trash[:, q * 128:(q + 1) * 128]
                            self.dve_sync(vec, vec.tensor_copy(wT_h[:, sl], bsl))
                            self.dve_sync(vec, vec.tensor_tensor(tsl, bsl, wT_h[:, sl],
                                                                 op=OP.subtract))
                            cp = vec.tensor_copy(wT_l[:, sl], tsl)
                        val = self.vm.inc(cp)
                        wT_marks.append((self.vm, val))
                        pool_t.done_mark_u(u, self.vm, val)
                    self.v(evw)

                def tp3(ten, wT_marks=tuple(wT_marks)):
                    u, bank = pool_x.begin(ten)
                    for so, val in wT_marks:
                        ten.wait_ge(so.h, val)
                    for k in range(NHC):
                        wsl = slice(k * 128, (k + 1) * 128)
                        xsl = slice(k * 3, (k + 1) * 3)
                        ten.matmul(bank[:, 0:3], wT_h[:, wsl], xln3T_h[:, xsl],
                                   start=(k == 0), stop=False)
                        ten.matmul(bank[:, 0:3], wT_h[:, wsl], xln3T_l[:, xsl],
                                   start=False, stop=False)
                        mm = ten.matmul(bank[:, 0:3], wT_l[:, wsl], xln3T_h[:, xsl],
                                        start=False, stop=(k == NHC - 1))
                    self.tm.inc(mm)
                    pool_x.took(u)
                self.t(tp3)
                tmv = self.tm.n

                def vp3(vec, tmv=tmv):
                    vec.wait_ge(self.tm.h, tmv)
                    cp = vec.tensor_scalar(pr3i[:, :], self.pb[7][:, 0:3],
                                           self.flag_sb[:, 0:1], None, op0=OP.mult)
                    pool_x.done_mark(self.vm, self.vm.inc(cp))
                self.v(vp3)

                for ts in range(4):
                    def tmm(ten, ts=ts):
                        u, bank = pool_m.begin(ten)
                        for k in range(NHC):
                            wsl = slice(k * 128, (k + 1) * 128)
                            xsl = slice(k * T + ts * TBLK, k * T + (ts + 1) * TBLK)
                            ten.matmul(bank[:, 0:TBLK], wT_h[:, wsl], xlnT_h[:, xsl],
                                       start=(k == 0), stop=False)
                            ten.matmul(bank[:, 0:TBLK], wT_h[:, wsl], xlnT_l[:, xsl],
                                       start=False, stop=False)
                            mm = ten.matmul(bank[:, 0:TBLK], wT_l[:, wsl], xlnT_h[:, xsl],
                                            start=False, stop=(k == NHC - 1))
                        self.tm.inc(mm)
                        pool_m.took(u)
                    self.t(tmm)
                    tmv = self.tm.n
                    self._a1_w_cons[widx] = tmv
                    u_used = pool_m.u - 1

                    def vcv(vec, ts=ts, u=u_used, tmv=tmv, p=p, osl=osl):
                        vec.wait_ge(self.tm.h, tmv)
                        if conv_sm[0]:
                            vec.wait_ge(self.sm.h, conv_sm[0])
                        self.dve_sync(vec, vec.tensor_copy(ext[:, 3:515],
                                                           self.pb[u % 2][:, 0:TBLK]))
                        src3 = pr3i if ts == 0 else pr3
                        self.dve_sync(vec, vec.tensor_copy(ext[:, 0:3], src3[:, :]))
                        cw = self.cw[p]
                        self.dve_sync(vec, vec.tensor_scalar(
                            qcA[:, :], ext[:, 0:512], cw[:, osl * 4:osl * 4 + 1],
                            None, op0=OP.mult))
                        self.dve_sync(vec, vec.scalar_tensor_tensor(
                            qcB[:, :], ext[:, 1:513], cw[:, osl * 4 + 1:osl * 4 + 2],
                            qcA[:, :], op0=OP.mult, op1=OP.add))
                        self.dve_sync(vec, vec.scalar_tensor_tensor(
                            qcA[:, :], ext[:, 2:514], cw[:, osl * 4 + 2:osl * 4 + 3],
                            qcB[:, :], op0=OP.mult, op1=OP.add))
                        self.dve_sync(vec, vec.scalar_tensor_tensor(
                            qcB[:, :], ext[:, 3:515], cw[:, osl * 4 + 3:osl * 4 + 4],
                            qcA[:, :], op0=OP.mult, op1=OP.add))
                        cp = vec.tensor_copy(pr3[:, :], ext[:, 512:515])
                        val = self.vm.inc(cp)
                        pool_m.done_mark_u(u, self.vm, val)
                    self.v(vcv)
                    vmv = self.vm.n
                    qitem = st_qcv.n

                    def scv(sc, vmv=vmv, qslot=qitem % 2, qitem=qitem):
                        sc.wait_ge(self.vm.h, vmv)
                        if qitem >= 2:
                            st_qcv.wait(sc, qitem - 2)
                        self.sm.inc(sc.activation(qcv[:, qslot * TBLK:(qslot + 1) * TBLK],
                                                  qcB[:, :], AF.Silu))
                        conv_sm[0] = self.sm.n
                    self.s(scv)
                    smv = self.sm.n

                    def ycv(sync, dsts=dsts, ts=ts, osl=osl, smv=smv, qslot=qitem % 2):
                        sync.wait_ge(self.sm.h, smv)
                        ssem = st_qcv.begin_item()
                        for q in range(4):
                            c = ts * 4 + q
                            dd = sync.dma_start(
                                out=dsts[c, osl],
                                in_=qcv[:, qslot * TBLK + q * 128:qslot * TBLK + (q + 1) * 128])
                            ssem.inc(dd, 16)
                        st_qcv.commit()
                    self.y(ycv)
        es.close()

    def attn_a(self):
        nc = self.nc
        es = ExitStack()
        HG = 4
        NGR = NH // HG
        W = HG * 128
        qb = {p: es.enter_context(nc.sbuf_tensor(f"aa_qb_{p}", [128, 2 * H], F32))
              for p in "qkv"}
        DW = lambda name, w=W: es.enter_context(nc.sbuf_tensor(name, [128, w], F32))
        qst = [DW("aa_qst0"), DW("aa_qst1")]
        kst = [DW("aa_kst0"), DW("aa_kst1")]
        kfn = [DW("aa_kfn0"), DW("aa_kfn1")]
        Bw = [DW("aa_B0"), DW("aa_B1")]
        tqw = [DW("aa_tq0"), DW("aa_tq1")]
        TBw = [[DW(f"aa_TB{p}_{i}") for i in range(7)] for p in range(2)]
        Bpw = [[DW(f"aa_Bp{p}_{i}") for i in range(2)] for p in range(2)]
        ZK = [DW("aa_ZK0"), DW("aa_ZK1")]
        ZV = [DW("aa_ZV0"), DW("aa_ZV1")]
        Uw = [DW("aa_U0"), DW("aa_U1")]
        Gw = [DW("aa_G0"), DW("aa_G1")]
        OIw = [DW("aa_OI0"), DW("aa_OI1")]
        WFw = [DW("aa_WF0"), DW("aa_WF1")]
        sqw = [DW("aa_sq0"), DW("aa_sq1")]
        eall = [DW("aa_e0", 8), DW("aa_e1", 8)]
        rq_all = DW("aa_rq", NH)
        betaT = DW("aa_betaT", NH)

        st_qkv = DmaStream(self, "aa_sqkv")
        st_loc = DmaStream(self, "aa_sloc")
        st_rq = DmaStream(self, "aa_srq")
        pool = PsumPool(self, [0, 1, 2, 3, 4, 5, 6, 7])

        chunk_cons = {}
        sa_mark = {}
        gidx = [0]

        def emit_group_pair(c, q_item, pair):
            cslot = c % 2
            groups = []
            for gsub in range(2):
                g = pair * 2 + gsub
                h0 = g * HG
                gi = gidx[0] + gsub
                p = gi % 2
                groups.append(dict(g=g, h0=h0, p=p, gi=gi,
                                   qf=qb["q"][:, cslot * H + h0 * 128:cslot * H + (h0 + HG) * 128],
                                   kf=qb["k"][:, cslot * H + h0 * 128:cslot * H + (h0 + HG) * 128],
                                   vf=qb["v"][:, cslot * H + h0 * 128:cslot * H + (h0 + HG) * 128],
                                   st={}))
            gidx[0] += 2

            def seg(wt, j, w=128):
                return wt[:, j * w:(j + 1) * w]

            stages = []

            def t1(G, ten):
                p = G["p"]
                banks = []
                for src in (G["qf"], G["kf"], G["vf"]):
                    u, bank = pool.begin(ten)
                    for j in range(HG):
                        mm = self.tr(ten, seg(bank, j), seg(src, j))
                    self.tm.inc(mm)
                    pool.took(u)
                    banks.append(u)
                G["st"]["b1"] = banks
                if G["gi"] >= 2:
                    pass
                G["tmv1"] = self.tm.n

            def v1(G, vec):
                p = G["p"]
                vec.wait_ge(self.tm.h, G["tmv1"])
                prev_loc = st_loc.vals.get(G["gi"] - 2)
                if prev_loc is not None:
                    vec.wait_ge(prev_loc[0].h, prev_loc[1])
                bq, bk, bv = [pool.bank(u) for u in G["st"]["b1"]]
                self.dve_sync(vec, vec.tensor_copy(qst[p][:, :], bq[:, 0:W]))
                self.dve_sync(vec, vec.tensor_copy(kst[p][:, :], bk[:, 0:W]))
                last = None
                for j in range(HG):
                    last = vec.tensor_scalar(seg(ZV[p], j), seg(bv, j),
                                             betaT[:, G["h0"] + j:G["h0"] + j + 1],
                                             None, op0=OP.mult)
                val = self.vm.inc(last)
                for u in G["st"]["b1"]:
                    pool.done_mark_u(u, self.vm, val)
            stages.append((t1, v1, None))

            def v2(G, vec):
                p = G["p"]
                self.dve_sync(vec, vec.tensor_tensor(sqw[p][:, :], qst[p][:, :],
                                                     qst[p][:, :], op=OP.mult))
                for j in range(HG):
                    self.dve_sync(vec, vec.reduce_sum(eall[p][:, j:j + 1],
                                                      seg(sqw[p], j), axis=AX.X))
                self.dve_sync(vec, vec.tensor_tensor(sqw[p][:, :], kst[p][:, :],
                                                     kst[p][:, :], op=OP.mult))
                for j in range(HG):
                    self.dve_sync(vec, vec.reduce_sum(eall[p][:, 4 + j:5 + j],
                                                      seg(sqw[p], j), axis=AX.X))
                self.vm.inc(vec.tensor_scalar(eall[p][:, :], eall[p][:, :],
                                              L2_EPS, None, op0=OP.add))
                G["v2"] = self.vm.n

            def s2(G, sc):
                sc.wait_ge(self.vm.h, G["v2"])
                self.sm.inc(sc.activation(eall[G["p"]][:, :], eall[G["p"]][:, :], AF.Sqrt))
                G["s2"] = self.sm.n
            stages.append((None, v2, s2))

            def v3(G, vec):
                p = G["p"]
                vec.wait_ge(self.sm.h, G["s2"])
                if G["g"] == 0 and c >= 1:
                    st_rq.wait(vec, c - 1)
                self.dve_sync(vec, vec.reciprocal(eall[p][:, :], eall[p][:, :]))
                self.dve_sync(vec, vec.tensor_scalar(
                    rq_all[:, G["h0"]:G["h0"] + HG], eall[p][:, 0:4], HD_SCALE,
                    None, op0=OP.mult))
                for j in range(HG):
                    self.dve_sync(vec, vec.tensor_scalar(
                        seg(kst[p], j), seg(kst[p], j), eall[p][:, 4 + j:5 + j],
                        None, op0=OP.mult))
                last = None
                for j in range(HG):
                    last = vec.tensor_scalar(seg(ZK[p], j), seg(kst[p], j),
                                             betaT[:, G["h0"] + j:G["h0"] + j + 1],
                                             None, op0=OP.mult)
                val = self.vm.inc(last)
                G["v3"] = val
            stages.append((None, v3, None))

            def t4(G, ten):
                p = G["p"]
                u, bank = pool.begin(ten)
                ten.wait_ge(self.vm.h, G["v3"])
                for j in range(HG):
                    mm = self.tr(ten, seg(bank, j), seg(kst[p], j))
                self.tm.inc(mm)
                pool.took(u)
                G["st"]["b4"] = u
                G["tmv4"] = self.tm.n

            def v4(G, vec):
                p = G["p"]
                vec.wait_ge(self.tm.h, G["tmv4"])
                bank = pool.bank(G["st"]["b4"])
                cp = vec.tensor_copy(kfn[p][:, :], bank[:, 0:W])
                pool.done_mark_u(G["st"]["b4"], self.vm, self.vm.inc(cp))
                G["v4"] = self.vm.n
            stages.append((t4, v4, None))

            def t5(G, ten):
                p = G["p"]
                uk, bk = pool.begin(ten)
                ten.wait_ge(self.vm.h, G["v4"])
                for j in range(HG):
                    mm = ten.matmul(seg(bk, j), seg(kfn[p], j), seg(kfn[p], j),
                                    start=True, stop=True)
                self.tm.inc(mm)
                pool.took(uk)
                uq, bq = pool.begin(ten)
                for j in range(HG):
                    mm = ten.matmul(seg(bq, j), seg(G["qf"], j), seg(kfn[p], j),
                                    start=True, stop=True)
                self.tm.inc(mm)
                pool.took(uq)
                G["st"]["b5"] = (uk, uq)
                G["tmv5"] = self.tm.n

            def v5(G, vec):
                p = G["p"]
                vec.wait_ge(self.tm.h, G["tmv5"])
                uk, uq = G["st"]["b5"]
                bk, bq = pool.bank(uk), pool.bank(uq)
                last = None
                for j in range(HG):
                    last = vec.scalar_tensor_tensor(
                        seg(Bw[p], j), seg(bk, j),
                        betaT[:, G["h0"] + j:G["h0"] + j + 1], self.stril[:, :],
                        op0=OP.mult, op1=OP.mult)
                self.dve_sync(vec, last)
                cp = vec.tensor_tensor(tqw[p][:, :], bq[:, 0:W], self.tril0w[:, :],
                                       op=OP.mult)
                val = self.vm.inc(cp)
                pool.done_mark_u(uk, self.vm, val)
                pool.done_mark_u(uq, self.vm, val)
                G["v5"] = val
            stages.append((t5, v5, None))

            def t6(G, ten):
                p = G["p"]
                ub, bb = pool.begin(ten)
                ten.wait_ge(self.vm.h, G["v5"])
                for j in range(HG):
                    mm = self.tr(ten, seg(bb, j), seg(Bw[p], j))
                self.tm.inc(mm)
                pool.took(ub)
                ut, bt = pool.begin(ten)
                for j in range(HG):
                    mm = self.tr(ten, seg(bt, j), seg(tqw[p], j))
                self.tm.inc(mm)
                pool.took(ut)
                G["st"]["b6"] = (ub, ut)
                G["tmv6"] = self.tm.n

            def v6(G, vec):
                p = G["p"]
                vec.wait_ge(self.tm.h, G["tmv6"])
                ub, ut = G["st"]["b6"]
                self.dve_sync(vec, vec.tensor_copy(TBw[p][0][:, :],
                                                   pool.bank(ub)[:, 0:W]))
                cp = vec.tensor_copy(tqw[p][:, :], pool.bank(ut)[:, 0:W])
                val = self.vm.inc(cp)
                pool.done_mark_u(ub, self.vm, val)
                pool.done_mark_u(ut, self.vm, val)
                G["v6"] = val
            stages.append((t6, v6, None))

            for i in range(1, 7):
                def tsq(G, ten, i=i):
                    p = G["p"]
                    Bp_cur = Bw[p] if i == 1 else Bpw[p][(i - 1) % 2]
                    ten.wait_ge(self.vm.h, G["v6"] if i == 1 else G[f"vsq{i-1}"])
                    if i >= 2:
                        ten.wait_ge(self.sm.h, G[f"ssq{i-1}"])
                    uT, bT = pool.begin(ten)
                    for j in range(HG):
                        mm = ten.matmul(seg(bT, j), seg(Bp_cur, j), seg(TBw[p][i - 1], j),
                                        start=True, stop=True)
                    self.tm.inc(mm)
                    pool.took(uT)
                    uB = None
                    if i < 6:
                        uB, bB = pool.begin(ten)
                        for j in range(HG):
                            mm = ten.matmul(seg(bB, j), seg(TBw[p][i - 1], j),
                                            seg(Bp_cur, j), start=True, stop=True)
                        self.tm.inc(mm)
                        pool.took(uB)
                    G["st"][f"bsq{i}"] = (uT, uB)
                    G[f"tmsq{i}"] = self.tm.n

                def vsq(G, vec, i=i):
                    p = G["p"]
                    vec.wait_ge(self.tm.h, G[f"tmsq{i}"])
                    uT, uB = G["st"][f"bsq{i}"]
                    cp = vec.tensor_copy(TBw[p][i][:, :], pool.bank(uT)[:, 0:W])
                    val = self.vm.inc(cp)
                    pool.done_mark_u(uT, self.vm, val)
                    G[f"vsq{i}"] = val

                def ssq(G, sc, i=i):
                    p = G["p"]
                    uT, uB = G["st"][f"bsq{i}"]
                    if uB is None:
                        G[f"ssq{i}"] = self.sm.n
                        return
                    sc.wait_ge(self.tm.h, G[f"tmsq{i}"])
                    cp = sc.copy(Bpw[p][i % 2][:, :], pool.bank(uB)[:, 0:W])
                    val = self.sm.inc(cp)
                    pool.done_mark_u(uB, self.sm, val)
                    G[f"ssq{i}"] = val
                stages.append((tsq, vsq, ssq))

            for ai, i in enumerate([6, 5, 4, 3, 2, 1, 0]):
                def tap(G, ten, i=i, ai=ai):
                    p = G["p"]
                    ten.wait_ge(self.vm.h, G["vsq6"] if ai == 0 else G[f"vap{ai-1}"])
                    TBi = Bw[p] if i == 0 else TBw[p][i]
                    TBi = TBw[p][0] if i == 0 else TBw[p][i]
                    uK, bK = pool.begin(ten)
                    for j in range(HG):
                        mm = ten.matmul(seg(bK, j), seg(TBi, j), seg(ZK[p], j),
                                        start=True, stop=True)
                    self.tm.inc(mm)
                    pool.took(uK)
                    uV, bV = pool.begin(ten)
                    for j in range(HG):
                        mm = ten.matmul(seg(bV, j), seg(TBi, j), seg(ZV[p], j),
                                        start=True, stop=True)
                    self.tm.inc(mm)
                    pool.took(uV)
                    G["st"][f"bap{ai}"] = (uK, uV)
                    G[f"tmap{ai}"] = self.tm.n

                def vap(G, vec, i=i, ai=ai):
                    p = G["p"]
                    vec.wait_ge(self.tm.h, G[f"tmap{ai}"])
                    uK, uV = G["st"][f"bap{ai}"]
                    op = OP.subtract if i == 0 else OP.add
                    self.dve_sync(vec, vec.tensor_tensor(
                        ZK[p][:, :], ZK[p][:, :], pool.bank(uK)[:, 0:W], op=op))
                    cp = vec.tensor_tensor(ZV[p][:, :], ZV[p][:, :],
                                           pool.bank(uV)[:, 0:W], op=op)
                    val = self.vm.inc(cp)
                    pool.done_mark_u(uK, self.vm, val)
                    pool.done_mark_u(uV, self.vm, val)
                    G[f"vap{ai}"] = val
                stages.append((tap, vap, None))

            def t20(G, ten):
                p = G["p"]
                ten.wait_ge(self.vm.h, G["vap6"])
                ug, bg = pool.begin(ten)
                for j in range(HG):
                    mm = ten.matmul(seg(bg, j), seg(ZK[p], j), seg(tqw[p], j),
                                    start=True, stop=True)
                self.tm.inc(mm)
                pool.took(ug)
                uo, bo = pool.begin(ten)
                for j in range(HG):
                    mm = ten.matmul(seg(bo, j), seg(tqw[p], j), seg(ZV[p], j),
                                    start=True, stop=True)
                self.tm.inc(mm)
                pool.took(uo)
                uw, bw = pool.begin(ten)
                for j in range(HG):
                    mm = self.tr(ten, seg(bw, j), seg(ZK[p], j))
                self.tm.inc(mm)
                pool.took(uw)
                G["st"]["b20"] = (ug, uo, uw)
                G["tmv20"] = self.tm.n

            def v20(G, vec):
                p = G["p"]
                vec.wait_ge(self.tm.h, G["tmv20"])
                ug, uo, uw = G["st"]["b20"]
                self.dve_sync(vec, vec.tensor_tensor(Gw[p][:, :], G["qf"],
                                                     pool.bank(ug)[:, 0:W],
                                                     op=OP.subtract))
                self.dve_sync(vec, vec.tensor_copy(OIw[p][:, :],
                                                   pool.bank(uo)[:, 0:W]))
                cp = vec.tensor_copy(WFw[p][:, :], pool.bank(uw)[:, 0:W])
                val = self.vm.inc(cp)
                for u in (ug, uo, uw):
                    pool.done_mark_u(u, self.vm, val)
                G["v20"] = val
            stages.append((t20, v20, None))

            def s21(G, sc):
                p = G["p"]
                sc.wait_ge(self.vm.h, G["v20"])
                ssem = st_loc.begin_item()
                for j in range(HG):
                    h = G["h0"] + j
                    ssem.inc(sc.dma_start(out=self.loc[c, h, 0], in_=seg(WFw[p], j)), 16)
                    ssem.inc(sc.dma_start(out=self.loc[c, h, 1], in_=seg(ZV[p], j)), 16)
                    ssem.inc(sc.dma_start(out=self.loc[c, h, 2], in_=seg(kst[p], j)), 16)
                    ssem.inc(sc.dma_start(out=self.loc[c, h, 3], in_=seg(Gw[p], j)), 16)
                    ssem.inc(sc.dma_start(out=self.loc[c, h, 4], in_=seg(OIw[p], j)), 16)
                it = st_loc.commit()
                assert it == G["gi"]
            stages.append((None, None, s21))

            def t22(G, ten):
                p = G["p"]
                prev = sa_mark.get(G["g"])
                if prev is not None:
                    ten.wait_ge(self.vm.h, prev)
                ten.wait_ge(self.vm.h, G["v20"])
                u, bank = pool.begin(ten)
                for j in range(HG):
                    mm = ten.matmul(seg(bank, j), seg(WFw[p], j),
                                    self.SA[:, (G["h0"] + j) * 128:(G["h0"] + j + 1) * 128],
                                    start=True, stop=True)
                self.tm.inc(mm)
                pool.took(u)
                G["st"]["b22"] = u
                G["tmv22"] = self.tm.n

            def v22(G, vec):
                p = G["p"]
                vec.wait_ge(self.tm.h, G["tmv22"])
                u = G["st"]["b22"]
                cp = vec.tensor_tensor(Uw[p][:, :], ZV[p][:, :],
                                       pool.bank(u)[:, 0:W], op=OP.subtract)
                val = self.vm.inc(cp)
                pool.done_mark_u(u, self.vm, val)
                G["v22"] = val
            stages.append((t22, v22, None))

            def t23(G, ten):
                p = G["p"]
                ten.wait_ge(self.vm.h, G["v22"])
                u, bank = pool.begin(ten)
                for j in range(HG):
                    mm = ten.matmul(seg(bank, j), seg(kst[p], j), seg(Uw[p], j),
                                    start=True, stop=True)
                self.tm.inc(mm)
                pool.took(u)
                G["st"]["b23"] = u
                G["tmv23"] = self.tm.n
                chunk_cons[(c, G["g"])] = self.tm.n

            def v23(G, vec):
                p = G["p"]
                vec.wait_ge(self.tm.h, G["tmv23"])
                u = G["st"]["b23"]
                sl = self.SA[:, G["h0"] * 128:(G["h0"] + HG) * 128]
                cp = vec.tensor_tensor(sl, sl, pool.bank(u)[:, 0:W], op=OP.add)
                val = self.vm.inc(cp)
                pool.done_mark_u(u, self.vm, val)
                sa_mark[G["g"]] = val
            stages.append((t23, v23, None))

            for (tf, vf, sf) in stages:
                if tf is not None:
                    for G in groups:
                        self.t(lambda ten, tf=tf, G=G: tf(G, ten))
                if vf is not None:
                    for G in groups:
                        self.v(lambda vec, vf=vf, G=G: vf(G, vec))
                if sf is not None:
                    for G in groups:
                        self.s(lambda sc, sf=sf, G=G: sf(G, sc))

        for c in range(NCH):
            def yq(sync, c=c):
                if c >= 2:
                    for g in range(NGR):
                        sync.wait_ge(self.tm.h, chunk_cons[(c - 2, g)])
                slot = c % 2
                ssem = st_qkv.begin_item()
                for p, src in (("q", self.qs), ("k", self.ks), ("v", self.vs)):
                    for oc in range(NHC):
                        dd = sync.dma_start(
                            out=qb[p][:, slot * H + oc * 128:slot * H + (oc + 1) * 128],
                            in_=src[c, oc])
                        ssem.inc(dd, 16)
                st_qkv.commit()
            self.y(yq)
            q_item = st_qkv.n - 1

            def tbt(ten, c=c, q_item=q_item):
                u, bank = pool.begin(ten)
                st_qkv.wait(ten, q_item)
                self.tm.inc(self.tr(ten, bank[:, 0:NH],
                                    self.beta_sb[:, c * 128:(c + 1) * 128]))
                pool.took(u)
                self._aa_bt = (u, self.tm.n)
            self.t(tbt)

            def vbt(vec, c=c):
                u, tmv = self._aa_bt
                vec.wait_ge(self.tm.h, tmv)
                cp = vec.tensor_copy(betaT[:, :], pool.bank(u)[:, 0:NH])
                pool.done_mark_u(u, self.vm, self.vm.inc(cp))
            self.v(vbt)

            for pair in range(NGR // 2):
                emit_group_pair(c, q_item, pair)

            v_rq = self.vm.n

            def srq(sc, c=c, v_rq=v_rq):
                sc.wait_ge(self.vm.h, v_rq)
                dd = sc.dma_start(out=self.locr[c], in_=rq_all[:, :])
                st_rq.mark(dd)
            self.s(srq)
        es.close()


    def attn_cc(self):
        sa_final = self.vm.n

        def gcc(gp):
            gp.wait_ge(self.vm.h, sa_final)
            gp.dma_start(out=self.cc_s_in[:, :], in_=self.SA[:, :]).then_inc(self.qg.h, 16)
            self.qg.n += 16
            gp.wait_ge(self.qg.h, self.qg.n)
            gp.collective_compute(
                "AllGather", OP.bypass,
                replica_groups=[[0, 1], [2, 3], [4, 5], [6, 7]],
                ins=[self.cc_s_in.opt()],
                outs=[self.cc_s_out.opt()],
            ).then_inc(self.gm.h, 1)
            self.gm.n += 1
            gp.wait_ge(self.gm.h, self.gm.n)
            gp.dma_start(out=self.SB[:, :], in_=self.cc_s_out[0:128, :]
                         ).then_inc(self.qg.h, 16)
            self.qg.n += 16
            gp.wait_ge(self.qg.h, self.qg.n)
            self.gm.inc(gp.nop())
        self.g(gcc)
        gm_v = self.gm.n

        def vflag(vec):
            vec.wait_ge(self.gm.h, gm_v)
            self.vm.inc(vec.tensor_scalar(self.SB[:, :], self.SB[:, :],
                                          self.flag_sb[:, 0:1], None, op0=OP.mult))
        self.v(vflag)

    def attn_b(self, o_fm):
        nc = self.nc
        es = ExitStack()
        lb = [es.enter_context(nc.sbuf_tensor(f"ab_lb{i}", [128, 5 * 128], F32))
              for i in range(2)]
        rqb = [es.enter_context(nc.sbuf_tensor(f"ab_rq{i}", [128, NH], F32))
               for i in range(2)]
        t_ou = [es.enter_context(nc.sbuf_tensor(f"ab_ou{i}", [128, 128], F32))
                for i in range(2)]
        t_u = es.enter_context(nc.sbuf_tensor("ab_u", [128, 128], F32))
        sso = es.enter_context(nc.sbuf_tensor("ab_sso", [128, 1], F32))
        rq2 = es.enter_context(nc.sbuf_tensor("ab_rq2", [128, 1], F32))
        tr3 = es.enter_context(nc.sbuf_tensor("ab_tr3", [128, 128], F32))

        st_lb = DmaStream(self, "ab_slb")
        st_rq = DmaStream(self, "ab_srq2")
        pool = PsumPool(self, [0, 1, 2, 3])
        sb_mark = {}
        ou_cons = {}
        hu = [0]

        for c in range(NCH):
            def yrq(sync, c=c):
                if c >= 2:
                    sync.wait_ge(self.vm.h, self._ab_rq_cons.get(c - 2, 0))
                dd = sync.dma_start(out=rqb[c % 2][:, :], in_=self.locr[c])
                st_rq.mark(dd)
            self.y(yrq)
            rq_item = st_rq.n - 1
            if not hasattr(self, "_ab_rq_cons"):
                self._ab_rq_cons = {}

            for h in range(NH):
                hs = hu[0] % 2
                def ylb(sync, c=c, h=h, hs=hs, it=hu[0]):
                    if it >= 2:
                        prev = ou_cons.get(it - 2)
                        if prev is not None:
                            sync.wait_ge(prev[0].h, prev[1])
                    ssem = st_lb.begin_item()
                    for j in range(5):
                        dd = sync.dma_start(out=lb[hs][:, j * 128:(j + 1) * 128],
                                            in_=self.loc[c, h, j])
                        ssem.inc(dd, 16)
                    st_lb.commit()
                self.y(ylb)
                lb_item = st_lb.n - 1
                W_ = lb[hs][:, 0:128]
                Uv = lb[hs][:, 128:256]
                Kst = lb[hs][:, 256:384]
                G_ = lb[hs][:, 384:512]
                Oint = lb[hs][:, 512:640]
                prev_sb = sb_mark.get(h)

                def t1(ten, hs=hs, h=h, lb_item=lb_item, prev_sb=prev_sb):
                    u, bank = pool.begin(ten)
                    st_lb.wait(ten, lb_item)
                    if prev_sb is not None:
                        ten.wait_ge(self.vm.h, prev_sb)
                    ten.matmul(bank[:, 0:128], lb[hs][:, 384:512],
                               self.SB[:, h * 128:(h + 1) * 128], start=True, stop=True)
                    self.tm.inc(ten.matmul(bank[:, 128:256], lb[hs][:, 0:128],
                                           self.SB[:, h * 128:(h + 1) * 128],
                                           start=True, stop=True))
                    pool.took(u)
                self.t(t1)
                tmv = self.tm.n
                uu = pool.u - 1

                def v2(vec, hs=hs, uu=uu, tmv=tmv):
                    vec.wait_ge(self.tm.h, tmv)
                    bank = self.pb[pool.banks[uu % len(pool.banks)]]
                    self.dve_sync(vec, vec.tensor_tensor(t_ou[hs][:, :], bank[:, 0:128],
                                                         lb[hs][:, 512:640], op=OP.add))
                    cp = vec.tensor_tensor(t_u[:, :], lb[hs][:, 128:256], bank[:, 128:256],
                                           op=OP.subtract)
                    pool.done_mark_u(uu, self.vm, self.vm.inc(cp))
                self.v(v2)
                vmv = self.vm.n

                def t3(ten, hs=hs, vmv=vmv):
                    u, bank = pool.begin(ten)
                    ten.wait_ge(self.vm.h, vmv)
                    self.tm.inc(ten.matmul(bank[:, 0:128], lb[hs][:, 256:384], t_u[:, :],
                                           start=True, stop=True))
                    pool.took(u)
                self.t(t3)
                tmv = self.tm.n
                uu = pool.u - 1

                def v3(vec, h=h, uu=uu, tmv=tmv):
                    vec.wait_ge(self.tm.h, tmv)
                    bank = self.pb[pool.banks[uu % len(pool.banks)]]
                    cp = vec.tensor_tensor(self.SB[:, h * 128:(h + 1) * 128],
                                           self.SB[:, h * 128:(h + 1) * 128],
                                           bank[:, 0:128], op=OP.add)
                    val = self.vm.inc(cp)
                    sb_mark[h] = val
                    pool.done_mark_u(uu, self.vm, val)
                self.v(v3)
                vmv = self.vm.n

                def v6(vec, hs=hs, rq_item=rq_item, h=h, c=c):
                    st_rq.wait(vec, rq_item)
                    self.dve_sync(vec, vec.tensor_tensor(tr3[:, :], t_ou[hs][:, :],
                                                         t_ou[hs][:, :], op=OP.mult))
                    self.dve_sync(vec, vec.reduce_sum(sso[:, 0:1], tr3[:, :], axis=AX.X))
                    self.dve_sync(vec, vec.tensor_tensor(rq2[:, :], rqb[c % 2][:, h:h + 1],
                                                         rqb[c % 2][:, h:h + 1], op=OP.mult))
                    self.dve_sync(vec, vec.tensor_scalar(sso[:, :], sso[:, :],
                                                         1.0 / HD, None, op0=OP.mult))
                    self.dve_sync(vec, vec.tensor_tensor(sso[:, :], sso[:, :], rq2[:, :],
                                                         op=OP.mult))
                    self.vm.inc(vec.tensor_scalar(sso[:, :], sso[:, :], RMS_EPS,
                                                  None, op0=OP.add))
                self.v(v6)
                vmv = self.vm.n

                def s6(sc, vmv=vmv):
                    sc.wait_ge(self.vm.h, vmv)
                    self.sm.inc(sc.activation(sso[:, :], sso[:, :], AF.Sqrt))
                self.s(s6)
                smv = self.sm.n

                def v7(vec, hs=hs, smv=smv, h=h, c=c):
                    vec.wait_ge(self.sm.h, smv)
                    self.dve_sync(vec, vec.reciprocal(sso[:, :], sso[:, :]))
                    self.dve_sync(vec, vec.tensor_tensor(sso[:, :], sso[:, :],
                                                         rqb[c % 2][:, h:h + 1], op=OP.mult))
                    self.vm.inc(vec.tensor_scalar(t_ou[hs][:, :], t_ou[hs][:, :],
                                                  sso[:, 0:1], None, op0=OP.mult))
                    self._ab_rq_cons[c] = self.vm.n
                self.v(v7)
                vmv = self.vm.n

                def t8(ten, hs=hs, vmv=vmv):
                    u, bank = pool.begin(ten)
                    ten.wait_ge(self.vm.h, vmv)
                    self.tm.inc(self.tr(ten, bank[:, 0:128], t_ou[hs][:, :]))
                    pool.took(u)
                self.t(t8)
                tmv = self.tm.n
                uu = pool.u - 1

                def s8(sc, h=h, c=c, uu=uu, tmv=tmv):
                    sc.wait_ge(self.tm.h, tmv)
                    bank = self.pb[pool.banks[uu % len(pool.banks)]]
                    cp = sc.copy(o_fm[:, h * T + c * 128:h * T + (c + 1) * 128],
                                 bank[:, 0:128])
                    val = self.sm.inc(cp)
                    ou_cons[hu[0]] = (self.sm, val)
                    pool.done_mark_u(uu, self.sm, val)
                self.s(s8)
                hu[0] += 1
        es.close()

    def attn_out(self, o_fm):
        nc = self.nc
        es = ExitStack()
        woT = es.enter_context(nc.sbuf_tensor("a2_woT", [128, 2 * H], F32))
        x2tm = es.enter_context(nc.sbuf_tensor("a2_x2tm", [128, 2 * H], F32))
        yb = es.enter_context(nc.sbuf_tensor("a2_yb", [128, TS_A2], F32))
        st_wo = DmaStream(self, "a2_swo")
        st_x = DmaStream(self, "a2_sx")
        st_st = DmaStream(self, "a2_sst")
        pool = PsumPool(self, [0, 1])
        pool_t = PsumPool(self, [2, 3])
        o_ready_v = self.vm.n
        o_ready_s = self.sm.n
        NTS = T // TS_A2
        wo_cons = {}
        x2_done = {}

        for ts in range(NTS):
            for tc in range(2):
                def yx(sync, ts=ts, tc=tc):
                    it = ts * 2 + tc
                    if it >= 2:
                        prev = x2_done.get(it - 2)
                        if prev is not None:
                            sync.wait_ge(prev[0].h, prev[1])
                    r0 = ts * TS_A2 + tc * 128
                    dd = sync.dma_start(out=x2tm[:, tc * H:(tc + 1) * H],
                                        in_=self.x1s[r0:r0 + 128, :])
                    st_x.mark(dd)
                self.y(yx)
            x_item = st_x.n - 1

            for osl in range(NHC):
                widx = ts * NHC + osl
                slot = widx % 2

                def ywo(sync, osl=osl, widx=widx, slot=slot):
                    if widx >= 2:
                        sync.wait_ge(self.tm.h, wo_cons[widx - 2])
                    dd = sync.dma_start(out=woT[:, slot * H:(slot + 1) * H],
                                        in_=self.wot[osl])
                    st_wo.mark(dd)
                self.y(ywo)
                w_item = st_wo.n - 1

                def tmm(ten, ts=ts, slot=slot, w_item=w_item, osl=osl, widx=widx):
                    u, bank = pool.begin(ten)
                    st_wo.wait(ten, w_item)
                    if widx == 0:
                        ten.wait_ge(self.vm.h, o_ready_v)
                        ten.wait_ge(self.sm.h, o_ready_s)
                    for k in range(NHC):
                        mm = ten.matmul(bank[:, 0:TS_A2],
                                        woT[:, slot * H + k * 128:slot * H + (k + 1) * 128],
                                        o_fm[:, k * T + ts * TS_A2:k * T + (ts + 1) * TS_A2],
                                        start=(k == 0), stop=(k == NHC - 1))
                    self.tm.inc(mm)
                    wo_cons[widx] = self.tm.n
                    pool.took(u)
                self.t(tmm)
                tmv = self.tm.n
                uu = pool.u - 1

                def vev(vec, uu=uu, tmv=tmv):
                    vec.wait_ge(self.tm.h, tmv)
                    bank = self.pb[pool.banks[uu % len(pool.banks)]]
                    cp = vec.tensor_copy(yb[:, :], bank[:, 0:TS_A2])
                    pool.done_mark_u(uu, self.vm, self.vm.inc(cp))
                self.v(vev)
                vmv = self.vm.n

                def ttr(ten, vmv=vmv):
                    u, bank = pool_t.begin(ten)
                    ten.wait_ge(self.vm.h, vmv)
                    self.tr(ten, bank[:, 0:128], yb[:, 0:128])
                    self.tm.inc(self.tr(ten, bank[:, 128:256], yb[:, 128:256]))
                    pool_t.took(u)
                self.t(ttr)
                tmv = self.tm.n
                uu = pool_t.u - 1

                def vadd(vec, osl=osl, uu=uu, tmv=tmv, x_item=x_item):
                    vec.wait_ge(self.tm.h, tmv)
                    st_x.wait(vec, x_item - 1)
                    st_x.wait(vec, x_item)
                    bank = self.pb[pool_t.banks[uu % len(pool_t.banks)]]
                    for tc in range(2):
                        dst = x2tm[:, tc * H + osl * 128:tc * H + (osl + 1) * 128]
                        cp = vec.tensor_tensor(dst, dst, bank[:, tc * 128:(tc + 1) * 128],
                                               op=OP.add)
                    pool_t.done_mark_u(uu, self.vm, self.vm.inc(cp))
                self.v(vadd)
            v_x2 = self.vm.n

            def yst(sync, ts=ts, v_x2=v_x2):
                sync.wait_ge(self.vm.h, v_x2)
                for tc in range(2):
                    r0 = ts * TS_A2 + tc * 128
                    dd = sync.dma_start(out=self.x2s[r0:r0 + 128, :],
                                        in_=x2tm[:, tc * H:(tc + 1) * H])
                    it = st_st.mark(dd)
                    x2_done[ts * 2 + tc] = st_st.vals[it]
            self.y(yst)
        es.close()

    def finish(self):
        final = self.qw.n

        def yend(sync):
            sync.wait_ge(self.qw.h, final)
        self.y(yend)
        self.es.close()
        return self.nc


def _build_nc(phases=("f1", "attn", "f2")):
    mk = MK()
    mk.p0()
    mk.phase_barrier()
    if "f1" in phases:
        mk.ffn(mk.x0, mk.x1s, "f1")
        mk.phase_barrier()
    if "f1out" in phases:
        mk.ffn(mk.x0, mk.out, "f1")
        mk.phase_barrier()
    if "attn" in phases:
        mk.attn_proj()
        mk.phase_barrier()
        mk.attn_a()
        mk.phase_barrier()
        mk.attn_cc()
        mk.phase_barrier()
        o_es = ExitStack()
        o_fm = o_es.enter_context(mk.nc.sbuf_tensor("o_fm", [128, NHC * T], F32))
        mk.attn_b(o_fm)
        mk.phase_barrier()
        mk.attn_out(o_fm)
        mk.phase_barrier()
        o_es.close()
    if "qdump" in phases:
        mk.attn_proj()
        mk.phase_barrier()

        def ydump(sync):
            src_flat = mk.qs
            for cc in range(NCH):
                for oc in range(NHC):
                    sync.dma_start(out=mk.out[(cc * NHC + oc) * 8:(cc * NHC + oc) * 8 + 8, :],
                                   in_=bass.AP(src_flat.tensor,
                                               (cc * NHC + oc) * 128 * C,
                                               [[H, 8], [1, H]])).then_inc(mk.qw.h, 16)
                    mk.qw.n += 16
            sync.wait_ge(mk.qw.h, mk.qw.n)
        mk.y(ydump)
    if "bdump" in phases:
        mk.attn_proj()
        mk.phase_barrier()

        def ybdump(sync):
            sync.dma_start(out=mk.out[0:NH, :], in_=mk.beta_sb[:, :]).then_inc(mk.qw.h, 16)
            mk.qw.n += 16
            sync.wait_ge(mk.qw.h, mk.qw.n)
        mk.y(ybdump)

        def gbdump(gp):
            gp.dma_start(out=mk.out[16:144, 0:NH * NHC], in_=mk._dbg_wbh[:, :]).then_inc(mk.qg.h, 16)
            gp.dma_start(out=mk.out[144:272, 0:NH * NHC], in_=mk._dbg_wbl[:, :]).then_inc(mk.qg.h, 16)
            mk.qg.n += 32
            gp.wait_ge(mk.qg.h, mk.qg.n)
        mk.g(gbdump)
    if "attnout" in phases:
        pass
    if "f2" in phases:
        mk.ffn(mk.x2s, mk.out, "f2")
        mk.phase_barrier()
    return mk.finish()


SEL8 = np.zeros((NG, NG * 128), np.float32)
for _g in range(NG):
    SEL8[_g, _g * 128:(_g + 1) * 128] = 1.0


def _make_in_maps(inputs):
    x = np.ascontiguousarray(inputs["L0_attn_x"])
    common = {
        "wq": np.ascontiguousarray(inputs["L0_attn_Wq"]),
        "wk": np.ascontiguousarray(inputs["L0_attn_Wk"]),
        "wv": np.ascontiguousarray(inputs["L0_attn_Wv"]),
        "wb": np.ascontiguousarray(inputs["L0_attn_Wb"]),
        "wo": np.ascontiguousarray(inputs["L0_attn_Wo"]),
        "cq": np.ascontiguousarray(inputs["L0_attn_conv_q"]),
        "ck": np.ascontiguousarray(inputs["L0_attn_conv_k"]),
        "cv": np.ascontiguousarray(inputs["L0_attn_conv_v"]),
        "rmsw": np.ascontiguousarray(
            np.asarray(inputs["L0_attn_rms_w"]).reshape(HD, 1)),
        "gw": np.ascontiguousarray(inputs["L0_ffn_gate_weight"]),
        "gup": np.ascontiguousarray(
            np.asarray(inputs["L0_ffn_gate_up_proj"]).reshape(NE * 2 * FI, H)),
        "dwn": np.ascontiguousarray(
            np.asarray(inputs["L0_ffn_down_proj"]).reshape(NE * H, FI)),
        "mg": np.ascontiguousarray(
            np.asarray(inputs["L0_ffn_merge_groups"]).reshape(NE, 1).astype(np.int32)),
        "dom": np.ascontiguousarray(
            np.asarray(inputs["L0_ffn_dominant_experts"]).reshape(NG, 1).astype(np.int32)),
    }
    in_maps = []
    for c in range(N_CORES):
        b, hf = divmod(c, 2)
        m = dict(common)
        m["x"] = np.ascontiguousarray(x[b, hf * T:(hf + 1) * T, :])
        m["flag"] = np.full((128, 1), float(hf), np.float32)
        m["sel8"] = SEL8
        in_maps.append(m)
    return in_maps


def kernel(**inputs):
    in_maps = _make_in_maps(inputs)
    if "nc" not in _NC_CACHE:
        _NC_CACHE["nc"] = _build_nc()
    nc = _NC_CACHE["nc"]
    res = run_bass_kernel_spmd(nc, in_maps, core_ids=list(range(N_CORES)))
    out = np.empty((4, 4096, H), np.float32)
    for c in range(N_CORES):
        b, hf = divmod(c, 2)
        out[b, hf * T:(hf + 1) * T, :] = res.results[c]["out"]
    return out
